# version 01
# baseline (speedup 1.0000x reference)
import math
from contextlib import ExitStack
import numpy as np
import ml_dtypes
import concourse.bass as bass
import concourse.mybir as mybir
from concourse.bass_utils import run_bass_kernel_spmd

F32 = mybir.dt.float32
BF16 = mybir.dt.bfloat16
AF = mybir.ActivationFunctionType
ALU = mybir.AluOpType
AX = mybir.AxisListType

D = 1024
T = 4096
NCORE = 8
ENG = ("sync", "tensor", "vector", "scalar", "gpsimd")

BF = ml_dtypes.bfloat16

_CONST_CACHE = {}


def host_constants():
    if _CONST_CACHE:
        return _CONST_CACHE
    N = D
    j = np.arange(N)
    M_om = np.zeros((N, 512), dtype=np.float64)
    for jj in range(1, 512):
        M_om[:, jj - 1] += -np.sin(2 * np.pi * jj * j / N)
    for jj in range(513, 1024):
        M_om[:, 1023 - jj] -= -np.sin(2 * np.pi * jj * j / N)
    k5 = np.arange(512)
    ang = 2 * np.pi * np.outer(k5, j) / N
    P_pack = np.concatenate([np.cos(ang), -np.sin(ang)], axis=0)
    P_pack[512] = (-1.0) ** j
    G = np.zeros((N, N))
    G[:, 0] = 1.0 / N
    G[:, 512] = ((-1.0) ** j) / N
    for k in range(1, 512):
        G[:, k] = 2.0 / N * np.cos(2 * np.pi * j * k / N)
        G[:, 512 + k] = -2.0 / N * np.sin(2 * np.pi * j * k / N)
    half = D // 2
    freq = np.exp(np.arange(half) * (-math.log(10000.0) / (half - 1)))
    pos = (np.arange(T) + 1.0)[:, None] * freq[None, :]
    pe = np.concatenate([np.sin(pos), np.cos(pos)], -1).astype(np.float32)
    hd = 64
    inv_freq = 1.0 / 10000.0 ** (np.arange(0, hd, 2) / hd)
    freqs = np.arange(T)[:, None] * inv_freq[None, :]
    emb = np.concatenate([freqs, freqs], -1)
    _CONST_CACHE.update(dict(
        M_om=M_om.astype(np.float32),
        P_packT=np.ascontiguousarray(P_pack.T).astype(BF),
        G=np.ascontiguousarray(G).astype(BF),
        posemb_f=np.ascontiguousarray(pe.T).astype(np.float32),
        rope_cos=np.ascontiguousarray(np.cos(emb).T).astype(np.float32),
        rope_sin=np.ascontiguousarray(np.sin(emb).T).astype(np.float32),
        cmask=(np.triu(np.ones((128, 128), np.float32), k=1) * -1e9),
        tri_keepT=np.ascontiguousarray(
            np.tril(np.ones((64, 64), np.float32)).T),
        ident128=np.eye(128, dtype=np.float32),
        ident128b=np.eye(128, dtype=BF),
        ones_row_f=np.ones((1, 128), np.float32),
        ones_row_b=np.ones((1, 128), dtype=BF),
        ones_col_b=np.ones((128, 1), dtype=BF),
        zeros64=np.zeros((64, 64), np.float32),
    ))
    return _CONST_CACHE


class Sem:
    __slots__ = ("h", "n", "name", "maxwait")

    def __init__(self, h, name):
        self.h = h
        self.n = 0
        self.name = name
        self.maxwait = 0


class Prog:
    def __init__(self, nc):
        self.nc = nc
        self.ops = {e: [] for e in ENG}
        self.sems = []
        self.bsem = None

    def sem(self, name):
        s = Sem(self.nc.alloc_semaphore(name), name)
        self.sems.append(s)
        return s

    def emit(self, eng, fn, waits=(), inc=None, inc_k=None):
        wl = []
        for s, v in waits:
            s.maxwait = max(s.maxwait, v)
            wl.append((s.h, v))
        incv = None
        if inc is not None:
            if isinstance(inc, tuple):
                sem, k = inc
            else:
                sem, k = inc, 1
            if inc_k is not None:
                k = inc_k
            sem.n += k
            incv = (sem.h, k)

        def g(engine, fn=fn, wl=wl, incv=incv):
            for h, v in wl:
                engine.wait_ge(h, v)
            ins = fn(engine)
            if incv is not None:
                ins.then_inc(*incv)
        self.ops[eng].append(g)

    def wait(self, eng, sem, val=None):
        v = sem.n if val is None else val
        sem.maxwait = max(sem.maxwait, v)
        self.ops[eng].append(lambda engine, h=sem.h, v=v: engine.wait_ge(h, v))

    def barrier(self, engines=ENG):
        if self.bsem is None:
            self.bsem = self.sem("bsem")
        s = self.bsem
        d = self.dummy
        target = s.n + len(engines)
        for e in engines:
            if e == "vector":
                self.ops[e].append(lambda engine, h=s.h: engine.tensor_copy(
                    d[0:1, 0:1], d[0:1, 1:2]).then_inc(h, 1))
            elif e == "scalar":
                self.ops[e].append(lambda engine, h=s.h: engine.copy(
                    d[0:1, 2:3], d[0:1, 3:4]).then_inc(h, 1))
            elif e == "gpsimd":
                self.ops[e].append(lambda engine, h=s.h: engine.tensor_copy(
                    d[0:1, 4:5], d[0:1, 5:6]).then_inc(h, 1))
            else:
                self.ops[e].append(lambda engine, h=s.h: engine.sem_inc(h, 1))
        s.n = target
        for e in engines:
            self.wait(e, s, target)

    def finalize(self):
        for s in self.sems:
            assert s.maxwait <= s.n, (
                f"sem {s.name}: waits up to {s.maxwait} but only {s.n} incs")
        nc = self.nc
        with nc.Block() as block:
            for name in ENG:
                fns = self.ops[name]
                getattr(block, name)(lambda eng, fns=fns: [f(eng) for f in fns])


def ceil_div(a, b):
    return (a + b - 1) // b


class Util:
    def __init__(self, P, consts_sb, pb, pb_bf):
        self.P = P
        self.C = consts_sb
        self.pb = pb
        self.pb_bf = pb_bf

    def scoped_sb(self, stack, name, shape, dtype):
        return stack.enter_context(self.P.nc.sbuf_tensor(name, shape, dtype))


def build_base(nc):
    H = host_constants()
    P = Prog(nc)
    pb = [nc.alloc_psum_tensor(f"pb{i}", [128, 512], F32) for i in range(7)]
    pb_bf = nc.alloc_psum_tensor("pb_bf", [128, 1024], BF16)
    consts = {}
    cin = {}
    s_c = P.sem("const_dma")

    def add_const(name, arr, dtype, stack=None):
        d = nc.dram_tensor("c_" + name, list(arr.shape), dtype,
                           kind="ExternalInput").ap()
        if stack is None:
            s = nc.alloc_sbuf_tensor("sb_" + name, list(arr.shape), dtype)
        else:
            s = stack.enter_context(nc.sbuf_tensor("sb_" + name, list(arr.shape), dtype))
        consts[name] = s
        cin["c_" + name] = arr
        P.emit("sync", lambda e, s=s, d=d: e.dma_start(s[:], d),
               inc=(s_c, 16))
        return s

    add_const("ident128", H["ident128"], F32)
    add_const("ident128b", H["ident128b"], BF16)
    add_const("ones_row_f", H["ones_row_f"], F32)
    add_const("ones_row_b", H["ones_row_b"], BF16)
    add_const("ones_col_b", H["ones_col_b"], BF16)
    add_const("ones_col2_b", np.ones((128, 2), dtype=BF), BF16)
    add_const("ones_col_f", np.ones((128, 1), np.float32), F32)
    epsvals = np.array([1e-5, 1e-6, -6.0, 0.0, 1.0, 1e-12], np.float32)
    add_const("epsb", np.broadcast_to(epsvals, (128, 6)).copy(), F32)
    add_const("ones_row_b512", np.ones((1, 512), dtype=BF), BF16)
    P.dummy = nc.alloc_sbuf_tensor("sb_dummy", [1, 16], F32)
    P.emit("vector", lambda e: e.memset(P.dummy[0:1, 0:16], 0.0))
    U = Util(P, consts, pb, pb_bf)
    U.s_const = s_c
    U.cin = cin
    U.add_const = add_const
    for e in ENG:
        P.wait(e, s_c)
    return P, U


def emit_cast_bf16(P, d_src, d_dst, s_gp):
    P.emit("gpsimd", lambda e: e.dma_start(d_dst, d_src), inc=(s_gp, 16))


def emit_transpose_load(P, d_bf, rows, cols, out_sb, out_col_stride, s_tr,
                        waits=()):
    nchunk = cols // 128
    for c in range(nchunk):
        P.emit("sync", lambda e, c=c: e.dma_start(
            out_sb[0:128, c * out_col_stride:c * out_col_stride + rows],
            d_bf[0:rows, c * 128:(c + 1) * 128], transpose=True),
            waits=waits if c == 0 else (), inc=(s_tr, 16))


def emit_load_bias_cols(P, d_vec, n, out_sb, col0, s_gp, queue="gpsimd"):
    for c in range(n // 128):
        P.emit(queue, lambda e, c=c: e.dma_start(
            out_sb[0:128, col0 + c:col0 + c + 1], d_vec[c * 128:(c + 1) * 128]),
            inc=(s_gp, 16))


def emit_ln_tile(P, U, xtile_f32, xn_bf, tw, sems, eps_col=0, affine=None,
                 extra_f32=None, rms_only=False, out_fn=None):
    if out_fn is None:
        out_fn = lambda c: xn_bf[0:128, c * tw:(c + 1) * tw]
    """xn = (x - mean)/sqrt(var+eps) [* scale + bias].
    Stats psum: pb[5] (S1 row0, S2 row32); replicate: m->pb[6], rstd->pb[5].
    scratch tensors live in U.ln_scr (xb, xsq, stat, tmp)."""
    C = U.C
    pbs = U.pb[5]
    pbr = U.pb[6]
    s_pe, s_ev, s_sc = sems["pe"], sems["ev"], sems["sc"]
    scr = U.ln_scr
    xsq, stat, tmp = scr["xsq"], scr["stat"], scr["tmp"]
    P.wait("tensor", s_ev)
    P.wait("scalar", s_pe)
    P.wait("scalar", s_ev)
    for c in range(8):
        sl = slice(c * tw, (c + 1) * tw)
        P.emit("scalar", lambda e, sl=sl: e.activation(
            xsq[0:128, sl], xtile_f32[0:128, sl], AF.Square),
            inc=(s_sc, 1) if c == 7 else None)
    P.wait("tensor", s_sc)
    if not rms_only:
        for c in range(8):
            sl = slice(c * tw, (c + 1) * tw)
            P.emit("tensor", lambda e, sl=sl, c=c: e.matmul(
                pbs[0:1, 0:tw], C["ones_col_f"][0:128, 0:1], xtile_f32[0:128, sl],
                start=(c == 0), stop=(c == 7)))
    for c in range(8):
        sl = slice(c * tw, (c + 1) * tw)
        P.emit("tensor", lambda e, sl=sl, c=c: e.matmul(
            pbs[32:33, 0:tw], C["ones_col_b"][0:128, 0:1], xsq[0:128, sl],
            start=(c == 0), stop=(c == 7)),
            inc=(s_pe, 1) if c == 7 else None)
    P.wait("vector", s_pe)
    if rms_only:
        P.emit("vector", lambda e: e.tensor_scalar(
            stat[0:1, tw:2 * tw], pbs[32:33, 0:tw], 1.0 / D, None, ALU.mult),
            inc=(s_ev, 1))
    else:
        P.emit("vector", lambda e: e.tensor_scalar(
            stat[0:1, 0:tw], pbs[0:1, 0:tw], 1.0 / D, None, ALU.mult),
            inc=(s_ev, 1))
        P.emit("vector", lambda e: e.tensor_mul(
            stat[0:1, tw:2 * tw], stat[0:1, 0:tw], stat[0:1, 0:tw]),
            waits=[(s_ev, s_ev.n)], inc=(s_ev, 1))
        P.emit("vector", lambda e: e.scalar_tensor_tensor(
            stat[0:1, tw:2 * tw], pbs[32:33, 0:tw], 1.0 / D,
            stat[0:1, tw:2 * tw], ALU.mult, ALU.subtract),
            waits=[(s_ev, s_ev.n)], inc=(s_ev, 1))
    P.wait("scalar", s_ev)
    P.emit("scalar", lambda e: e.activation(
        stat[0:1, tw:2 * tw], stat[0:1, tw:2 * tw], AF.Ln,
        bias=C["epsb"][0:1, eps_col:eps_col + 1]), inc=(s_sc, 1))
    P.emit("scalar", lambda e: e.activation(
        stat[0:1, tw:2 * tw], stat[0:1, tw:2 * tw], AF.Exp, scale=-0.5),
        waits=[(s_sc, s_sc.n)], inc=(s_sc, 1))
    P.wait("tensor", s_sc)
    P.wait("tensor", s_ev)
    if not rms_only:
        P.emit("tensor", lambda e: e.matmul(
            pbr[0:128, 0:tw], C["ones_row_f"][0:1, 0:128], stat[0:1, 0:tw],
            start=True, stop=True))
    P.emit("tensor", lambda e: e.matmul(
        pbs[0:128, 0:tw], C["ones_row_f"][0:1, 0:128], stat[0:1, tw:2 * tw],
        start=True, stop=True), inc=(s_pe, 1))
    P.wait("vector", s_pe)
    for c in range(8):
        sl = slice(c * tw, (c + 1) * tw)
        if rms_only:
            src = xtile_f32
        else:
            P.emit("vector", lambda e, sl=sl: e.tensor_sub(
                tmp[0:128, sl], xtile_f32[0:128, sl], pbr[0:128, 0:tw]))
            src = tmp
        last = (c == 7)
        if affine is None:
            P.emit("vector", lambda e, sl=sl, src=src, c=c: e.tensor_mul(
                out_fn(c), src[0:128, sl], pbs[0:128, 0:tw]),
                inc=(s_ev, 1) if last else None)
        else:
            sc_cols, bi_cols = affine
            P.emit("vector", lambda e, sl=sl, src=src: e.tensor_mul(
                tmp[0:128, sl], src[0:128, sl], pbs[0:128, 0:tw]))
            if bi_cols is None:
                P.emit("vector", lambda e, sl=sl, c=c: e.tensor_scalar(
                    out_fn(c), tmp[0:128, sl],
                    sc_cols[0:128, c:c + 1], None, ALU.mult),
                    inc=(s_ev, 1) if last else None)
            else:
                P.emit("vector", lambda e, sl=sl, c=c: e.tensor_scalar(
                    out_fn(c), tmp[0:128, sl],
                    sc_cols[0:128, c:c + 1], bi_cols[0:128, c:c + 1],
                    ALU.mult, ALU.add),
                    inc=(s_ev, 1) if last else None)
        if extra_f32 is not None:
            P.emit("vector", lambda e, sl=sl, src=src: e.tensor_mul(
                extra_f32[0:128, sl], src[0:128, sl], pbs[0:128, 0:tw]),
                inc=(s_ev, 1) if last else None)


def build_L0_ffn(P, U, io, tag):
    with ExitStack() as _ST:
        _build_L0_ffn(P, U, io, tag, _ST)
    P.barrier()


def _build_L0_ffn(P, U, io, tag, _ST):
    nc, C = P.nc, U.C
    TW = 512
    s_gp = P.sem(f"{tag}_gp")
    s_tr = P.sem(f"{tag}_tr")
    s_x = P.sem(f"{tag}_x")
    s_pe = P.sem(f"{tag}_pe")
    s_ev = P.sem(f"{tag}_ev")
    s_sc = P.sem(f"{tag}_sc")
    s_st = P.sem(f"{tag}_st")
    sems = {"pe": s_pe, "ev": s_ev, "sc": s_sc}

    sb = lambda name, shape, dtype: U.scoped_sb(_ST, name, shape, dtype)
    xtile = sb(f"{tag}_xtile", [128, 8 * TW], F32)
    xn = sb(f"{tag}_xn", [128, 8 * TW], BF16)
    xsq = sb(f"{tag}_xsq", [128, 8 * TW], BF16)
    tmp = sb(f"{tag}_tmp", [128, 8 * TW], F32)
    stat = sb(f"{tag}_stat", [1, 2 * TW], F32)
    WT = sb(f"{tag}_WT", [128, 8 * 4096], BF16)
    htile = sb(f"{tag}_htile", [128, 32 * TW], BF16)
    xtok = sb(f"{tag}_xtok", [1, 8 * 1024], BF16)
    erow = sb(f"{tag}_erow", [1, 8 * 1024], BF16)
    ocol = sb(f"{tag}_ocol", [128, 64], BF16)
    dots = [sb(f"{tag}_dot{i}", [1, TW], BF16) for i in range(4)]
    d2sb = [sb(f"{tag}_d2_{k}", [1, TW], BF16) for k in range(16)]
    bcol = sb(f"{tag}_bcol", [128, 40], F32)
    scl = sb(f"{tag}_scl", [1, 1], F32)
    U.ln_scr = {"xsq": xsq, "stat": stat, "tmp": tmp}

    emit_cast_bf16(P, io["W1"], io["W1bf"], s_gp)
    emit_cast_bf16(P, io["W2"], io["W2bf"], s_gp)
    emit_load_bias_cols(P, io["b1"], 4096, bcol, 0, s_gp)
    emit_load_bias_cols(P, io["b2"], 1024, bcol, 32, s_gp)
    P.emit("gpsimd", lambda e: e.dma_start(scl[0:1, 0:1], io["scale"]),
           inc=(s_gp, 16))
    GP_ALL = s_gp.n
    emit_transpose_load(P, io["W1bf"], 4096, 1024, WT, 4096, s_tr,
                        waits=[(s_gp, GP_ALL)])
    TR1 = s_tr.n

    def ln_tile(t):
        P.emit("sync", lambda e, t=t: e.dma_start(
            xtile[0:128, 0:8 * TW],
            io["x_in"][0:1024, t * TW:(t + 1) * TW].rearrange(
                "(c p) w -> p c w", p=128)),
            waits=[(s_ev, s_ev.n)], inc=(s_x, 16))
        P.wait("scalar", s_x)
        P.wait("vector", s_x)
        emit_ln_tile(P, U, xtile, xn, TW, sems)

    ln_tile(0)
    P.wait("tensor", s_ev)
    for j in range(8):
        col = 2 * j + 1
        for c in range(8):
            P.emit("tensor", lambda e, c=c, col=col: e.transpose(
                U.pb_bf[0:1, c * 128:(c + 1) * 128],
                xn[0:128, c * TW + col:c * TW + col + 1],
                C["ident128b"][0:128, 0:128]),
                inc=(s_pe, 1) if c == 7 else None)
        P.emit("vector", lambda e, j=j: e.tensor_copy(
            xtok[0:1, j * 1024:(j + 1) * 1024], U.pb_bf[0:1, 0:1024]),
            waits=[(s_pe, s_pe.n)], inc=(s_ev, 1))
        P.wait("tensor", s_ev)
    P.wait("vector", s_gp, GP_ALL)
    for c in range(8):
        P.emit("vector", lambda e, c=c: e.tensor_copy(
            ocol[0:128, c * 8:c * 8 + 4],
            xn[0:128, c * TW + 2:c * TW + 10:2]))
        P.emit("vector", lambda e, c=c: e.tensor_copy(
            ocol[0:128, c * 8 + 4:c * 8 + 8],
            xn[0:128, c * TW + 10:c * TW + 18:2]),
            inc=(s_ev, 1) if c == 7 else None)
    for i in range(4):
        P.emit("vector", lambda e, i=i: e.tensor_scalar(
            erow[0:1, i * 1024:(i + 1) * 1024],
            xtok[0:1, i * 1024:(i + 1) * 1024],
            scl[0:1, 0:1], None, ALU.mult), inc=(s_ev, 1))
    for i in range(4):
        for pm in range(4):
            P.emit("vector", lambda e, i=i, pm=pm: e.tensor_scalar(
                erow[0:1, 4096 + (i * 4 + pm) * 256:4096 + (i * 4 + pm) * 256 + 256],
                xtok[0:1, (4 + i) * 1024 + pm:(5 + i) * 1024:4],
                scl[0:1, 0:1], None, ALU.mult),
                inc=(s_ev, 1))
    EXTRACT_EV = s_ev.n

    P.wait("tensor", s_tr, TR1)
    for t in range(8):
        if t > 0:
            ln_tile(t)
        P.wait("tensor", s_ev)
        P.wait("scalar", s_st)
        for i in range(4):
            for c in range(8):
                P.emit("tensor", lambda e, i=i, c=c: e.matmul(
                    U.pb[4][0:1, 0:TW], ocol[0:128, c * 8 + i:c * 8 + i + 1],
                    xn[0:128, c * TW:(c + 1) * TW],
                    start=(c == 0), stop=(c == 7)),
                    inc=(s_pe, 1) if c == 7 else None)
            P.emit("vector", lambda e, i=i: e.tensor_copy(
                dots[i][0:1, 0:TW], U.pb[4][0:1, 0:TW]),
                waits=[(s_pe, s_pe.n)], inc=(s_ev, 1))
            P.wait("tensor", s_ev)
        for hp in range(32):
            bank = U.pb[hp % 2]
            i = hp // 8
            cch = hp % 8
            P.wait("tensor", s_sc, s_sc.n)
            for c in range(8):
                P.emit("tensor", lambda e, hp=hp, c=c, bank=bank: e.matmul(
                    bank[0:128, 0:512],
                    WT[0:128, c * 4096 + hp * 128:c * 4096 + (hp + 1) * 128],
                    xn[0:128, c * TW:(c + 1) * TW], start=(c == 0), stop=False))
            P.emit("tensor", lambda e, i=i, cch=cch, bank=bank: e.matmul(
                bank[0:128, 0:512],
                erow[0:1, i * 1024 + cch * 128:i * 1024 + (cch + 1) * 128],
                dots[i][0:1, 0:TW], start=False, stop=True),
                inc=(s_pe, 1))
            P.emit("scalar", lambda e, hp=hp, bank=bank: e.activation(
                htile[0:128, hp * TW:(hp + 1) * TW], bank[0:128, 0:512],
                AF.Gelu, bias=bcol[0:128, hp:hp + 1]),
                waits=[(s_pe, s_pe.n)], inc=(s_sc, 1))
        P.emit("sync", lambda e, t=t: e.dma_start(
            io["h_scr"][0:4096, t * TW:(t + 1) * TW].rearrange(
                "(c p) w -> p c w", p=128),
            htile[0:128, 0:32 * TW]),
            waits=[(s_sc, s_sc.n)], inc=(s_st, 16))
    ST1 = s_st.n

    emit_transpose_load(P, io["W2bf"], 1024, 4096, WT, 1024, s_tr,
                        waits=[(s_st, ST1)])
    TR2 = s_tr.n

    P.wait("tensor", s_tr, TR2)
    P.wait("sync", s_st, ST1)
    for t in range(8):
        P.emit("sync", lambda e, t=t: e.dma_start(
            htile[0:128, 0:32 * TW],
            io["h_scr"][0:4096, t * TW:(t + 1) * TW].rearrange(
                "(c p) w -> p c w", p=128)),
            waits=[(s_ev, s_ev.n)], inc=(s_x, 16))
        P.emit("sync", lambda e, t=t: e.dma_start(
            xtile[0:128, 0:8 * TW],
            io["x_in"][0:1024, t * TW:(t + 1) * TW].rearrange(
                "(c p) w -> p c w", p=128)),
            inc=(s_x, 16))
        P.wait("tensor", s_x)
        P.wait("vector", s_x)
        P.wait("vector", s_st)
        for jj in range(4):
            for i in range(4):
                for c in range(8):
                    cc = jj * 8 + c
                    P.emit("tensor", lambda e, i=i, c=c, cc=cc: e.matmul(
                        U.pb[4][0:1, 0:TW], ocol[0:128, c * 8 + 4 + i:c * 8 + 5 + i],
                        htile[0:128, cc * TW:(cc + 1) * TW],
                        start=(c == 0), stop=(c == 7)),
                        inc=(s_pe, 1) if c == 7 else None)
                P.emit("vector", lambda e, jj=jj, i=i: e.tensor_copy(
                    d2sb[jj * 4 + i][0:1, 0:TW], U.pb[4][0:1, 0:TW]),
                    waits=[(s_pe, s_pe.n)], inc=(s_ev, 1))
                P.wait("tensor", s_ev)
        for op in range(8):
            bank = U.pb[2 + op % 2]
            i = op // 2
            P.wait("tensor", s_ev, s_ev.n)
            for cc in range(32):
                P.emit("tensor", lambda e, op=op, cc=cc, bank=bank: e.matmul(
                    bank[0:128, 0:512],
                    WT[0:128, cc * 1024 + op * 128:cc * 1024 + (op + 1) * 128],
                    htile[0:128, cc * TW:(cc + 1) * TW], start=(cc == 0), stop=False))
            for jj in range(4):
                P.emit("tensor", lambda e, op=op, jj=jj, i=i, bank=bank: e.matmul(
                    bank[0:128, 0:512],
                    erow[0:1, 4096 + (i * 4 + jj) * 256 + (op % 2) * 128:
                         4096 + (i * 4 + jj) * 256 + (op % 2) * 128 + 128],
                    d2sb[jj * 4 + i][0:1, 0:TW], start=False, stop=(jj == 3)),
                    inc=(s_pe, 1) if jj == 3 else None)
            P.emit("vector", lambda e, op=op, bank=bank: e.scalar_tensor_tensor(
                tmp[0:128, op * TW:(op + 1) * TW], bank[0:128, 0:512],
                bcol[0:128, 32 + op:33 + op], xtile[0:128, op * TW:(op + 1) * TW],
                ALU.add, ALU.add),
                waits=[(s_pe, s_pe.n)], inc=(s_ev, 1))
        P.emit("sync", lambda e, t=t: e.dma_start(
            io["x_out"][0:1024, t * TW:(t + 1) * TW].rearrange(
                "(c p) w -> p c w", p=128),
            tmp[0:128, 0:8 * TW]),
            waits=[(s_ev, s_ev.n)], inc=(s_st, 16))
    P.wait("sync", s_st)


def build_L1_attn_a(P, U, io, tag):
    nc, C = P.nc, U.C
    H = host_constants()
    TW = 512
    s_gp = P.sem(f"{tag}_gp")
    s_x = P.sem(f"{tag}_x")
    s_pe = P.sem(f"{tag}_pe")
    s_ev = P.sem(f"{tag}_ev")
    s_sc = P.sem(f"{tag}_sc")
    s_st = P.sem(f"{tag}_st")
    s_c = U.s_const
    sems = {"pe": s_pe, "ev": s_ev, "sc": s_sc}

    with ExitStack() as STC:
        sb = lambda name, shape, dtype: U.scoped_sb(STC, name, shape, dtype)
        xtile = sb(f"{tag}_xtile", [128, 8 * TW], F32)
        xin = sb(f"{tag}_xin", [128, 8 * TW], F32)
        xn = sb(f"{tag}_xn", [128, 8 * TW], BF16)
        normed = sb(f"{tag}_nrm", [128, 8 * TW], BF16)
        xsq = sb(f"{tag}_xsq", [128, 8 * TW], BF16)
        tmp = sb(f"{tag}_tmp", [128, 8 * TW], F32)
        stat = sb(f"{tag}_stat", [1, 2 * TW], F32)
        gcol = sb(f"{tag}_gcol", [128, 32], F32)
        brow = sb(f"{tag}_brow", [1, 9216], BF16)
        U.ln_scr = {"xsq": xsq, "stat": stat, "tmp": tmp}

        emit_load_bias_cols(P, io["ln1_g"], 1024, gcol, 0, s_gp)
        emit_load_bias_cols(P, io["ln1_b"], 1024, gcol, 8, s_gp)
        emit_load_bias_cols(P, io["ln2_g"], 1024, gcol, 16, s_gp)
        emit_load_bias_cols(P, io["ln2_b"], 1024, gcol, 24, s_gp)
        P.emit("gpsimd", lambda e: e.dma_start(brow[0:1, 0:3072], io["bqkv"]), inc=(s_gp, 16))
        P.emit("gpsimd", lambda e: e.dma_start(brow[0:1, 3072:4096], io["bo"]), inc=(s_gp, 16))
        P.emit("gpsimd", lambda e: e.dma_start(brow[0:1, 4096:8192], io["b1"]), inc=(s_gp, 16))
        P.emit("gpsimd", lambda e: e.dma_start(brow[0:1, 8192:9216], io["b2"]), inc=(s_gp, 16))

        with ExitStack() as STB:
            sbB = lambda name, shape, dtype: U.scoped_sb(STB, name, shape, dtype)
            WQKV = sbB(f"{tag}_wqkv", [128, 8 * 3072], BF16)
            acc = sbB(f"{tag}_acc", [128, 24 * TW], BF16)
            emit_cast_bf16(P, io["Wqkv"].rearrange("(c p) n -> p c n", p=128),
                           WQKV[0:128, 0:8 * 3072], s_gp)
            GP_ALL = s_gp.n
            for e_ in ("tensor", "vector", "scalar"):
                P.wait(e_, s_gp, GP_ALL)
            for t in range(8):
                P.emit("sync", lambda e, t=t: e.dma_start(
                    xtile[0:128, 0:8 * TW],
                    io["x_in"][0:1024, t * TW:(t + 1) * TW].rearrange(
                        "(c p) w -> p c w", p=128)),
                    waits=[(s_ev, s_ev.n)], inc=(s_x, 16))
                P.wait("scalar", s_x)
                P.wait("vector", s_x)
                P.wait("vector", s_st)
                emit_ln_tile(P, U, xtile, xn, TW, sems, extra_f32=xin)
                P.wait("sync", s_ev)
                P.emit("sync", lambda e, t=t: e.dma_start(
                    io["xin_scr"][0:1024, t * TW:(t + 1) * TW].rearrange(
                        "(c p) w -> p c w", p=128),
                    xin[0:128, 0:8 * TW]), inc=(s_st, 16))
                emit_ln_tile(P, U, xin, normed, TW, sems,
                             affine=(gcol[0:128, 0:8], gcol[0:128, 8:16]))
                P.wait("tensor", s_ev)
                for op in range(24):
                    bank = U.pb[op % 2]
                    P.wait("tensor", s_ev, s_ev.n)
                    for c in range(8):
                        P.emit("tensor", lambda e, op=op, c=c, bank=bank: e.matmul(
                            bank[0:128, 0:512],
                            WQKV[0:128, c * 3072 + op * 128:c * 3072 + (op + 1) * 128],
                            normed[0:128, c * TW:(c + 1) * TW], start=(c == 0), stop=False))
                    P.emit("tensor", lambda e, op=op, bank=bank: e.matmul(
                        bank[0:128, 0:512], brow[0:1, op * 128:(op + 1) * 128],
                        C["ones_row_b512"][0:1, 0:TW], start=False, stop=True),
                        inc=(s_pe, 1))
                    P.emit("vector", lambda e, op=op, bank=bank: e.tensor_copy(
                        acc[0:128, op * TW:(op + 1) * TW], bank[0:128, 0:512]),
                        waits=[(s_pe, s_pe.n)], inc=(s_ev, 1))
                P.emit("sync", lambda e, t=t: e.dma_start(
                    io["qkv_scr"][0:3072, t * TW:(t + 1) * TW].rearrange(
                        "(c p) w -> p c w", p=128),
                    acc[0:128, 0:24 * TW]),
                    waits=[(s_ev, s_ev.n)], inc=(s_st, 16))
        ST_B = s_st.n
        P.barrier()

        with ExitStack() as STCC:
            sbC = lambda name, shape, dtype: U.scoped_sb(STCC, name, shape, dtype)
            rc = U.add_const(tag + "ropec", H["rope_cos"].astype(BF), BF16, stack=STCC)
            rs2_np = H["rope_sin"].copy()
            rs2_np[0:32] = -rs2_np[0:32]
            rs = U.add_const(tag + "ropes2", rs2_np.astype(BF), BF16, stack=STCC)
            swap_np = np.zeros((64, 64), np.float32)
            for m in range(64):
                swap_np[(m + 32) % 64, m] = 1.0
            swp = U.add_const(tag + "swap64", swap_np.astype(BF), BF16, stack=STCC)
            qs = U.scoped_sb(STCC, f"{tag}_qs", [64, T], BF16)
            cm4 = np.zeros((128, 2048), np.float32)
            for a in range(4):
                i_ = np.arange(128)[:, None]
                j_ = np.arange(512)[None, :]
                cm4[:, a * 512:(a + 1) * 512] = np.where(j_ > a * 128 + i_, -1e9, 0.0)
            cmask = U.add_const(tag + "cmask", cm4, F32, stack=STCC)
            qh = sbC(f"{tag}_qh", [64, T], BF16)
            kh = sbC(f"{tag}_kh", [64, T], BF16)
            vh = sbC(f"{tag}_vh", [64, T], BF16)
            qr = sbC(f"{tag}_qr", [64, T], BF16)
            kr = sbC(f"{tag}_kr", [64, T], BF16)
            vtok = sbC(f"{tag}_vtok", [128, 32 * 64], BF16)
            Pbuf = sbC(f"{tag}_P", [128, T], BF16)
            PT = sbC(f"{tag}_PT", [128, T], BF16)
            oh = sbC(f"{tag}_oh", [64, T], BF16)
            lrow = sbC(f"{tag}_l", [128, 16], F32)
            P.wait("sync", s_st, ST_B)
            P.wait("vector", s_c)
            P.wait("tensor", s_c)
            for h in range(16):
                for (dst, off) in ((qh, 0), (kh, 1024), (vh, 2048)):
                    P.emit("sync", lambda e, dst=dst, off=off, h=h: e.dma_start(
                        dst[0:64, 0:T],
                        io["qkv_scr"][off + h * 64:off + (h + 1) * 64, 0:T]),
                        waits=[(s_ev, s_ev.n), (s_pe, s_pe.n), (s_st, s_st.n)],
                        inc=(s_x, 16))
                P.wait("vector", s_x)
                P.wait("vector", s_st)
                P.wait("tensor", s_x)
                for (src_, dst) in ((qh, qr), (kh, kr)):
                    for blk in range(8):
                        P.emit("tensor", lambda e, src=src_, blk=blk: e.matmul(
                            U.pb[3][0:64, 0:512], swp[0:64, 0:64],
                            src[0:64, blk * 512:(blk + 1) * 512],
                            start=True, stop=True),
                            waits=[(s_ev, s_ev.n)], inc=(s_pe, 1))
                        P.emit("vector", lambda e, blk=blk: e.tensor_copy(
                            qs[0:64, blk * 512:(blk + 1) * 512], U.pb[3][0:64, 0:512]),
                            waits=[(s_pe, s_pe.n)], inc=(s_ev, 1))
                    P.wait("vector", s_ev)
                    P.emit("vector", lambda e: e.tensor_mul(
                        qs[0:64, 0:T], qs[0:64, 0:T], rs[0:64, 0:T]))
                    P.emit("vector", lambda e, src=src_, dst=dst: e.tensor_mul(
                        dst[0:64, 0:T], src[0:64, 0:T], rc[0:64, 0:T]))
                    P.emit("vector", lambda e, dst=dst: e.tensor_add(
                        dst[0:64, 0:T], dst[0:64, 0:T], qs[0:64, 0:T]),
                        inc=(s_ev, 1))
                for kc in range(0, 32, 4):
                    for k2 in range(4):
                        P.emit("tensor", lambda e, kc=kc, k2=k2: e.transpose(
                            U.pb_bf[0:128, k2 * 64:(k2 + 1) * 64],
                            vh[0:64, (kc + k2) * 128:(kc + k2 + 1) * 128],
                            C["ident128b"][0:64, 0:64]),
                            inc=(s_pe, 1) if k2 == 3 else None)
                    P.emit("vector", lambda e, kc=kc: e.tensor_copy(
                        vtok[0:128, kc * 64:(kc + 4) * 64], U.pb_bf[0:128, 0:256]),
                        waits=[(s_pe, s_pe.n)], inc=(s_ev, 1))
                    P.wait("tensor", s_ev)
                P.wait("tensor", s_ev)
                if h == 0 and "dbg_qr" in io:
                    P.emit("sync", lambda e: e.dma_start(io["dbg_qr"], qr[0:64, 0:T]),
                           waits=[(s_ev, s_ev.n)], inc=(s_st, 16))
                    P.emit("sync", lambda e: e.dma_start(io["dbg_kr"], kr[0:64, 0:T]),
                           inc=(s_st, 16))
                    P.emit("sync", lambda e: e.dma_start(io["dbg_vt"], vtok[0:128, 0:2048]),
                           inc=(s_st, 16))
                for qt in range(32):
                    nb = (qt * 128 + 128 + 511) // 512
                    a = qt % 4
                    for b in range(nb):
                        bank = U.pb[b % 2]
                        P.wait("tensor", s_sc, s_sc.n)
                        P.emit("tensor", lambda e, qt=qt, b=b, bank=bank: e.matmul(
                            bank[0:128, 0:512], qr[0:64, qt * 128:(qt + 1) * 128],
                            kr[0:64, b * 512:(b + 1) * 512], start=True, stop=True),
                            inc=(s_pe, 1))
                        if b == nb - 1:
                            P.emit("vector", lambda e, a=a, bank=bank: e.tensor_add(
                                bank[0:128, 0:512], bank[0:128, 0:512],
                                cmask[0:128, a * 512:(a + 1) * 512]),
                                waits=[(s_pe, s_pe.n)], inc=(s_ev, 1))
                            P.emit("scalar", lambda e, b=b, bank=bank: e.activation(
                                Pbuf[0:128, b * 512:(b + 1) * 512], bank[0:128, 0:512],
                                AF.Exp, bias=C["epsb"][0:128, 2:3], scale=0.125),
                                waits=[(s_ev, s_ev.n)], inc=(s_sc, 1))
                            P.emit("vector", lambda e, b=b: e.reduce_sum(
                                lrow[0:128, b:b + 1], Pbuf[0:128, b * 512:(b + 1) * 512],
                                axis=AX.X),
                                waits=[(s_sc, s_sc.n)], inc=(s_ev, 1))
                        else:
                            P.emit("scalar", lambda e, b=b, bank=bank: e.activation(
                                Pbuf[0:128, b * 512:(b + 1) * 512], bank[0:128, 0:512],
                                AF.Exp, bias=C["epsb"][0:128, 2:3], scale=0.125,
                                accum_out=lrow[0:128, b:b + 1]),
                                waits=[(s_pe, s_pe.n), (s_ev, s_ev.n)], inc=(s_sc, 1))
                    P.wait("vector", s_sc)
                    P.emit("vector", lambda e, nb=nb: e.reduce_sum(
                        lrow[0:128, 8:9], lrow[0:128, 0:nb], axis=AX.X),
                        waits=[(s_ev, s_ev.n)], inc=(s_ev, 1))
                    P.emit("vector", lambda e: e.reciprocal(
                        lrow[0:128, 9:10], lrow[0:128, 8:9]),
                        waits=[(s_ev, s_ev.n)], inc=(s_ev, 1))
                    P.emit("vector", lambda e, nb=nb: e.tensor_scalar(
                        Pbuf[0:128, 0:nb * 512], Pbuf[0:128, 0:nb * 512],
                        lrow[0:128, 9:10], None, ALU.mult),
                        waits=[(s_ev, s_ev.n)], inc=(s_ev, 1))
                    P.wait("tensor", s_ev)
                    nkc = nb * 4
                    for kc0 in range(0, nkc, 4):
                        kn = min(4, nkc - kc0)
                        for k2 in range(kn):
                            P.emit("tensor", lambda e, kc0=kc0, k2=k2: e.transpose(
                                U.pb_bf[0:128, k2 * 128:(k2 + 1) * 128],
                                Pbuf[0:128, (kc0 + k2) * 128:(kc0 + k2 + 1) * 128],
                                C["ident128b"][0:128, 0:128]),
                                inc=(s_pe, 1) if k2 == kn - 1 else None)
                        P.emit("vector", lambda e, kc0=kc0, kn=kn: e.tensor_copy(
                            PT[0:128, kc0 * 128:(kc0 + kn) * 128],
                            U.pb_bf[0:128, 0:kn * 128]),
                            waits=[(s_pe, s_pe.n)], inc=(s_ev, 1))
                        P.wait("tensor", s_ev)
                    for kc in range(nkc):
                        P.emit("tensor", lambda e, kc=kc, nkc=nkc: e.matmul(
                            U.pb[2][0:64, 0:128], vtok[0:128, kc * 64:(kc + 1) * 64],
                            PT[0:128, kc * 128:(kc + 1) * 128],
                            start=(kc == 0), stop=(kc == nkc - 1)),
                            inc=(s_pe, 1) if kc == nkc - 1 else None)
                    P.emit("vector", lambda e, qt=qt: e.tensor_copy(
                        oh[0:64, qt * 128:(qt + 1) * 128], U.pb[2][0:64, 0:128]),
                        waits=[(s_pe, s_pe.n)], inc=(s_ev, 1))
                    P.wait("tensor", s_ev)
                    if h == 0 and qt == 4 and "dbg_P" in io:
                        P.emit("sync", lambda e: e.dma_start(io["dbg_P"], Pbuf[0:128, 0:1024]),
                               waits=[(s_ev, s_ev.n)], inc=(s_st, 16))
                        P.emit("sync", lambda e: e.dma_start(io["dbg_l"], lrow[0:128, 0:16]),
                               inc=(s_st, 16))
                P.emit("sync", lambda e, h=h: e.dma_start(
                    io["attn_scr"][h * 64:(h + 1) * 64, 0:T], oh[0:64, 0:T]),
                    waits=[(s_ev, s_ev.n)], inc=(s_st, 16))
        ST_C = s_st.n
        P.barrier()

        with ExitStack() as STD:
            sbD = lambda name, shape, dtype: U.scoped_sb(STD, name, shape, dtype)
            WO = sbD(f"{tag}_wo", [128, 8 * 1024], BF16)
            accD = sbD(f"{tag}_accD", [128, 8 * TW], BF16)
            P.wait("gpsimd", s_st, ST_C)
            emit_cast_bf16(P, io["Wo"].rearrange("(c p) n -> p c n", p=128),
                           WO[0:128, 0:8 * 1024], s_gp)
            P.wait("tensor", s_gp)
            P.wait("sync", s_st, ST_C)
            for t in range(8):
                P.emit("sync", lambda e, t=t: e.dma_start(
                    accD[0:128, 0:8 * TW],
                    io["attn_scr"][0:1024, t * TW:(t + 1) * TW].rearrange(
                        "(c p) w -> p c w", p=128)),
                    waits=[(s_ev, s_ev.n)], inc=(s_x, 16))
                P.emit("sync", lambda e, t=t: e.dma_start(
                    xin[0:128, 0:8 * TW],
                    io["xin_scr"][0:1024, t * TW:(t + 1) * TW].rearrange(
                        "(c p) w -> p c w", p=128)), inc=(s_x, 16))
                P.wait("tensor", s_x)
                P.wait("vector", s_x)
                P.wait("vector", s_st)
                for op in range(8):
                    bank = U.pb[op % 2]
                    P.wait("tensor", s_ev, s_ev.n)
                    for c in range(8):
                        P.emit("tensor", lambda e, op=op, c=c, bank=bank: e.matmul(
                            bank[0:128, 0:512],
                            WO[0:128, c * 1024 + op * 128:c * 1024 + (op + 1) * 128],
                            accD[0:128, c * TW:(c + 1) * TW], start=(c == 0), stop=False))
                    P.emit("tensor", lambda e, op=op, bank=bank: e.matmul(
                        bank[0:128, 0:512], brow[0:1, 3072 + op * 128:3072 + (op + 1) * 128],
                        C["ones_row_b512"][0:1, 0:TW], start=False, stop=True),
                        inc=(s_pe, 1))
                    P.emit("vector", lambda e, op=op, bank=bank: e.tensor_add(
                        tmp[0:128, op * TW:(op + 1) * TW], bank[0:128, 0:512],
                        xin[0:128, op * TW:(op + 1) * TW]),
                        waits=[(s_pe, s_pe.n)], inc=(s_ev, 1))
                P.emit("sync", lambda e, t=t: e.dma_start(
                    io["xa_scr"][0:1024, t * TW:(t + 1) * TW].rearrange(
                        "(c p) w -> p c w", p=128),
                    tmp[0:128, 0:8 * TW]),
                    waits=[(s_ev, s_ev.n)], inc=(s_st, 16))
        ST_D = s_st.n
        P.barrier()

        with ExitStack() as STE:
            sbE = lambda name, shape, dtype: U.scoped_sb(STE, name, shape, dtype)
            W1 = sbE(f"{tag}_W1", [128, 8 * 4096], BF16)
            ffn1 = sbE(f"{tag}_ffn1", [128, 32 * TW], BF16)
            P.wait("gpsimd", s_st, ST_D)
            emit_cast_bf16(P, io["W1"].rearrange("(c p) n -> p c n", p=128),
                           W1[0:128, 0:8 * 4096], s_gp)
            P.wait("tensor", s_gp)
            P.wait("sync", s_st, ST_D)
            for t in range(8):
                P.emit("sync", lambda e, t=t: e.dma_start(
                    xin[0:128, 0:8 * TW],
                    io["xa_scr"][0:1024, t * TW:(t + 1) * TW].rearrange(
                        "(c p) w -> p c w", p=128)),
                    waits=[(s_ev, s_ev.n)], inc=(s_x, 16))
                P.wait("scalar", s_x)
                P.wait("vector", s_x)
                emit_ln_tile(P, U, xin, normed, TW, sems,
                             affine=(gcol[0:128, 16:24], gcol[0:128, 24:32]))
                P.wait("tensor", s_ev)
                P.wait("scalar", s_st)
                for hp in range(32):
                    bank = U.pb[hp % 2]
                    P.wait("tensor", s_sc, s_sc.n)
                    for c in range(8):
                        P.emit("tensor", lambda e, hp=hp, c=c, bank=bank: e.matmul(
                            bank[0:128, 0:512],
                            W1[0:128, c * 4096 + hp * 128:c * 4096 + (hp + 1) * 128],
                            normed[0:128, c * TW:(c + 1) * TW], start=(c == 0), stop=False))
                    P.emit("tensor", lambda e, hp=hp, bank=bank: e.matmul(
                        bank[0:128, 0:512], brow[0:1, 4096 + hp * 128:4096 + (hp + 1) * 128],
                        C["ones_row_b512"][0:1, 0:TW], start=False, stop=True),
                        inc=(s_pe, 1))
                    P.emit("scalar", lambda e, hp=hp, bank=bank: e.activation(
                        ffn1[0:128, hp * TW:(hp + 1) * TW], bank[0:128, 0:512], AF.Gelu),
                        waits=[(s_pe, s_pe.n)], inc=(s_sc, 1))
                P.emit("sync", lambda e, t=t: e.dma_start(
                    io["h_scr"][0:4096, t * TW:(t + 1) * TW].rearrange(
                        "(c p) w -> p c w", p=128),
                    ffn1[0:128, 0:32 * TW]),
                    waits=[(s_sc, s_sc.n)], inc=(s_st, 16))
        ST_E1 = s_st.n
        P.barrier()

        with ExitStack() as STE2:
            sbE2 = lambda name, shape, dtype: U.scoped_sb(STE2, name, shape, dtype)
            W2 = sbE2(f"{tag}_W2", [128, 32 * 1024], BF16)
            ffn1b = sbE2(f"{tag}_ffn1b", [128, 32 * TW], BF16)
            P.wait("gpsimd", s_st, ST_E1)
            emit_cast_bf16(P, io["W2"].rearrange("(c p) n -> p c n", p=128),
                           W2[0:128, 0:32 * 1024], s_gp)
            P.wait("tensor", s_gp)
            P.wait("sync", s_st, ST_E1)
            for t in range(8):
                P.emit("sync", lambda e, t=t: e.dma_start(
                    ffn1b[0:128, 0:32 * TW],
                    io["h_scr"][0:4096, t * TW:(t + 1) * TW].rearrange(
                        "(c p) w -> p c w", p=128)),
                    waits=[(s_ev, s_ev.n)], inc=(s_x, 16))
                P.emit("sync", lambda e, t=t: e.dma_start(
                    xin[0:128, 0:8 * TW],
                    io["xa_scr"][0:1024, t * TW:(t + 1) * TW].rearrange(
                        "(c p) w -> p c w", p=128)), inc=(s_x, 16))
                P.emit("sync", lambda e, t=t: e.dma_start(
                    xtile[0:128, 0:8 * TW],
                    io["x_in"][0:1024, t * TW:(t + 1) * TW].rearrange(
                        "(c p) w -> p c w", p=128)), inc=(s_x, 16))
                P.wait("tensor", s_x)
                P.wait("vector", s_x)
                P.wait("vector", s_st)
                for op in range(8):
                    bank = U.pb[op % 2]
                    P.wait("tensor", s_ev, s_ev.n)
                    for cc in range(32):
                        P.emit("tensor", lambda e, op=op, cc=cc, bank=bank: e.matmul(
                            bank[0:128, 0:512],
                            W2[0:128, cc * 1024 + op * 128:cc * 1024 + (op + 1) * 128],
                            ffn1b[0:128, cc * TW:(cc + 1) * TW], start=(cc == 0), stop=False))
                    P.emit("tensor", lambda e, op=op, bank=bank: e.matmul(
                        bank[0:128, 0:512], brow[0:1, 8192 + op * 128:8192 + (op + 1) * 128],
                        C["ones_row_b512"][0:1, 0:TW], start=False, stop=True),
                        inc=(s_pe, 1))
                    P.emit("vector", lambda e, op=op, bank=bank: e.tensor_add(
                        tmp[0:128, op * TW:(op + 1) * TW], bank[0:128, 0:512],
                        xin[0:128, op * TW:(op + 1) * TW]),
                        waits=[(s_pe, s_pe.n)])
                    P.emit("vector", lambda e, op=op: e.tensor_add(
                        tmp[0:128, op * TW:(op + 1) * TW],
                        tmp[0:128, op * TW:(op + 1) * TW],
                        xtile[0:128, op * TW:(op + 1) * TW]), inc=(s_ev, 1))
                P.emit("sync", lambda e, t=t: e.dma_start(
                    io["x_out"][0:1024, t * TW:(t + 1) * TW].rearrange(
                        "(c p) w -> p c w", p=128),
                    tmp[0:128, 0:8 * TW]),
                    waits=[(s_ev, s_ev.n)], inc=(s_st, 16))
            P.wait("sync", s_st)
    P.barrier()


def build_moe_prologue(P, U, io, tag):
    s_gp = P.sem(f"{tag}_pgp")
    s_tr = P.sem(f"{tag}_ptr")
    s_st = P.sem(f"{tag}_pst")
    with ExitStack() as ST:
        wt = U.scoped_sb(ST, f"{tag}_wtbuf", [128, 5632], BF16)
        wt2 = U.scoped_sb(ST, f"{tag}_wtbuf2", [128, 5632], BF16)
        for e in range(8):
            P.emit("gpsimd", lambda en, e=e: en.dma_start(
                io["up_bf"][e * 5632:(e + 1) * 5632, 0:1024],
                io["w_up"][e * 5632:(e + 1) * 5632, 0:1024]), inc=(s_gp, 16))
            P.emit("gpsimd", lambda en, e=e: en.dma_start(
                io["down_bf"][e * 1024:(e + 1) * 1024, 0:2816],
                io["w_down"][e * 1024:(e + 1) * 1024, 0:2816]), inc=(s_gp, 16))
        GP = s_gp.n
        for e in range(8):
            for j in range(8):
                buf = wt if j % 2 == 0 else wt2
                P.emit("sync", lambda en, e=e, j=j, buf=buf: en.dma_start(
                    buf[0:128, 0:5632],
                    io["up_bf"][e * 5632:(e + 1) * 5632, j * 128:(j + 1) * 128],
                    transpose=True),
                    waits=[(s_gp, GP), (s_st, max(0, s_st.n - 16))], inc=(s_tr, 16))
                P.emit("sync", lambda en, e=e, j=j, buf=buf: en.dma_start(
                    io["w_upT_bf"][e * 1024 + j * 128:e * 1024 + (j + 1) * 128, 0:5632],
                    buf[0:128, 0:5632]),
                    waits=[(s_tr, s_tr.n)], inc=(s_st, 16))
        for e in range(8):
            for j in range(22):
                buf = wt if j % 2 == 0 else wt2
                P.emit("sync", lambda en, e=e, j=j, buf=buf: en.dma_start(
                    buf[0:128, 0:1024],
                    io["down_bf"][e * 1024:(e + 1) * 1024, j * 128:(j + 1) * 128],
                    transpose=True),
                    waits=[(s_st, max(0, s_st.n - 16))], inc=(s_tr, 16))
                P.emit("sync", lambda en, e=e, j=j, buf=buf: en.dma_start(
                    io["w_downT_bf"][e * 2816 + j * 128:e * 2816 + (j + 1) * 128, 0:1024],
                    buf[0:128, 0:1024]),
                    waits=[(s_tr, s_tr.n)], inc=(s_st, 16))
        P.wait("sync", s_st)
    P.barrier()


def build_L1_ffn(P, U, io, tag):
    nc, C = P.nc, U.C
    TW = 512
    s_gp = P.sem(f"{tag}_gp")
    s_x = P.sem(f"{tag}_x")
    s_pe = P.sem(f"{tag}_pe")
    s_ev = P.sem(f"{tag}_ev")
    s_sc = P.sem(f"{tag}_sc")
    s_st = P.sem(f"{tag}_st")
    sems = {"pe": s_pe, "ev": s_ev, "sc": s_sc}
    with ExitStack() as ST0:
        c_tok = U.scoped_sb(ST0, f"{tag}_ctok", [128, 256], F32)
        with ExitStack() as STN:
            sbN = lambda n, s, d: U.scoped_sb(STN, n, s, d)
            xtile = sbN(f"{tag}_xt", [128, 8 * TW], F32)
            xin_t = sbN(f"{tag}_xint", [128, 8 * TW], F32)
            xn = sbN(f"{tag}_xn", [128, 8 * TW], BF16)
            xsq = sbN(f"{tag}_xsq", [128, 8 * TW], BF16)
            tmp = sbN(f"{tag}_tmp", [128, 8 * TW], F32)
            stat = sbN(f"{tag}_stat", [1, 2 * TW], F32)
            scol = sbN(f"{tag}_scol", [128, 8], F32)
            rrow = sbN(f"{tag}_rr", [8, 1024], F32)
            rTs = sbN(f"{tag}_rTs", [128, 64], F32)
            rt = sbN(f"{tag}_rt", [128, 48], F32)
            rs = sbN(f"{tag}_rs", [128, 8], F32)
            U.ln_scr = {"xsq": xsq, "stat": stat, "tmp": tmp}
            emit_load_bias_cols(P, io["norm_scale"], 1024, scol, 0, s_gp)
            P.emit("gpsimd", lambda e: e.dma_start(rrow[0:8, 0:1024], io["w_router"]),
                   inc=(s_gp, 16))
            GP_N = s_gp.n
            P.wait("tensor", s_gp, GP_N)
            P.wait("vector", s_gp, GP_N)
            for c in range(8):
                P.emit("tensor", lambda e, c=c: e.transpose(
                    U.pb[4][0:128, 0:8], rrow[0:8, c * 128:(c + 1) * 128],
                    C["ident128"][0:8, 0:8]), inc=(s_pe, 1))
                P.emit("vector", lambda e, c=c: e.tensor_scalar(
                    rTs[0:128, c * 8:(c + 1) * 8], U.pb[4][0:128, 0:8],
                    scol[0:128, c:c + 1], None, ALU.mult),
                    waits=[(s_pe, s_pe.n)], inc=(s_ev, 1))
                P.wait("tensor", s_ev)
            for t in range(8):
                P.emit("sync", lambda e, t=t: e.dma_start(
                    xtile[0:128, 0:8 * TW],
                    io["x_in"][0:1024, t * TW:(t + 1) * TW].rearrange(
                        "(c p) w -> p c w", p=128)),
                    waits=[(s_ev, s_ev.n)], inc=(s_x, 16))
                P.wait("scalar", s_x)
                P.wait("vector", s_x)
                P.wait("vector", s_st)
                emit_ln_tile(P, U, xtile, xn, TW, sems, extra_f32=xin_t)
                P.wait("sync", s_ev)
                P.emit("sync", lambda e, t=t: e.dma_start(
                    io["xin_scr"][0:1024, t * TW:(t + 1) * TW].rearrange(
                        "(c p) w -> p c w", p=128),
                    xin_t[0:128, 0:8 * TW]), inc=(s_st, 16))
                emit_ln_tile(P, U, xin_t, xn, TW, sems, eps_col=1,
                             affine=(scol, None), rms_only=True)
                P.wait("sync", s_ev)
                P.emit("sync", lambda e, t=t: e.dma_start(
                    io["xn_scr"][0:1024, t * TW:(t + 1) * TW].rearrange(
                        "(c p) w -> p c w", p=128),
                    xn[0:128, 0:8 * TW]), inc=(s_st, 16))
                P.wait("tensor", s_ev)
                for i in range(4):
                    tt = t * 4 + i
                    for c in range(8):
                        P.emit("tensor", lambda e, c=c, i=i: e.matmul(
                            U.pb[4][0:128, 0:8],
                            xin_t[0:128, c * TW + i * 128:c * TW + (i + 1) * 128],
                            rTs[0:128, c * 8:(c + 1) * 8],
                            start=(c == 0), stop=(c == 7)),
                            waits=[(s_ev, s_ev.n)] if c == 0 else ())
                    P.emit("tensor", lambda e, i=i: e.transpose(
                        U.pb[4][0:128, 8:9], stat[0:1, TW + i * 128:TW + (i + 1) * 128],
                        C["ident128"][0:1, 0:1]), inc=(s_pe, 1))
                    def ve(fn, w=None):
                        P.emit("vector", fn,
                               waits=[(s_ev, s_ev.n)] if w is None else w,
                               inc=(s_ev, 1))
                    ve(lambda e: e.tensor_copy(rt[0:128, 0:9], U.pb[4][0:128, 0:9]),
                       w=[(s_pe, s_pe.n)])
                    ve(lambda e: e.tensor_scalar(rt[0:128, 0:8], rt[0:128, 0:8],
                                                 rt[0:128, 8:9], None, ALU.mult))
                    ve(lambda e: e.reduce_max(rs[0:128, 0:1], rt[0:128, 0:8], axis=AX.X))
                    ve(lambda e: e.tensor_scalar(rt[0:128, 8:16], rt[0:128, 0:8],
                                                 rs[0:128, 0:1], None, ALU.is_ge))
                    ve(lambda e: e.scalar_tensor_tensor(rt[0:128, 16:24], rt[0:128, 8:16],
                                                        -1e9, rt[0:128, 0:8],
                                                        ALU.mult, ALU.add))
                    ve(lambda e: e.reduce_max(rs[0:128, 1:2], rt[0:128, 16:24], axis=AX.X))
                    ve(lambda e: e.tensor_scalar(rt[0:128, 24:32], rt[0:128, 16:24],
                                                 rs[0:128, 1:2], None, ALU.is_ge))
                    ve(lambda e: e.tensor_add(rt[0:128, 32:40], rt[0:128, 8:16],
                                              rt[0:128, 24:32]))
                    ve(lambda e: e.tensor_scalar(rs[0:128, 2:3], rs[0:128, 0:1],
                                                 -1.0, None, ALU.mult))
                    P.emit("scalar", lambda e: e.activation(
                        rt[0:128, 40:48], rt[0:128, 0:8], AF.Exp,
                        bias=rs[0:128, 2:3], accum_out=rs[0:128, 3:4]),
                        waits=[(s_ev, s_ev.n)], inc=(s_sc, 1))
                    ve(lambda e: e.reciprocal(rs[0:128, 4:5], rs[0:128, 3:4]),
                       w=[(s_sc, s_sc.n)])
                    ve(lambda e, tt=tt: e.scalar_tensor_tensor(
                        c_tok[0:128, tt * 8:(tt + 1) * 8], rt[0:128, 40:48],
                        rs[0:128, 4:5], rt[0:128, 32:40], ALU.mult, ALU.mult))
        ST_N = s_st.n
        P.barrier()

        with ExitStack() as STE:
            sbE = lambda n, s, d: U.scoped_sb(STE, n, s, d)
            xn_q = sbE(f"{tag}_xnq", [128, 8 * 1024], BF16)
            wbuf = sbE(f"{tag}_wbuf", [128, 8 * 5632], BF16)
            hm = sbE(f"{tag}_hm", [128, 22 * 1024], BF16)
            y_acc = sbE(f"{tag}_yacc", [128, 8 * 1024], F32)
            sg = sbE(f"{tag}_sg", [128, TW], BF16)
            x_q = wbuf[0:128, 0:16384].bitcast(F32)
            xin_q = wbuf[0:128, 16384:32768].bitcast(F32)
            for q in range(4):
                P.emit("sync", lambda e, q=q: e.dma_start(
                    xn_q[0:128, 0:8 * 1024],
                    io["xn_scr"][0:1024, q * 1024:(q + 1) * 1024].rearrange(
                        "(c p) w -> p c w", p=128)),
                    waits=[(s_st, ST_N), (s_ev, s_ev.n)], inc=(s_x, 16))
                P.wait("vector", s_x)
                P.wait("vector", s_st)
                P.wait("vector", s_pe)
                P.emit("vector", lambda e: e.memset(y_acc[0:128, 0:8 * 1024], 0.0),
                       inc=(s_ev, 1))
                P.wait("tensor", s_x)
                for ex in range(8):
                    P.emit("sync", lambda e, ex=ex: e.dma_start(
                        wbuf[0:128, 0:8 * 5632],
                        io["w_upT_bf"][ex * 1024:(ex + 1) * 1024, 0:5632].rearrange(
                            "(c p) n -> p c n", p=128)),
                        waits=[(s_pe, s_pe.n), (s_ev, s_ev.n), (s_st, s_st.n)],
                        inc=(s_x, 16))
                    P.wait("tensor", s_x)
                    for th in range(2):
                        for hp in range(22):
                            for c in range(8):
                                P.emit("tensor", lambda e, c=c, hp=hp, th=th: e.matmul(
                                    U.pb[0][0:128, 0:512],
                                    wbuf[0:128, c * 5632 + hp * 128:c * 5632 + (hp + 1) * 128],
                                    xn_q[0:128, c * 1024 + th * TW:c * 1024 + (th + 1) * TW],
                                    start=(c == 0), stop=(c == 7)),
                                    waits=[(s_ev, s_ev.n)] if c == 0 else ())
                            for c in range(8):
                                P.emit("tensor", lambda e, c=c, hp=hp, th=th: e.matmul(
                                    U.pb[1][0:128, 0:512],
                                    wbuf[0:128, c * 5632 + 2816 + hp * 128:
                                         c * 5632 + 2816 + (hp + 1) * 128],
                                    xn_q[0:128, c * 1024 + th * TW:c * 1024 + (th + 1) * TW],
                                    start=(c == 0), stop=(c == 7)),
                                    inc=(s_pe, 1) if c == 7 else None)
                            P.emit("scalar", lambda e: e.activation(
                                sg[0:128, 0:TW], U.pb[1][0:128, 0:512], AF.Silu),
                                waits=[(s_sc, s_sc.n), (s_pe, s_pe.n)], inc=(s_sc, 1))
                            P.emit("vector", lambda e, hp=hp, th=th: e.tensor_mul(
                                hm[0:128, hp * 1024 + th * TW:hp * 1024 + (th + 1) * TW],
                                U.pb[0][0:128, 0:512], sg[0:128, 0:TW]),
                                waits=[(s_sc, s_sc.n)], inc=(s_ev, 1))
                    P.emit("sync", lambda e, ex=ex: e.dma_start(
                        wbuf[0:128, 0:22 * 1024],
                        io["w_downT_bf"][ex * 2816:(ex + 1) * 2816, 0:1024].rearrange(
                            "(c p) n -> p c n", p=128)),
                        waits=[(s_pe, s_pe.n), (s_ev, s_ev.n)], inc=(s_x, 16))
                    P.wait("tensor", s_x)
                    for t1 in range(8):
                        tt = q * 8 + t1
                        for half in range(2):
                            bank = U.pb[2 + half]
                            for hc in range(22):
                                P.emit("tensor", lambda e, hc=hc, t1=t1, half=half, bank=bank: e.matmul(
                                    bank[0:128, 0:512],
                                    hm[0:128, hc * 1024 + t1 * 128:hc * 1024 + (t1 + 1) * 128],
                                    wbuf[0:128, hc * 1024 + half * 512:hc * 1024 + (half + 1) * 512],
                                    start=(hc == 0), stop=(hc == 21)),
                                    waits=[(s_ev, s_ev.n)] if (hc == 0) else (),
                                    inc=(s_pe, 1) if hc == 21 else None)
                            P.emit("vector", lambda e, t1=t1, half=half, tt=tt, ex=ex, bank=bank:
                                   e.scalar_tensor_tensor(
                                y_acc[0:128, t1 * 1024 + half * 512:t1 * 1024 + (half + 1) * 512],
                                bank[0:128, 0:512],
                                c_tok[0:128, tt * 8 + ex:tt * 8 + ex + 1],
                                y_acc[0:128, t1 * 1024 + half * 512:t1 * 1024 + (half + 1) * 512],
                                ALU.mult, ALU.add),
                                waits=[(s_pe, s_pe.n)], inc=(s_ev, 1))
                P.emit("sync", lambda e, q=q: e.dma_start(
                    x_q[0:128, 0:8 * 1024],
                    io["x_in"][0:1024, q * 1024:(q + 1) * 1024].rearrange(
                        "(c p) w -> p c w", p=128)),
                    waits=[(s_st, s_st.n), (s_pe, s_pe.n), (s_sc, s_sc.n),
                           (s_ev, s_ev.n)], inc=(s_x, 16))
                P.emit("sync", lambda e, q=q: e.dma_start(
                    xin_q[0:128, 0:8 * 1024],
                    io["xin_scr"][0:1024, q * 1024:(q + 1) * 1024].rearrange(
                        "(c p) w -> p c w", p=128)), inc=(s_x, 16))
                P.wait("vector", s_x)
                P.wait("tensor", s_x)
                for t1 in range(8):
                    for dc in range(8):
                        P.emit("tensor", lambda e, t1=t1, dc=dc: e.transpose(
                            U.pb[4][0:128, 0:128],
                            y_acc[0:128, t1 * 1024 + dc * 128:t1 * 1024 + (dc + 1) * 128],
                            C["ident128"][0:128, 0:128]),
                            waits=[(s_ev, s_ev.n)], inc=(s_pe, 1))
                        P.emit("vector", lambda e, t1=t1, dc=dc: e.tensor_add(
                            x_q[0:128, dc * 1024 + t1 * 128:dc * 1024 + (t1 + 1) * 128],
                            U.pb[4][0:128, 0:128],
                            x_q[0:128, dc * 1024 + t1 * 128:dc * 1024 + (t1 + 1) * 128]),
                            waits=[(s_pe, s_pe.n)], inc=(s_ev, 1))
                        P.emit("vector", lambda e, t1=t1, dc=dc: e.tensor_add(
                            x_q[0:128, dc * 1024 + t1 * 128:dc * 1024 + (t1 + 1) * 128],
                            x_q[0:128, dc * 1024 + t1 * 128:dc * 1024 + (t1 + 1) * 128],
                            xin_q[0:128, dc * 1024 + t1 * 128:dc * 1024 + (t1 + 1) * 128]),
                            waits=[(s_ev, s_ev.n)], inc=(s_ev, 1))
                        P.wait("tensor", s_ev)
                P.emit("sync", lambda e, q=q: e.dma_start(
                    io["x_out"][0:1024, q * 1024:(q + 1) * 1024].rearrange(
                        "(c p) w -> p c w", p=128),
                    x_q[0:128, 0:8 * 1024]),
                    waits=[(s_ev, s_ev.n)], inc=(s_st, 16))
            P.wait("sync", s_st)
    P.barrier()


def build_L0_attn_a(P, U, io, tag):
    nc, C = P.nc, U.C
    H = host_constants()
    TW = 512
    s_gp = P.sem(f"{tag}_gp")
    s_tr = P.sem(f"{tag}_tr")
    s_x = P.sem(f"{tag}_x")
    s_pe = P.sem(f"{tag}_pe")
    s_ev = P.sem(f"{tag}_ev")
    s_sc = P.sem(f"{tag}_sc")
    s_st = P.sem(f"{tag}_st")
    s_c = U.s_const
    sems = {"pe": s_pe, "ev": s_ev, "sc": s_sc}

    def vsync(fn, extra=()):
        P.emit("vector", fn, waits=[(s_ev, s_ev.n)] + list(extra), inc=(s_ev, 1))

    def pack128(arr, nch):
        n = arr.shape[1]
        return np.ascontiguousarray(
            arr.reshape(nch, 128, n).transpose(1, 0, 2).reshape(128, nch * n))

    with ExitStack() as ST0:
        sb0 = lambda n, s, d: U.scoped_sb(ST0, n, s, d)
        xn = sb0(f"{tag}_xn", [128, 8 * T], BF16)
        Hsb = sb0(f"{tag}_H", [128, 8 * T], BF16)
        BrT = sb0(f"{tag}_BrT", [128, 8 * 1024], BF16)
        lam = sb0(f"{tag}_lam", [128, 8], F32)
        pw = sb0(f"{tag}_pw", [128, 32], F32)
        Dcol = sb0(f"{tag}_D", [128, 8], F32)
        omdbg = sb0(f"{tag}_omdbg", [128, 4], F32)
        acol = sb0(f"{tag}_acol", [128, 8], F32)
        P.emit("vector", lambda e: e.memset(acol[0:128, 0:8], 0.0), inc=(s_ev, 1))
        for j in range(4):
            P.emit("gpsimd", lambda e, j=j: e.dma_start(
                acol[0:128, 2 * j:2 * j + 1], io["a_params"][j * 128:(j + 1) * 128]),
                waits=[(s_ev, s_ev.n)], inc=(s_gp, 16))
        emit_load_bias_cols(P, io["Dd"], 1024, Dcol, 0, s_gp)
        GP0 = s_gp.n

        with ExitStack() as STL:
            sbL = lambda n, s, d: U.scoped_sb(STL, n, s, d)
            xtile = sbL(f"{tag}_xt", [128, 8 * TW], F32)
            xsq = sbL(f"{tag}_xsq", [128, 8 * TW], BF16)
            tmp = sbL(f"{tag}_tmp", [128, 8 * TW], F32)
            stat = sbL(f"{tag}_stat", [1, 2 * TW], F32)
            U.ln_scr = {"xsq": xsq, "stat": stat, "tmp": tmp}
            for t in range(8):
                P.emit("sync", lambda e, t=t: e.dma_start(
                    xtile[0:128, 0:8 * TW],
                    io["x_in"][0:1024, t * TW:(t + 1) * TW].rearrange(
                        "(c p) w -> p c w", p=128)),
                    waits=[(s_ev, s_ev.n)], inc=(s_x, 16))
                P.wait("scalar", s_x)
                P.wait("vector", s_x)
                emit_ln_tile(P, U, xtile, None, TW, sems,
                             out_fn=lambda c, t=t: xn[0:128, c * T + t * TW:
                                                      c * T + t * TW + TW])
        P.barrier()

        with ExitStack() as STO:
            Momk = U.add_const(
                tag + "Momk",
                pack128(np.ascontiguousarray(H["M_om"][:512, :]), 4), F32, stack=STO)
            arow = U.scoped_sb(STO, f"{tag}_arow", [1, 512], F32)
            tmpo = U.scoped_sb(STO, f"{tag}_tmpo", [128, 512], F32)
            P.emit("gpsimd", lambda e: e.dma_start(arow[0:1, 0:512], io["a_params"]),
                   inc=(s_gp, 16))
            GPA = s_gp.n
            P.wait("tensor", s_gp, GPA)
            P.wait("tensor", s_c)
            P.wait("vector", s_c)
            P.emit("tensor", lambda e: e.matmul(
                U.pb[4][0:128, 0:512], C["ones_row_f"][0:1, 0:128],
                arow[0:1, 0:512], start=True, stop=True), inc=(s_pe, 1))
            P.wait("vector", s_pe)
            for kp in range(4):
                vsync(lambda e, kp=kp: e.tensor_mul(
                    tmpo[0:128, 0:512], Momk[0:128, kp * 512:(kp + 1) * 512],
                    U.pb[4][0:128, 0:512]))
                vsync(lambda e, kp=kp: e.reduce_sum(
                    pw[0:128, 20 + kp:21 + kp], tmpo[0:128, 0:512], axis=AX.X))
            P.wait("tensor", s_ev)
            vsync(lambda e: e.tensor_copy(omdbg[0:128, 0:4], pw[0:128, 20:24]))
            vsync(lambda e: e.tensor_mul(pw[0:128, 0:4], pw[0:128, 20:24], pw[0:128, 20:24]))
            vsync(lambda e: e.tensor_scalar(pw[0:128, 4:8], pw[0:128, 0:4], 1.0, None, ALU.add))
            vsync(lambda e: e.reciprocal(pw[0:128, 8:12], pw[0:128, 4:8]))
            vsync(lambda e: e.tensor_scalar(pw[0:128, 0:4], pw[0:128, 0:4], -1.0, 1.0,
                                            ALU.mult, ALU.add))
            vsync(lambda e: e.tensor_mul(lam[0:128, 0:4], pw[0:128, 0:4], pw[0:128, 8:12]))
            vsync(lambda e: e.tensor_scalar(pw[0:128, 12:16], pw[0:128, 20:24], -2.0, None,
                                            ALU.mult))
            vsync(lambda e: e.tensor_mul(lam[0:128, 4:8], pw[0:128, 12:16], pw[0:128, 8:12]))

        if "dbg_xn2" in io:
            P.emit("sync", lambda e: e.dma_start(io["dbg_xn2"], xn[0:128, 0:8 * T]),
                   waits=[(s_ev, s_ev.n)], inc=(s_st, 16))
        if "dbg_lam" in io:
            P.emit("sync", lambda e: e.dma_start(io["dbg_lam"], lam[0:128, 0:8]),
                   waits=[(s_ev, s_ev.n)], inc=(s_st, 16))
        P.barrier()
        with ExitStack() as STB:
            sbB = lambda n, s, d: U.scoped_sb(STB, n, s, d)
            Bw = sbB(f"{tag}_Bw", [128, 8 * 1024], BF16)
            Ppt = U.add_const(tag + "Ppt", pack128(np.asarray(H["P_packT"], np.float32), 8).astype(BF),
                              BF16, stack=STB)
            emit_cast_bf16(P, io["B_w"].rearrange("(c p) n -> p c n", p=128),
                           Bw[0:128, 0:8 * 1024], s_gp)
            P.wait("tensor", s_gp)
            P.wait("tensor", s_c)
            for dp in range(8):
                for half in range(2):
                    bank = U.pb[half]
                    for j in range(8):
                        P.emit("tensor", lambda e, dp=dp, half=half, j=j, bank=bank: e.matmul(
                            bank[0:128, 0:512],
                            Bw[0:128, j * 1024 + dp * 128:j * 1024 + (dp + 1) * 128],
                            Ppt[0:128, j * 1024 + half * 512:j * 1024 + (half + 1) * 512],
                            start=(j == 0), stop=(j == 7)),
                            waits=[(s_ev, s_ev.n)] if j == 0 else (),
                            inc=(s_pe, 1) if j == 7 else None)
                    P.emit("vector", lambda e, dp=dp, half=half, bank=bank: e.tensor_copy(
                        BrT[0:128, dp * 1024 + half * 512:dp * 1024 + (half + 1) * 512],
                        bank[0:128, 0:512]),
                        waits=[(s_pe, s_pe.n)], inc=(s_ev, 1))
        if "dbg_brt" in io:
            P.emit("sync", lambda e: e.dma_start(io["dbg_brt"], BrT[0:128, 0:8192]),
                   waits=[(s_ev, s_ev.n)], inc=(s_st, 16))
        P.barrier()

        with ExitStack() as STS:
            sbS = lambda n, s, d: U.scoped_sb(STS, n, s, d)
            vre = sbS(f"{tag}_vre", [128, T], BF16)
            vim = sbS(f"{tag}_vim", [128, T], BF16)
            Mre = sbS(f"{tag}_Mre", [128, 1024], F32)
            Mim = sbS(f"{tag}_Mim", [128, 1024], F32)
            Tre = sbS(f"{tag}_Tre", [128, 1024], F32)
            Tim = sbS(f"{tag}_Tim", [128, 1024], F32)
            sre = sbS(f"{tag}_sre", [128, 1024], F32)
            sim = sbS(f"{tag}_sim", [128, 1024], F32)
            tt1 = sbS(f"{tag}_tt1", [128, 1024], F32)
            tt2 = sbS(f"{tag}_tt2", [128, 1024], F32)
            tt3 = sbS(f"{tag}_tt3", [128, 1024], F32)
            for p in range(4):
                for (dst, pkoff, bank) in ((vre, p * 128, U.pb[0]),
                                           (vim, 512 + p * 128, U.pb[1])):
                    for t in range(8):
                        for cc in range(8):
                            P.emit("tensor", lambda e, pkoff=pkoff, t=t, cc=cc, bank=bank: e.matmul(
                                bank[0:128, 0:512],
                                BrT[0:128, cc * 1024 + pkoff:cc * 1024 + pkoff + 128],
                                xn[0:128, cc * T + t * TW:cc * T + (t + 1) * TW],
                                start=(cc == 0), stop=(cc == 7)),
                                waits=[(s_ev, s_ev.n)] if cc == 0 else (),
                                inc=(s_pe, 1) if cc == 7 else None)
                        P.emit("vector", lambda e, dst=dst, t=t, bank=bank: e.tensor_copy(
                            dst[0:128, t * TW:(t + 1) * TW], bank[0:128, 0:512]),
                            waits=[(s_pe, s_pe.n)], inc=(s_ev, 1))
                vsync(lambda e, p=p: e.tensor_copy(pw[0:128, 0:1], lam[0:128, p:p + 1]))
                vsync(lambda e, p=p: e.tensor_scalar(pw[0:128, 16:17], lam[0:128, 4 + p:5 + p],
                                                     -1.0, None, ALU.mult))
                for j in range(10):
                    vsync(lambda e, j=j: e.tensor_mul(pw[0:128, 12:13], pw[0:128, j:j + 1],
                                                      pw[0:128, j:j + 1]))
                    vsync(lambda e, j=j: e.tensor_mul(pw[0:128, 13:14], pw[0:128, 16 + j:17 + j],
                                                      pw[0:128, 16 + j:17 + j]))
                    vsync(lambda e, j=j: e.tensor_sub(pw[0:128, j + 1:j + 2], pw[0:128, 12:13],
                                                      pw[0:128, 13:14]))
                    vsync(lambda e, j=j: e.tensor_mul(pw[0:128, 14:15], pw[0:128, j:j + 1],
                                                      pw[0:128, 16 + j:17 + j]))
                    vsync(lambda e, j=j: e.tensor_scalar(pw[0:128, 17 + j:18 + j],
                                                         pw[0:128, 14:15], 2.0, None, ALU.mult))
                vsync(lambda e: e.memset(Mre[0:128, 0:1], 1.0))
                vsync(lambda e: e.memset(Mim[0:128, 0:1], 0.0))
                for j in range(10):
                    w = 1 << j
                    vsync(lambda e, j=j, w=w: e.tensor_scalar(
                        tt1[0:128, 0:w], Mre[0:128, 0:w], pw[0:128, j:j + 1], None, ALU.mult))
                    vsync(lambda e, j=j, w=w: e.tensor_scalar(
                        tt2[0:128, 0:w], Mim[0:128, 0:w], pw[0:128, 16 + j:17 + j], None, ALU.mult))
                    vsync(lambda e, j=j, w=w: e.tensor_scalar(
                        tt3[0:128, 0:w], Mre[0:128, 0:w], pw[0:128, 16 + j:17 + j], None, ALU.mult))
                    vsync(lambda e, j=j, w=w: e.scalar_tensor_tensor(
                        Mim[0:128, w:2 * w], Mim[0:128, 0:w], pw[0:128, j:j + 1],
                        tt3[0:128, 0:w], ALU.mult, ALU.add))
                    vsync(lambda e, j=j, w=w: e.tensor_sub(
                        Mre[0:128, w:2 * w], tt1[0:128, 0:w], tt2[0:128, 0:w]))
                vsync(lambda e: e.memset(pw[0:128, 30:31], 1.0))
                vsync(lambda e: e.memset(pw[0:128, 31:32], 0.0))
                for cch in range(4):
                    sl = slice(cch * 1024, (cch + 1) * 1024)
                    vsync(lambda e: e.tensor_scalar(tt1[0:128, 0:1024], Mre[0:128, 0:1024],
                                                    pw[0:128, 30:31], None, ALU.mult))
                    vsync(lambda e: e.tensor_scalar(tt2[0:128, 0:1024], Mim[0:128, 0:1024],
                                                    pw[0:128, 31:32], None, ALU.mult))
                    vsync(lambda e: e.tensor_sub(Tre[0:128, 0:1024], tt1[0:128, 0:1024],
                                                 tt2[0:128, 0:1024]))
                    vsync(lambda e: e.tensor_scalar(tt1[0:128, 0:1024], Mre[0:128, 0:1024],
                                                    pw[0:128, 31:32], None, ALU.mult))
                    vsync(lambda e: e.tensor_scalar(tt2[0:128, 0:1024], Mim[0:128, 0:1024],
                                                    pw[0:128, 30:31], None, ALU.mult))
                    vsync(lambda e: e.tensor_add(Tim[0:128, 0:1024], tt1[0:128, 0:1024],
                                                 tt2[0:128, 0:1024]))
                    vsync(lambda e, sl=sl: e.tensor_mul(tt1[0:128, 0:1024], Tre[0:128, 0:1024],
                                                        vre[0:128, sl]))
                    vsync(lambda e, sl=sl: e.tensor_mul(tt3[0:128, 0:1024], Tim[0:128, 0:1024],
                                                        vim[0:128, sl]))
                    vsync(lambda e: e.tensor_sub(tt1[0:128, 0:1024], tt1[0:128, 0:1024],
                                                 tt3[0:128, 0:1024]))
                    vsync(lambda e, sl=sl: e.tensor_mul(tt2[0:128, 0:1024], Tre[0:128, 0:1024],
                                                        vim[0:128, sl]))
                    vsync(lambda e, sl=sl: e.tensor_mul(tt3[0:128, 0:1024], Tim[0:128, 0:1024],
                                                        vre[0:128, sl]))
                    vsync(lambda e: e.tensor_add(tt2[0:128, 0:1024], tt2[0:128, 0:1024],
                                                 tt3[0:128, 0:1024]))
                    ini_re = 0.0 if cch == 0 else pw[0:128, 28:29]
                    ini_im = 0.0 if cch == 0 else pw[0:128, 29:30]
                    vsync(lambda e, ini=ini_re: e.tensor_tensor_scan(
                        sre[0:128, 0:1024], tt1[0:128, 0:1024], tt1[0:128, 0:1024],
                        ini, ALU.add, ALU.bypass))
                    vsync(lambda e, ini=ini_im: e.tensor_tensor_scan(
                        sim[0:128, 0:1024], tt2[0:128, 0:1024], tt2[0:128, 0:1024],
                        ini, ALU.add, ALU.bypass))
                    vsync(lambda e: e.tensor_copy(pw[0:128, 28:29], sre[0:128, 1023:1024]))
                    vsync(lambda e: e.tensor_copy(pw[0:128, 29:30], sim[0:128, 1023:1024]))
                    vsync(lambda e: e.tensor_mul(tt1[0:128, 0:1024], Tre[0:128, 0:1024],
                                                 sre[0:128, 0:1024]))
                    vsync(lambda e: e.tensor_mul(tt2[0:128, 0:1024], Tim[0:128, 0:1024],
                                                 sim[0:128, 0:1024]))
                    vsync(lambda e, p=p, cch=cch: e.tensor_add(
                        Hsb[0:128, p * T + cch * 1024:p * T + (cch + 1) * 1024],
                        tt1[0:128, 0:1024], tt2[0:128, 0:1024]))
                    vsync(lambda e: e.tensor_mul(tt1[0:128, 0:1024], Tre[0:128, 0:1024],
                                                 sim[0:128, 0:1024]))
                    vsync(lambda e: e.tensor_mul(tt2[0:128, 0:1024], Tim[0:128, 0:1024],
                                                 sre[0:128, 0:1024]))
                    vsync(lambda e, p=p, cch=cch: e.tensor_sub(
                        Hsb[0:128, (4 + p) * T + cch * 1024:(4 + p) * T + (cch + 1) * 1024],
                        tt1[0:128, 0:1024], tt2[0:128, 0:1024]))
                    vsync(lambda e: e.tensor_mul(pw[0:128, 24:25], pw[0:128, 30:31],
                                                 pw[0:128, 10:11]))
                    vsync(lambda e: e.tensor_mul(pw[0:128, 25:26], pw[0:128, 31:32],
                                                 pw[0:128, 26:27]))
                    vsync(lambda e: e.tensor_mul(pw[0:128, 27:28], pw[0:128, 30:31],
                                                 pw[0:128, 26:27]))
                    vsync(lambda e: e.tensor_mul(pw[0:128, 22:23], pw[0:128, 31:32],
                                                 pw[0:128, 10:11]))
                    vsync(lambda e: e.tensor_sub(pw[0:128, 30:31], pw[0:128, 24:25],
                                                 pw[0:128, 25:26]))
                    vsync(lambda e: e.tensor_add(pw[0:128, 31:32], pw[0:128, 27:28],
                                                 pw[0:128, 22:23]))
        if "dbg_h" in io:
            P.emit("sync", lambda e: e.dma_start(io["dbg_h"], Hsb[0:128, 0:8 * T]),
                   waits=[(s_ev, s_ev.n)], inc=(s_st, 16))
            P.emit("sync", lambda e: e.dma_start(io["dbg_xn"], xn[0:128, 0:8 * T]),
                   inc=(s_st, 16))
        P.barrier()

        with ExitStack() as STC:
            sbC = lambda n, s, d: U.scoped_sb(STC, n, s, d)
            CrT = sbC(f"{tag}_CrT", [128, 8 * 1024], BF16)
            STC1 = STC.enter_context(ExitStack())
            CwT = U.scoped_sb(STC1, f"{tag}_CwT", [128, 8 * 1024], BF16)
            Gc = U.add_const(tag + "G", pack128(np.asarray(H["G"], np.float32), 8).astype(BF),
                             BF16, stack=STC1)
            emit_cast_bf16(P, io["C_w"], io["C_wbf"], s_gp)
            GPC = s_gp.n
            emit_transpose_load(P, io["C_wbf"], 1024, 1024, CwT, 1024, s_tr,
                                waits=[(s_gp, GPC)])
            P.wait("tensor", s_tr)
            P.wait("tensor", s_c)
            for pkp in range(8):
                for half in range(2):
                    bank = U.pb[half]
                    for j in range(8):
                        P.emit("tensor", lambda e, pkp=pkp, half=half, j=j, bank=bank: e.matmul(
                            bank[0:128, 0:512],
                            Gc[0:128, j * 1024 + pkp * 128:j * 1024 + (pkp + 1) * 128],
                            CwT[0:128, j * 1024 + half * 512:j * 1024 + (half + 1) * 512],
                            start=(j == 0), stop=(j == 7)),
                            waits=[(s_ev, s_ev.n)] if j == 0 else (),
                            inc=(s_pe, 1) if j == 7 else None)
                    P.emit("vector", lambda e, pkp=pkp, half=half, bank=bank: e.tensor_copy(
                        CrT[0:128, pkp * 1024 + half * 512:pkp * 1024 + (half + 1) * 512],
                        bank[0:128, 0:512]),
                        waits=[(s_pe, s_pe.n)], inc=(s_ev, 1))
            if "dbg_crt" in io:
                P.emit("sync", lambda e: e.dma_start(io["dbg_crt"], CrT[0:128, 0:8192]),
                       waits=[(s_ev, s_ev.n)], inc=(s_st, 16))
            STC1.close()
            P.barrier()
            xtile2 = sbC(f"{tag}_xt2", [128, 8 * TW], F32)
            out2 = sbC(f"{tag}_out2", [128, 8 * TW], F32)
            for t in range(8):
                P.emit("sync", lambda e, t=t: e.dma_start(
                    xtile2[0:128, 0:8 * TW],
                    io["x_in"][0:1024, t * TW:(t + 1) * TW].rearrange(
                        "(c p) w -> p c w", p=128)),
                    waits=[(s_st, s_st.n), (s_ev, s_ev.n)], inc=(s_x, 16))
                P.wait("tensor", s_x)
                P.wait("vector", s_x)
                P.wait("vector", s_st)
                for dp in range(8):
                    bank = U.pb[2 + dp % 2]
                    P.wait("tensor", s_ev, s_ev.n)
                    for pk in range(8):
                        P.emit("tensor", lambda e, dp=dp, pk=pk, t=t, bank=bank: e.matmul(
                            bank[0:128, 0:512],
                            CrT[0:128, pk * 1024 + dp * 128:pk * 1024 + (dp + 1) * 128],
                            Hsb[0:128, pk * T + t * TW:pk * T + (t + 1) * TW],
                            start=(pk == 0), stop=(pk == 7)),
                            inc=(s_pe, 1) if pk == 7 else None)
                    P.emit("vector", lambda e, dp=dp, bank=bank: e.tensor_add(
                        out2[0:128, dp * TW:(dp + 1) * TW], bank[0:128, 0:512],
                        xtile2[0:128, dp * TW:(dp + 1) * TW]),
                        waits=[(s_pe, s_pe.n)], inc=(s_ev, 1))
                    P.emit("vector", lambda e, dp=dp, t=t: e.scalar_tensor_tensor(
                        out2[0:128, dp * TW:(dp + 1) * TW],
                        xn[0:128, dp * T + t * TW:dp * T + (t + 1) * TW],
                        Dcol[0:128, dp:dp + 1],
                        out2[0:128, dp * TW:(dp + 1) * TW], ALU.mult, ALU.add),
                        waits=[(s_ev, s_ev.n)], inc=(s_ev, 1))
                P.emit("sync", lambda e, t=t: e.dma_start(
                    io["x_out"][0:1024, t * TW:(t + 1) * TW].rearrange(
                        "(c p) w -> p c w", p=128),
                    out2[0:128, 0:8 * TW]),
                    waits=[(s_ev, s_ev.n)], inc=(s_st, 16))
            P.wait("sync", s_st)
    P.barrier()


def _enc_layer(P, U, io, tag, li, kvs, gcol, brow, ln_sweep, elu1_evac,
               sems, s_gp, s_x, s_pe, s_ev, s_sc, s_st):
    nc, C = P.nc, U.C
    TW = 512
    P.emit("gpsimd", lambda e, li=li: e.dma_start(
        brow[0:1, 0:1024], io["bq"][li * 1024:(li + 1) * 1024]), inc=(s_gp, 16))
    P.emit("gpsimd", lambda e, li=li: e.dma_start(
        brow[0:1, 1024:2048], io["bk"][li * 1024:(li + 1) * 1024]), inc=(s_gp, 16))
    P.emit("gpsimd", lambda e, li=li: e.dma_start(
        brow[0:1, 2048:3072], io["bv"][li * 1024:(li + 1) * 1024]), inc=(s_gp, 16))
    P.emit("gpsimd", lambda e, li=li: e.dma_start(
        brow[0:1, 3072:4096], io["bo"][li * 1024:(li + 1) * 1024]), inc=(s_gp, 16))
    P.emit("gpsimd", lambda e, li=li: e.dma_start(
        brow[0:1, 4096:8192], io["b1"][li * 4096:(li + 1) * 4096]), inc=(s_gp, 16))
    P.emit("gpsimd", lambda e, li=li: e.dma_start(
        brow[0:1, 8192:9216], io["b2"][li * 1024:(li + 1) * 1024]), inc=(s_gp, 16))
    emit_load_bias_cols(P, io["ln1_s"][li * 1024:(li + 1) * 1024], 1024, gcol, 0, s_gp)
    emit_load_bias_cols(P, io["ln1_b"][li * 1024:(li + 1) * 1024], 1024, gcol, 8, s_gp)
    emit_load_bias_cols(P, io["ln2_s"][li * 1024:(li + 1) * 1024], 1024, gcol, 16, s_gp)
    emit_load_bias_cols(P, io["ln2_b"][li * 1024:(li + 1) * 1024], 1024, gcol, 24, s_gp)

    with ExitStack() as SA:
        sbA = lambda n, s, d: U.scoped_sb(SA, n, s, d)
        Wk = sbA(f"{tag}_wk{li}", [128, 8 * 1024], BF16)
        Wv = sbA(f"{tag}_wv{li}", [128, 8 * 1024], BF16)
        xtile_pa = sbA(f"{tag}_xtA{li}", [128, 8 * TW], F32)
        xn_pa = sbA(f"{tag}_xnA{li}", [128, 8 * TW], BF16)
        xsq_pa = sbA(f"{tag}_xsqA{li}", [128, 8 * TW], BF16)
        tmp_pa = sbA(f"{tag}_tmpA{li}", [128, 8 * TW], F32)
        stat_pa = sbA(f"{tag}_statA{li}", [1, 2 * TW], F32)
        kp_sb_pa = sbA(f"{tag}_kp{li}", [128, 1024], BF16)
        v_sb_pa = sbA(f"{tag}_vv{li}", [128, 1024], BF16)
        t1_pa = sbA(f"{tag}_t1{li}", [128, TW], F32)
        t2_pa = sbA(f"{tag}_t2{li}", [128, TW], F32)
        U.ln_scr = {"xsq": xsq_pa, "stat": stat_pa, "tmp": tmp_pa}
        emit_cast_bf16(P, io["Wk"][li * 1024:(li + 1) * 1024, 0:1024]
                       .rearrange("(c p) n -> p c n", p=128), Wk[0:128, 0:8192], s_gp)
        emit_cast_bf16(P, io["Wv"][li * 1024:(li + 1) * 1024, 0:1024]
                       .rearrange("(c p) n -> p c n", p=128), Wv[0:128, 0:8192], s_gp)
        GPA = s_gp.n
        P.wait("tensor", s_gp, GPA)
        P.wait("scalar", s_gp, GPA)
        P.wait("vector", s_gp, GPA)
        first_bank_touch = [True, True, True]
        for t in range(8):
            ln_sweep(io["enc_scr"][0:1024, t * TW:(t + 1) * TW].rearrange(
                "(c p) w -> p c w", p=128), None, xtile_pa, xn_pa,
                (gcol[0:128, 0:8], gcol[0:128, 8:16]))
            P.wait("tensor", s_ev)
            for i4 in range(4):
                for (Wm, boff, dst) in ((Wk, 1024, kp_sb_pa), (Wv, 2048, v_sb_pa)):
                    for half in range(2):
                        bank = U.pb[0]
                        for c in range(8):
                            P.emit("tensor", lambda e, Wm=Wm, c=c, half=half, i4=i4, bank=bank: e.matmul(
                                bank[0:128, 0:512],
                                xn_pa[0:128, c * TW + i4 * 128:c * TW + (i4 + 1) * 128],
                                Wm[0:128, c * 1024 + half * 512:c * 1024 + (half + 1) * 512],
                                start=(c == 0), stop=False),
                                waits=[(s_ev, s_ev.n)] if c == 0 else ())
                        P.emit("tensor", lambda e, boff=boff, half=half, bank=bank: e.matmul(
                            bank[0:128, 0:512], C["ones_row_b"][0:1, 0:128],
                            brow[0:1, boff + half * 512:boff + (half + 1) * 512],
                            start=False, stop=True), inc=(s_pe, 1))
                        if dst is kp_sb_pa:
                            elu1_evac(bank, kp_sb_pa[0:128, half * 512:(half + 1) * 512], t1_pa, t2_pa)
                        else:
                            P.emit("vector", lambda e, half=half, bank=bank: e.tensor_copy(
                                v_sb_pa[0:128, half * 512:(half + 1) * 512],
                                bank[0:128, 0:512]),
                                waits=[(s_pe, s_pe.n)], inc=(s_ev, 1))
                P.wait("tensor", s_ev)
                for h in range(16):
                    kvbank = U.pb[2] if h < 8 else U.pb[3]
                    col = (h % 8) * 64
                    stt = first_bank_touch[0 if h < 8 else 1]
                    P.emit("tensor", lambda e, h=h, col=col, kvbank=kvbank, stt=stt: e.matmul(
                        kvbank[0:64, col:col + 64],
                        kp_sb_pa[0:128, h * 64:(h + 1) * 64],
                        v_sb_pa[0:128, h * 64:(h + 1) * 64],
                        start=stt, stop=False, skip_group_check=True))
                    if h < 8:
                        first_bank_touch[0] = False
                    else:
                        first_bank_touch[1] = False
                    P.emit("tensor", lambda e, h=h, stt=first_bank_touch[2]: e.matmul(
                        U.pb[4][0:64, 2 * h:2 * h + 2],
                        kp_sb_pa[0:128, h * 64:(h + 1) * 64],
                        C["ones_col2_b"][0:128, 0:2],
                        start=stt, stop=False, skip_group_check=True),
                        inc=(s_pe, 1) if (h == 15) else None)
                    first_bank_touch[2] = False
        P.wait("vector", s_pe)
        for h in range(16):
            kvbank = U.pb[2] if h < 8 else U.pb[3]
            col = (h % 8) * 64
            po = (h % 2) * 64
            P.emit("vector", lambda e, h=h, col=col, po=po, kvbank=kvbank: e.tensor_copy(
                kvs[po:po + 64, (h // 2) * 65:(h // 2) * 65 + 64],
                kvbank[0:64, col:col + 64]))
            P.emit("vector", lambda e, h=h, po=po: e.tensor_copy(
                kvs[po:po + 64, (h // 2) * 65 + 64:(h // 2) * 65 + 65],
                U.pb[4][0:64, 2 * h:2 * h + 1]),
                inc=(s_ev, 1) if h == 15 else None)
    P.barrier()

    with ExitStack() as SB:
        sbB = lambda n, s, d: U.scoped_sb(SB, n, s, d)
        Wq = sbB(f"{tag}_wq{li}", [128, 8 * 1024], BF16)
        Wo = sbB(f"{tag}_wo{li}", [128, 8 * 1024], BF16)
        xtile_pb = sbB(f"{tag}_xtB{li}", [128, 8 * TW], F32)
        xn_pb = sbB(f"{tag}_xnB{li}", [128, 8 * TW], BF16)
        xsq_pb = sbB(f"{tag}_xsqB{li}", [128, 8 * TW], BF16)
        tmp_pb = sbB(f"{tag}_tmpB{li}", [128, 8 * TW], F32)
        stat_pb = sbB(f"{tag}_statB{li}", [1, 2 * TW], F32)
        qp_sb_pb = sbB(f"{tag}_qp{li}", [128, 8 * TW], BF16)
        at_tok_pb = sbB(f"{tag}_atok{li}", [128, 1024], BF16)
        atT_pb = sbB(f"{tag}_atT{li}", [128, 8 * TW], BF16)
        t1_pb = sbB(f"{tag}_t1B{li}", [128, TW], F32)
        t2_pb = sbB(f"{tag}_t2B{li}", [128, TW], F32)
        zrow_pb = sbB(f"{tag}_z{li}", [128, 4], F32)
        U.ln_scr = {"xsq": xsq_pb, "stat": stat_pb, "tmp": tmp_pb}
        emit_cast_bf16(P, io["Wq"][li * 1024:(li + 1) * 1024, 0:1024]
                       .rearrange("(c p) n -> p c n", p=128), Wq[0:128, 0:8192], s_gp)
        emit_cast_bf16(P, io["Wo"][li * 1024:(li + 1) * 1024, 0:1024]
                       .rearrange("(c p) n -> p c n", p=128), Wo[0:128, 0:8192], s_gp)
        GPB = s_gp.n
        P.wait("tensor", s_gp, GPB)
        for t in range(8):
            ln_sweep(io["enc_scr"][0:1024, t * TW:(t + 1) * TW].rearrange(
                "(c p) w -> p c w", p=128), None, xtile_pb, xn_pb,
                (gcol[0:128, 0:8], gcol[0:128, 8:16]))
            P.wait("tensor", s_ev)
            for op in range(8):
                bank = U.pb[0]
                for c in range(8):
                    P.emit("tensor", lambda e, op=op, c=c, bank=bank: e.matmul(
                        bank[0:128, 0:512],
                        Wq[0:128, c * 1024 + op * 128:c * 1024 + (op + 1) * 128],
                        xn_pb[0:128, c * TW:(c + 1) * TW], start=(c == 0), stop=False),
                        waits=[(s_ev, s_ev.n)] if c == 0 else ())
                P.emit("tensor", lambda e, op=op, bank=bank: e.matmul(
                    bank[0:128, 0:512], brow[0:1, op * 128:(op + 1) * 128],
                    C["ones_row_b512"][0:1, 0:TW], start=False, stop=True),
                    inc=(s_pe, 1))
                elu1_evac(bank, qp_sb_pb[0:128, op * TW:(op + 1) * TW], t1_pb, t2_pb)
            P.wait("tensor", s_ev)
            for i4 in range(4):
                for h in range(16):
                    po = (h % 2) * 64
                    P.emit("tensor", lambda e, h=h, i4=i4, po=po: e.matmul(
                        U.pb[2][0:128, 0:65],
                        qp_sb_pb[po:po + 64, (h // 2) * TW + i4 * 128:(h // 2) * TW + (i4 + 1) * 128],
                        kvs[po:po + 64, (h // 2) * 65:(h // 2 + 1) * 65],
                        start=True, stop=True),
                        waits=[(s_ev, s_ev.n)], inc=(s_pe, 1))
                    P.emit("vector", lambda e: e.tensor_scalar(
                        zrow_pb[0:128, 0:1], U.pb[2][0:128, 64:65], 1e-6, None, ALU.add),
                        waits=[(s_pe, s_pe.n)], inc=(s_ev, 1))
                    P.emit("vector", lambda e: e.reciprocal(
                        zrow_pb[0:128, 1:2], zrow_pb[0:128, 0:1]),
                        waits=[(s_ev, s_ev.n)], inc=(s_ev, 1))
                    P.emit("vector", lambda e, h=h: e.tensor_scalar(
                        at_tok_pb[0:128, h * 64:(h + 1) * 64], U.pb[2][0:128, 0:64],
                        zrow_pb[0:128, 1:2], None, ALU.mult),
                        waits=[(s_ev, s_ev.n)], inc=(s_ev, 1))
                P.wait("tensor", s_ev)
                for dc in range(8):
                    P.emit("tensor", lambda e, dc=dc: e.transpose(
                        U.pb_bf[0:128, 0:128],
                        at_tok_pb[0:128, dc * 128:(dc + 1) * 128],
                        C["ident128b"][0:128, 0:128]), inc=(s_pe, 1))
                    P.emit("vector", lambda e, dc=dc, i4=i4: e.tensor_copy(
                        atT_pb[0:128, dc * TW + i4 * 128:dc * TW + (i4 + 1) * 128],
                        U.pb_bf[0:128, 0:128]),
                        waits=[(s_pe, s_pe.n)], inc=(s_ev, 1))
                    P.wait("tensor", s_ev)
            for op in range(8):
                bank = U.pb[0 if op % 2 == 0 else 1]
                P.wait("tensor", s_ev, s_ev.n)
                for c in range(8):
                    P.emit("tensor", lambda e, op=op, c=c, bank=bank: e.matmul(
                        bank[0:128, 0:512],
                        Wo[0:128, c * 1024 + op * 128:c * 1024 + (op + 1) * 128],
                        atT_pb[0:128, c * TW:(c + 1) * TW], start=(c == 0), stop=False))
                P.emit("tensor", lambda e, op=op, bank=bank: e.matmul(
                    bank[0:128, 0:512], brow[0:1, 3072 + op * 128:3072 + (op + 1) * 128],
                    C["ones_row_b512"][0:1, 0:TW], start=False, stop=True),
                    inc=(s_pe, 1))
                P.emit("vector", lambda e, op=op, bank=bank: e.tensor_add(
                    tmp_pb[0:128, op * TW:(op + 1) * TW], bank[0:128, 0:512],
                    xtile_pb[0:128, op * TW:(op + 1) * TW]),
                    waits=[(s_pe, s_pe.n)], inc=(s_ev, 1))
            P.emit("sync", lambda e, t=t: e.dma_start(
                io["enc2_scr"][0:1024, t * TW:(t + 1) * TW].rearrange(
                    "(c p) w -> p c w", p=128),
                tmp_pb[0:128, 0:8 * TW]),
                waits=[(s_ev, s_ev.n)], inc=(s_st, 16))
    P.barrier()

    with ExitStack() as SC:
        sbC = lambda n, s, d: U.scoped_sb(SC, n, s, d)
        W1 = sbC(f"{tag}_w1{li}", [128, 8 * 4096], BF16)
        xtile_pc = sbC(f"{tag}_xtC{li}", [128, 8 * TW], F32)
        xn_pc = sbC(f"{tag}_xnC{li}", [128, 8 * TW], BF16)
        xsq_pc = sbC(f"{tag}_xsqC{li}", [128, 8 * TW], BF16)
        tmp_pc = sbC(f"{tag}_tmpC{li}", [128, 8 * TW], F32)
        stat_pc = sbC(f"{tag}_statC{li}", [1, 2 * TW], F32)
        f1_pc = sbC(f"{tag}_f1{li}", [128, 32 * TW], BF16)
        U.ln_scr = {"xsq": xsq_pc, "stat": stat_pc, "tmp": tmp_pc}
        emit_cast_bf16(P, io["W1"][li * 1024:(li + 1) * 1024, 0:4096]
                       .rearrange("(c p) n -> p c n", p=128), W1[0:128, 0:32768], s_gp)
        GPC = s_gp.n
        P.wait("tensor", s_gp, GPC)
        for t in range(8):
            ln_sweep(io["enc2_scr"][0:1024, t * TW:(t + 1) * TW].rearrange(
                "(c p) w -> p c w", p=128), None, xtile_pc, xn_pc,
                (gcol[0:128, 16:24], gcol[0:128, 24:32]))
            P.wait("tensor", s_ev)
            P.wait("scalar", s_st)
            for hp in range(32):
                bank = U.pb[hp % 2]
                P.wait("tensor", s_sc, s_sc.n)
                for c in range(8):
                    P.emit("tensor", lambda e, hp=hp, c=c, bank=bank: e.matmul(
                        bank[0:128, 0:512],
                        W1[0:128, c * 4096 + hp * 128:c * 4096 + (hp + 1) * 128],
                        xn_pc[0:128, c * TW:(c + 1) * TW], start=(c == 0), stop=False))
                P.emit("tensor", lambda e, hp=hp, bank=bank: e.matmul(
                    bank[0:128, 0:512], brow[0:1, 4096 + hp * 128:4096 + (hp + 1) * 128],
                    C["ones_row_b512"][0:1, 0:TW], start=False, stop=True),
                    inc=(s_pe, 1))
                P.emit("scalar", lambda e, hp=hp, bank=bank: e.activation(
                    f1_pc[0:128, hp * TW:(hp + 1) * TW], bank[0:128, 0:512], AF.Relu),
                    waits=[(s_pe, s_pe.n)], inc=(s_sc, 1))
            P.emit("sync", lambda e, t=t: e.dma_start(
                io["h_scr"][0:4096, t * TW:(t + 1) * TW].rearrange(
                    "(c p) w -> p c w", p=128),
                f1_pc[0:128, 0:32 * TW]),
                waits=[(s_sc, s_sc.n)], inc=(s_st, 16))
    P.barrier()

    with ExitStack() as SD:
        sbD = lambda n, s, d: U.scoped_sb(SD, n, s, d)
        W2 = sbD(f"{tag}_w2{li}", [128, 32 * 1024], BF16)
        xtile_pd = sbD(f"{tag}_xtD{li}", [128, 8 * TW], F32)
        tmp_pd = sbD(f"{tag}_tmpD{li}", [128, 8 * TW], F32)
        f1_pd = sbD(f"{tag}_f1D{li}", [128, 32 * TW], BF16)
        emit_cast_bf16(P, io["W2"][li * 4096:(li + 1) * 4096, 0:1024]
                       .rearrange("(c p) n -> p c n", p=128), W2[0:128, 0:32768], s_gp)
        GPD = s_gp.n
        P.wait("tensor", s_gp, GPD)
        for t in range(8):
            P.emit("sync", lambda e, t=t: e.dma_start(
                f1_pd[0:128, 0:32 * TW],
                io["h_scr"][0:4096, t * TW:(t + 1) * TW].rearrange(
                    "(c p) w -> p c w", p=128)),
                waits=[(s_ev, s_ev.n)], inc=(s_x, 16))
            P.emit("sync", lambda e, t=t: e.dma_start(
                xtile_pd[0:128, 0:8 * TW],
                io["enc2_scr"][0:1024, t * TW:(t + 1) * TW].rearrange(
                    "(c p) w -> p c w", p=128)), inc=(s_x, 16))
            P.wait("tensor", s_x)
            P.wait("vector", s_x)
            P.wait("vector", s_st)
            for op in range(8):
                bank = U.pb[2 + op % 2]
                P.wait("tensor", s_ev, s_ev.n)
                for cc in range(32):
                    P.emit("tensor", lambda e, op=op, cc=cc, bank=bank: e.matmul(
                        bank[0:128, 0:512],
                        W2[0:128, cc * 1024 + op * 128:cc * 1024 + (op + 1) * 128],
                        f1_pd[0:128, cc * TW:(cc + 1) * TW], start=(cc == 0), stop=False))
                P.emit("tensor", lambda e, op=op, bank=bank: e.matmul(
                    bank[0:128, 0:512], brow[0:1, 8192 + op * 128:8192 + (op + 1) * 128],
                    C["ones_row_b512"][0:1, 0:TW], start=False, stop=True),
                    inc=(s_pe, 1))
                P.emit("vector", lambda e, op=op, bank=bank: e.tensor_add(
                    tmp_pd[0:128, op * TW:(op + 1) * TW], bank[0:128, 0:512],
                    xtile_pd[0:128, op * TW:(op + 1) * TW]),
                    waits=[(s_pe, s_pe.n)], inc=(s_ev, 1))
            P.emit("sync", lambda e, t=t: e.dma_start(
                io["enc_scr"][0:1024, t * TW:(t + 1) * TW].rearrange(
                    "(c p) w -> p c w", p=128),
                tmp_pd[0:128, 0:8 * TW]),
                waits=[(s_ev, s_ev.n)], inc=(s_st, 16))
    P.barrier()



def build_L0_attn_b(P, U, io, tag):
    nc, C = P.nc, U.C
    H = host_constants()
    TW = 512
    s_gp = P.sem(f"{tag}_gp")
    s_x = P.sem(f"{tag}_x")
    s_pe = P.sem(f"{tag}_pe")
    s_ev = P.sem(f"{tag}_ev")
    s_sc = P.sem(f"{tag}_sc")
    s_st = P.sem(f"{tag}_st")
    sems = {"pe": s_pe, "ev": s_ev, "sc": s_sc}

    def ln_sweep(src_hbm, dst_buf, xtile, xn, aff, extra=None, eps_col=0,
                 out_fn=None):
        P.emit("sync", lambda e: e.dma_start(
            xtile[0:128, 0:8 * TW], src_hbm),
            waits=[(s_ev, s_ev.n)], inc=(s_x, 16))
        P.wait("scalar", s_x)
        P.wait("vector", s_x)
        P.wait("vector", s_st)
        emit_ln_tile(P, U, xtile, xn, TW, sems, affine=aff, extra_f32=extra,
                     eps_col=eps_col, out_fn=out_fn)

    def elu1_evac(bank, out_ap, tmp1, tmp2):
        P.emit("vector", lambda e: e.tensor_scalar(
            tmp2[0:128, 0:TW], bank[0:128, 0:512], 0.0, None, ALU.max),
            waits=[(s_pe, s_pe.n), (s_ev, s_ev.n)], inc=(s_ev, 1))
        P.emit("scalar", lambda e: e.activation(
            tmp1[0:128, 0:TW], bank[0:128, 0:512], AF.Relu, scale=-1.0),
            waits=[(s_pe, s_pe.n), (s_sc, s_sc.n), (s_ev, s_ev.n)], inc=(s_sc, 1))
        P.emit("scalar", lambda e: e.activation(
            tmp1[0:128, 0:TW], tmp1[0:128, 0:TW], AF.Exp, scale=-1.0),
            waits=[(s_sc, s_sc.n)], inc=(s_sc, 1))
        P.emit("vector", lambda e: e.tensor_add(
            out_ap, tmp1[0:128, 0:TW], tmp2[0:128, 0:TW]),
            waits=[(s_sc, s_sc.n), (s_ev, s_ev.n)], inc=(s_ev, 1))

    with ExitStack() as ST0:
        sb0 = lambda n, s, d: U.scoped_sb(ST0, n, s, d)
        kvs = sb0(f"{tag}_kvs", [128, 8 * 65], BF16)
        gcol = sb0(f"{tag}_gcol", [128, 6 * 8], F32)
        brow = sb0(f"{tag}_brow", [1, 4 * 1024 + 4096 + 1024], BF16)

        with ExitStack() as S0:
            sbA = lambda n, s, d: U.scoped_sb(S0, n, s, d)
            xtile_p0 = sbA(f"{tag}_xt0", [128, 8 * TW], F32)
            xn_p0 = sbA(f"{tag}_xn0", [128, 8 * TW], BF16)
            xin_p0 = sbA(f"{tag}_xin0", [128, 8 * TW], F32)
            pe_t_p0 = sbA(f"{tag}_pe0", [128, 8 * TW], F32)
            xsq_p0 = sbA(f"{tag}_xsq0", [128, 8 * TW], BF16)
            tmp_p0 = sbA(f"{tag}_tmp0", [128, 8 * TW], F32)
            stat_p0 = sbA(f"{tag}_stat0", [1, 2 * TW], F32)
            U.ln_scr = {"xsq": xsq_p0, "stat": stat_p0, "tmp": tmp_p0}
            for t in range(8):
                P.emit("sync", lambda e, t=t: e.dma_start(
                    pe_t_p0[0:128, 0:8 * TW],
                    io["posemb"][0:1024, t * TW:(t + 1) * TW].rearrange(
                        "(c p) w -> p c w", p=128)),
                    waits=[(s_ev, s_ev.n)], inc=(s_x, 16))
                ln_sweep(io["x_in"][0:1024, t * TW:(t + 1) * TW].rearrange(
                    "(c p) w -> p c w", p=128), None, xtile_p0, xn_p0, None, extra=xin_p0)
                P.wait("vector", s_x)
                for c in range(8):
                    P.emit("vector", lambda e, c=c: e.scalar_tensor_tensor(
                        tmp_p0[0:128, c * TW:(c + 1) * TW],
                        xin_p0[0:128, c * TW:(c + 1) * TW], 32.0,
                        pe_t_p0[0:128, c * TW:(c + 1) * TW], ALU.mult, ALU.add),
                        waits=[(s_ev, s_ev.n)], inc=(s_ev, 1))
                P.emit("sync", lambda e, t=t: e.dma_start(
                    io["enc_scr"][0:1024, t * TW:(t + 1) * TW].rearrange(
                        "(c p) w -> p c w", p=128),
                    tmp_p0[0:128, 0:8 * TW]),
                    waits=[(s_ev, s_ev.n)], inc=(s_st, 16))
        P.barrier()

        for li in range(4):
            _enc_layer(P, U, io, tag, li, kvs, gcol, brow, ln_sweep,
                       elu1_evac, sems, s_gp, s_x, s_pe, s_ev, s_sc, s_st)
        with ExitStack() as SF:
            sbF = lambda n, s, d: U.scoped_sb(SF, n, s, d)
            xtile_pf = sbF(f"{tag}_xtF", [128, 8 * TW], F32)
            x0_pf = sbF(f"{tag}_x0F", [128, 8 * TW], F32)
            xn_pf = sbF(f"{tag}_xnF", [128, 8 * TW], BF16)
            xin_pf = sbF(f"{tag}_xinF", [128, 8 * TW], F32)
            xsq_pf = sbF(f"{tag}_xsqF", [128, 8 * TW], BF16)
            tmp_pf = sbF(f"{tag}_tmpF", [128, 8 * TW], F32)
            stat_pf = sbF(f"{tag}_statF", [1, 2 * TW], F32)
            U.ln_scr = {"xsq": xsq_pf, "stat": stat_pf, "tmp": tmp_pf}
            emit_load_bias_cols(P, io["lnf_s"], 1024, gcol, 32, s_gp)
            emit_load_bias_cols(P, io["lnf_b"], 1024, gcol, 40, s_gp)
            P.wait("vector", s_gp)
            for t in range(8):
                P.emit("sync", lambda e, t=t: e.dma_start(
                    x0_pf[0:128, 0:8 * TW],
                    io["x_in"][0:1024, t * TW:(t + 1) * TW].rearrange(
                        "(c p) w -> p c w", p=128)),
                    waits=[(s_ev, s_ev.n)], inc=(s_x, 16))
                ln_sweep(io["enc_scr"][0:1024, t * TW:(t + 1) * TW].rearrange(
                    "(c p) w -> p c w", p=128), None, xtile_pf, xn_pf,
                    (gcol[0:128, 32:40], gcol[0:128, 40:48]),
                    out_fn=lambda c: xin_pf[0:128, c * TW:(c + 1) * TW])
                P.wait("vector", s_x)
                for c in range(8):
                    P.emit("vector", lambda e, c=c: e.tensor_add(
                        xin_pf[0:128, c * TW:(c + 1) * TW],
                        xin_pf[0:128, c * TW:(c + 1) * TW],
                        x0_pf[0:128, c * TW:(c + 1) * TW]),
                        waits=[(s_ev, s_ev.n)], inc=(s_ev, 1))
                P.emit("sync", lambda e, t=t: e.dma_start(
                    io["x_out"][0:1024, t * TW:(t + 1) * TW].rearrange(
                        "(c p) w -> p c w", p=128),
                    xin_pf[0:128, 0:8 * TW]),
                    waits=[(s_ev, s_ev.n)], inc=(s_st, 16))
            P.wait("sync", s_st)
    P.barrier()

def build_L1_attn_b(P, U, io, tag):
    nc, C = P.nc, U.C
    H = host_constants()
    SW = 256
    NSLAB = T // SW
    NSLAB = min(NSLAB, globals().get("GLA_LIMIT", NSLAB))
    PH = globals().get("GLA_PHASE", 8)
    CH = 64
    s_gp = P.sem(f"{tag}_gp")
    s_tr = P.sem(f"{tag}_tr")
    s_x = P.sem(f"{tag}_x")
    s_pe = P.sem(f"{tag}_pe")
    s_ev = P.sem(f"{tag}_ev")
    s_sc = P.sem(f"{tag}_sc")
    s_st = P.sem(f"{tag}_st")
    sems = {"pe": s_pe, "ev": s_ev, "sc": s_sc}

    def VS(fn, w=()):
        P.emit("vector", fn, waits=[(s_ev, s_ev.n)] + list(w), inc=(s_ev, 1))

    def SC_(fn, w=()):
        P.emit("scalar", fn, waits=[(s_sc, s_sc.n)] + list(w), inc=(s_sc, 1))

    with ExitStack() as ST0:
        sb0 = lambda n, s, d: U.scoped_sb(ST0, n, s, d)
        upT = sb0(f"{tag}_upT", [128, 8 * 2048], BF16)
        gaT = sb0(f"{tag}_gaT", [128, 8 * 2048], BF16)
        dnT = sb0(f"{tag}_dnT", [128, 16 * 1024], BF16)
        qkT = sb0(f"{tag}_qkT", [128, 16 * 256], BF16)
        ig1T = sb0(f"{tag}_ig1T", [128, 16 * 16], BF16)
        ig2T = sb0(f"{tag}_ig2T", [16, 2048], BF16)
        bcol = sb0(f"{tag}_bcol", [128, 16 * 4], F32)
        ccol = sb0(f"{tag}_ccol", [128, 64], F32)
        gbr = sb0(f"{tag}_gbr", [1, 128], BF16)
        Sf = sb0(f"{tag}_Sf", [64, 2048], F32)
        Sb = sb0(f"{tag}_Sb", [64, 2048], BF16)
        emit_cast_bf16(P, io["up_w"], io["upw_bf"], s_gp)
        emit_cast_bf16(P, io["gate_w"], io["gatew_bf"], s_gp)
        emit_cast_bf16(P, io["down_w"], io["downw_bf"], s_gp)
        emit_cast_bf16(P, io["qkgt_w"], io["qkgt_bf"], s_gp)
        emit_cast_bf16(P, io["ig_w1"], io["ig1_bf"], s_gp)
        GP0 = s_gp.n
        emit_transpose_load(P, io["upw_bf"], 2048, 1024, upT, 2048, s_tr,
                            waits=[(s_gp, GP0)])
        emit_transpose_load(P, io["gatew_bf"], 2048, 1024, gaT, 2048, s_tr)
        emit_transpose_load(P, io["downw_bf"], 1024, 2048, dnT, 1024, s_tr)
        for c in range(16):
            P.emit("sync", lambda e, c=c: e.dma_start(
                qkT[0:128, c * 256:(c + 1) * 256],
                io["qkgt_bf"][0:256, c * 128:(c + 1) * 128], transpose=True),
                inc=(s_tr, 16))
        for c in range(16):
            P.emit("sync", lambda e, c=c: e.dma_start(
                ig1T[0:128, c * 16:(c + 1) * 16],
                io["ig1_bf"][0:16, c * 128:(c + 1) * 128], transpose=True),
                inc=(s_tr, 16))
        TR0 = s_tr.n
        ig2tmp = sb0(f"{tag}_ig2tmp", [128, 16 * 16], BF16)
        P.emit("gpsimd", lambda e: e.dma_start(
            ig2tmp[0:128, 0:256],
            io["ig_w2"].rearrange("(c p) n -> p c n", p=128)), inc=(s_gp, 16))
        P.wait("tensor", s_gp)
        for c in range(16):
            P.emit("tensor", lambda e, c=c: e.transpose(
                U.pb_bf[0:16, 0:128], ig2tmp[0:128, c * 16:(c + 1) * 16],
                C["ident128b"][0:128, 0:128]), inc=(s_pe, 1))
            P.emit("vector", lambda e, c=c: e.tensor_copy(
                ig2T[0:16, c * 128:(c + 1) * 128], U.pb_bf[0:16, 0:128]),
                waits=[(s_pe, s_pe.n)], inc=(s_ev, 1))
            P.wait("tensor", s_ev)
        emit_load_bias_cols(P, io["conv_b"], 2048, bcol, 0, s_gp)
        emit_load_bias_cols(P, io["ig_b2"], 2048, bcol, 16, s_gp)
        emit_load_bias_cols(P, io["residual_weight"], 2048, bcol, 32, s_gp)
        emit_load_bias_cols(P, io["norm_weight"], 2048, bcol, 48, s_gp)
        P.emit("gpsimd", lambda e: e.dma_start(gbr[0:1, 0:64], io["g_b"]), inc=(s_gp, 16))
        P.emit("gpsimd", lambda e: e.dma_start(gbr[0:1, 64:128], io["tau_b"]), inc=(s_gp, 16))
        P.emit("gpsimd", lambda e: e.dma_start(
            ccol[0:128, 0:64], io["conv_w"].rearrange("(c p) j -> p c j", p=128),
            allow_slow_non_contiguous=True), inc=(s_gp, 16))
        GP1 = s_gp.n

        with ExitStack() as SL:
            sbL = lambda n, s, d: U.scoped_sb(SL, n, s, d)
            xtile = sbL(f"{tag}_xt", [128, 8 * SW], F32)
            hsn = sbL(f"{tag}_hsn", [128, 8 * SW], BF16)
            xsq = sbL(f"{tag}_xsq", [128, 8 * SW], BF16)
            tmp = sbL(f"{tag}_tmp", [128, 8 * SW], F32)
            stat = sbL(f"{tag}_stat", [1, 2 * SW], F32)
            xup = sbL(f"{tag}_xup", [128, 16 * (SW + 4)], BF16)
            shf = sbL(f"{tag}_shf", [128, 16 * SW], BF16)
            sfg = sbL(f"{tag}_sfg", [128, 16 * SW], BF16)
            vb = sbL(f"{tag}_vb", [128, 16 * SW], BF16)
            ofb = sbL(f"{tag}_ofb", [128, 16 * SW], BF16)
            onb = vb
            qrow = sbL(f"{tag}_qrow", [64, 9 * SW], F32)
            crow = sbL(f"{tag}_crow", [64, 32], F32)
            qtb = sbL(f"{tag}_qtb", [64, 3 * CH], BF16)
            atb = sbL(f"{tag}_atb", [64, CH], BF16)
            kdb = sbL(f"{tag}_kdb", [64, CH], BF16)
            vtokb = sbL(f"{tag}_vtok", [64, 2048], BF16)
            otok = hsn
            t16a = sbL(f"{tag}_t16a", [16, SW], BF16)
            zer = sbL(f"{tag}_zer", [64, CH], F32)
            trik = sbL(f"{tag}_trik", [64, CH], F32)
            U.ln_scr = {"xsq": xsq, "stat": stat, "tmp": tmp}
            P.emit("vector", lambda e: e.memset(Sf[0:64, 0:2048], 0.0), inc=(s_ev, 1))
            P.emit("vector", lambda e: e.memset(Sb[0:64, 0:2048], 0.0), inc=(s_ev, 1))
            P.emit("vector", lambda e: e.memset(zer[0:64, 0:CH], 0.0), inc=(s_ev, 1))
            for c in range(16):
                P.emit("vector", lambda e, c=c: e.memset(
                    xup[0:128, c * (SW + 4):c * (SW + 4) + 4], 0.0),
                    inc=(s_ev, 1) if c == 15 else None)
            trid = nc.inline_tensor(H["tri_keepT"], "c_" + tag + "_trik").ap()
            P.emit("sync", lambda e: e.dma_start(trik[0:64, 0:CH], trid), inc=(s_x, 16))
            P.wait("tensor", s_tr, TR0)
            P.wait("tensor", s_gp, GP1)
            P.wait("vector", s_gp, GP1)
            P.wait("scalar", s_gp, GP1)
            P.wait("vector", s_x)

            for sl_i in range(NSLAB):
                t0c = sl_i * SW
                P.emit("sync", lambda e, t0c=t0c: e.dma_start(
                    xtile[0:128, 0:8 * SW],
                    io["x_in"][0:1024, t0c:t0c + SW].rearrange(
                        "(c p) w -> p c w", p=128)),
                    waits=[(s_ev, s_ev.n), (s_st, s_st.n)], inc=(s_x, 16))
                P.wait("scalar", s_x)
                P.wait("vector", s_x)
                P.wait("vector", s_st)
                emit_ln_tile(P, U, xtile, hsn, SW, sems)
                if PH < 2:
                    continue
                P.wait("tensor", s_ev)
                for op in range(16):
                    bank = U.pb[op % 2]
                    P.wait("tensor", s_sc, s_sc.n)
                    P.wait("tensor", s_ev, s_ev.n)
                    for cc in range(8):
                        P.emit("tensor", lambda e, op=op, cc=cc, bank=bank: e.matmul(
                            bank[0:128, 0:SW],
                            upT[0:128, cc * 2048 + op * 128:cc * 2048 + (op + 1) * 128],
                            hsn[0:128, cc * SW:(cc + 1) * SW], start=(cc == 0), stop=(cc == 7)),
                            inc=(s_pe, 1) if cc == 7 else None)
                    P.emit("scalar", lambda e, op=op, bank=bank: e.copy(
                        xup[0:128, op * (SW + 4) + 4:op * (SW + 4) + 4 + SW],
                        bank[0:128, 0:SW]),
                        waits=[(s_pe, s_pe.n), (s_sc, s_sc.n)], inc=(s_sc, 1))
                    bank2 = U.pb[2 + op % 2]
                    for cc in range(8):
                        P.emit("tensor", lambda e, op=op, cc=cc, bank2=bank2: e.matmul(
                            bank2[0:128, 0:SW],
                            gaT[0:128, cc * 2048 + op * 128:cc * 2048 + (op + 1) * 128],
                            hsn[0:128, cc * SW:(cc + 1) * SW], start=(cc == 0), stop=(cc == 7)),
                            inc=(s_pe, 1) if cc == 7 else None)
                    P.emit("scalar", lambda e, op=op, bank2=bank2: e.activation(
                        sfg[0:128, op * SW:(op + 1) * SW], bank2[0:128, 0:SW], AF.Silu),
                        waits=[(s_pe, s_pe.n), (s_sc, s_sc.n)], inc=(s_sc, 1))
                if PH < 3:
                    continue
                P.wait("vector", s_sc)
                for cc in range(16):
                    base = cc * (SW + 4)
                    VS(lambda e, cc=cc, base=base: e.tensor_scalar(
                        tmp[0:128, 0:SW], xup[0:128, base + 1:base + 1 + SW],
                        ccol[0:128, cc * 4:cc * 4 + 1], None, ALU.mult),
                       w=[(s_sc, s_sc.n)])
                    for j in range(1, 4):
                        VS(lambda e, cc=cc, base=base, j=j: e.scalar_tensor_tensor(
                            tmp[0:128, 0:SW], xup[0:128, base + 1 + j:base + 1 + j + SW],
                            ccol[0:128, cc * 4 + j:cc * 4 + j + 1],
                            tmp[0:128, 0:SW], ALU.mult, ALU.add))
                    VS(lambda e, cc=cc: e.tensor_scalar(
                        tmp[0:128, 0:SW], tmp[0:128, 0:SW],
                        bcol[0:128, cc:cc + 1], None, ALU.add))
                    SC_(lambda e, cc=cc: e.activation(
                        shf[0:128, cc * SW:(cc + 1) * SW], tmp[0:128, 0:SW],
                        AF.Silu), w=[(s_ev, s_ev.n)])
                if PH < 4:
                    continue
                P.wait("tensor", s_sc)
                for cc in range(16):
                    P.emit("tensor", lambda e, cc=cc: e.matmul(
                        U.pb[4][0:16, 0:SW], ig1T[0:128, cc * 16:(cc + 1) * 16],
                        xup[0:128, cc * (SW + 4) + 4:cc * (SW + 4) + 4 + SW],
                        start=(cc == 0), stop=(cc == 15)),
                        inc=(s_pe, 1) if cc == 15 else None)
                P.emit("vector", lambda e: e.tensor_copy(
                    t16a[0:16, 0:SW], U.pb[4][0:16, 0:SW]),
                    waits=[(s_pe, s_pe.n)], inc=(s_ev, 1))
                P.wait("tensor", s_ev)
                for cc in range(16):
                    bank = U.pb[cc % 2]
                    P.wait("tensor", s_ev, s_ev.n)
                    P.emit("tensor", lambda e, cc=cc, bank=bank: e.matmul(
                        bank[0:128, 0:SW], ig2T[0:16, cc * 128:(cc + 1) * 128],
                        t16a[0:16, 0:SW], start=True, stop=True),
                        inc=(s_pe, 1))
                    SC_(lambda e, cc=cc, bank=bank: e.activation(
                        tmp[0:128, 0:SW], bank[0:128, 0:SW], AF.Sigmoid,
                        bias=bcol[0:128, 16 + cc:17 + cc]), w=[(s_pe, s_pe.n), (s_ev, s_ev.n)])
                    VS(lambda e, cc=cc: e.tensor_mul(
                        vb[0:128, cc * SW:(cc + 1) * SW], tmp[0:128, 0:SW],
                        xup[0:128, cc * (SW + 4) + 4:cc * (SW + 4) + 4 + SW]),
                        w=[(s_sc, s_sc.n)])
                for cc in range(16):
                    VS(lambda e, cc=cc: e.tensor_copy(
                        xup[0:128, cc * (SW + 4) + 1:cc * (SW + 4) + 4],
                        xup[0:128, cc * (SW + 4) + 4 + SW - 3:cc * (SW + 4) + 4 + SW]))
                if PH < 5:
                    continue
                P.wait("tensor", s_ev)
                for w_i, bias_mm, evac in (
                    (0, None, "q"), (1, None, "k"), (2, 0, "g"), (3, 64, "tau")
                )[:globals().get("GLA_NGRP", 4)]:
                    if globals().get("GLA_NOBIAS", 0):
                        bias_mm = None
                    bank = U.pb[4]
                    for cc in range(16):
                        last = (cc == 15 and bias_mm is None)
                        P.emit("tensor", lambda e, w_i=w_i, cc=cc, bank=bank, last=last: e.matmul(
                            bank[0:64, 0:SW],
                            qkT[0:128, cc * 256 + w_i * 64:cc * 256 + (w_i + 1) * 64],
                            shf[0:128, cc * SW:(cc + 1) * SW],
                            start=(cc == 0), stop=last),
                            waits=[(s_ev, s_ev.n), (s_sc, s_sc.n)] if cc == 0 else (),
                            inc=(s_pe, 1) if last else None)
                    if bias_mm is not None:
                        P.emit("tensor", lambda e, bias_mm=bias_mm, bank=bank: e.matmul(
                            bank[0:64, 0:SW], gbr[0:1, bias_mm:bias_mm + 64],
                            C["ones_row_b512"][0:1, 0:SW], start=False, stop=True),
                            inc=(s_pe, 1))
                    if evac == "q":
                        VS(lambda e, bank=bank: e.tensor_scalar(
                            qrow[0:64, 0:SW], bank[0:64, 0:SW], 0.125, None, ALU.mult),
                            w=[(s_pe, s_pe.n)])
                    elif evac == "k":
                        VS(lambda e, bank=bank: e.tensor_copy(
                            qrow[0:64, SW:2 * SW], bank[0:64, 0:SW]),
                            w=[(s_pe, s_pe.n)])
                    elif evac == "g":
                        VS(lambda e, bank=bank: e.tensor_scalar(
                            qrow[0:64, 6 * SW:7 * SW], bank[0:64, 0:SW], 0.0, None,
                            ALU.max), w=[(s_pe, s_pe.n)])
                        SC_(lambda e, bank=bank: e.activation(
                            qrow[0:64, 7 * SW:8 * SW], bank[0:64, 0:SW], AF.Abs),
                            w=[(s_pe, s_pe.n), (s_ev, s_ev.n)])
                        SC_(lambda e: e.activation(
                            qrow[0:64, 7 * SW:8 * SW], qrow[0:64, 7 * SW:8 * SW],
                            AF.Exp, scale=-1.0))
                        SC_(lambda e: e.activation(
                            qrow[0:64, 7 * SW:8 * SW], qrow[0:64, 7 * SW:8 * SW],
                            AF.Ln, bias=C["epsb"][0:64, 4:5]))
                        VS(lambda e: e.tensor_add(
                            qrow[0:64, 2 * SW:3 * SW], qrow[0:64, 6 * SW:7 * SW],
                            qrow[0:64, 7 * SW:8 * SW]), w=[(s_sc, s_sc.n)])
                    else:
                        SC_(lambda e, bank=bank: e.activation(
                            qrow[0:64, 3 * SW:4 * SW], bank[0:64, 0:SW], AF.Sigmoid),
                            w=[(s_pe, s_pe.n)])
                if PH < 5.3:
                    continue
                P.wait("scalar", s_ev)
                SC_(lambda e: e.activation(
                    qrow[0:64, 7 * SW:8 * SW], qrow[0:64, 2 * SW:3 * SW], AF.Ln))
                VS(lambda e: e.tensor_mul(
                    qrow[0:64, 7 * SW:8 * SW], qrow[0:64, 7 * SW:8 * SW],
                    qrow[0:64, 3 * SW:4 * SW]), w=[(s_sc, s_sc.n)])
                SC_(lambda e: e.activation(
                    qrow[0:64, 4 * SW:5 * SW], qrow[0:64, 7 * SW:8 * SW], AF.Exp),
                    w=[(s_ev, s_ev.n)])
                VS(lambda e: e.scalar_tensor_tensor(
                    qrow[0:64, 5 * SW:6 * SW], qrow[0:64, 2 * SW:3 * SW], -1.0,
                    qrow[0:64, 3 * SW:4 * SW], ALU.mult, ALU.mult))
                if PH < 5.6:
                    continue
                SC_(lambda e: e.activation(
                    qrow[0:64, 6 * SW:7 * SW], qrow[0:64, SW:2 * SW], AF.Square),
                    w=[(s_ev, s_ev.n)])
                P.wait("tensor", s_sc)
                P.emit("tensor", lambda e: e.matmul(
                    U.pb[4][0:1, 0:SW], C["ones_col_f"][0:64, 0:1],
                    qrow[0:64, 6 * SW:7 * SW], start=True, stop=True), inc=(s_pe, 1))
                SC_(lambda e: e.activation(
                    stat[0:1, 0:SW], U.pb[4][0:1, 0:SW], AF.Sqrt),
                    w=[(s_pe, s_pe.n)])
                VS(lambda e: e.tensor_scalar(
                    stat[0:1, 0:SW], stat[0:1, 0:SW], 1e-12, None, ALU.max),
                    w=[(s_sc, s_sc.n)])
                VS(lambda e: e.reciprocal(stat[0:1, SW:2 * SW], stat[0:1, 0:SW]))
                P.wait("tensor", s_ev)
                P.emit("tensor", lambda e: e.matmul(
                    U.pb[4][0:64, 0:SW], C["ones_row_f"][0:1, 0:64],
                    stat[0:1, SW:2 * SW], start=True, stop=True), inc=(s_pe, 1))
                VS(lambda e: e.tensor_mul(
                    qrow[0:64, SW:2 * SW], qrow[0:64, SW:2 * SW], U.pb[4][0:64, 0:SW]),
                    w=[(s_pe, s_pe.n)])
                VS(lambda e: e.tensor_mul(
                    qrow[0:64, SW:2 * SW], qrow[0:64, SW:2 * SW],
                    qrow[0:64, 4 * SW:5 * SW]))
                if PH < 6:
                    continue
                for cp in range(SW // (2 * CH)):
                    for sub in range(2):
                        ch0 = cp * 2 * CH + sub * CH
                        csl = slice(ch0, ch0 + CH)
                        VS(lambda e, csl=csl: e.tensor_tensor_scan(
                            qrow[0:64, 8 * SW + csl.start:8 * SW + csl.stop],
                            qrow[0:64, 5 * SW + csl.start:5 * SW + csl.stop],
                            qrow[0:64, 5 * SW + csl.start:5 * SW + csl.stop],
                            0.0, ALU.add, ALU.bypass))
                        bsl = slice(8 * SW + csl.start, 8 * SW + csl.stop)
                        VS(lambda e, bsl=bsl: e.tensor_copy(
                            crow[0:64, 0:1], qrow[0:64, bsl.start + 31:bsl.start + 32]))
                        VS(lambda e, bsl=bsl: e.tensor_copy(
                            crow[0:64, 1:2], qrow[0:64, bsl.start + 63:bsl.start + 64]))
                        SC_(lambda e: e.activation(
                            crow[0:64, 2:3], crow[0:64, 1:2], AF.Exp),
                            w=[(s_ev, s_ev.n)])
                        VS(lambda e: e.tensor_sub(
                            crow[0:64, 3:4], crow[0:64, 1:2], crow[0:64, 0:1]))
                        SC_(lambda e: e.activation(
                            crow[0:64, 3:4], crow[0:64, 3:4], AF.Exp),
                            w=[(s_ev, s_ev.n)])
                        VS(lambda e, bsl=bsl: e.tensor_scalar(
                            tmp[0:64, 0:CH], qrow[0:64, bsl], crow[0:64, 0:1],
                            None, ALU.subtract))
                        SC_(lambda e: e.activation(
                            tmp[0:64, CH:2 * CH], tmp[0:64, 0:CH], AF.Exp),
                            w=[(s_ev, s_ev.n)])
                        SC_(lambda e: e.activation(
                            tmp[0:64, 2 * CH:3 * CH], tmp[0:64, 0:CH], AF.Exp,
                            scale=-1.0))
                        SC_(lambda e, bsl=bsl: e.activation(
                            tmp[0:64, 3 * CH:4 * CH], qrow[0:64, bsl], AF.Exp))
                        VS(lambda e, csl=csl: e.tensor_mul(
                            qtb[0:64, 0:CH], qrow[0:64, csl], tmp[0:64, CH:2 * CH]),
                            w=[(s_sc, s_sc.n)])
                        VS(lambda e, csl=csl: e.tensor_mul(
                            qtb[0:64, CH:2 * CH],
                            qrow[0:64, SW + csl.start:SW + csl.stop],
                            tmp[0:64, 2 * CH:3 * CH]))
                        VS(lambda e, csl=csl: e.tensor_mul(
                            qtb[0:64, 2 * CH:3 * CH], qrow[0:64, csl],
                            tmp[0:64, 3 * CH:4 * CH]))
                        VS(lambda e: e.tensor_scalar(
                            tmp[0:64, 0:CH], qtb[0:64, CH:2 * CH], crow[0:64, 3:4],
                            None, ALU.mult))
                        P.wait("tensor", s_ev)
                        P.emit("scalar", lambda e: e.copy(
                            atb[0:64, 0:CH], tmp[0:64, 0:CH]),
                            waits=[(s_sc, s_sc.n), (s_ev, s_ev.n)], inc=(s_sc, 1))
                        P.wait("tensor", s_sc)
                        P.emit("tensor", lambda e: e.transpose(
                            U.pb_bf[0:CH, 0:CH], atb[0:64, 0:CH],
                            C["ident128b"][0:64, 0:64]), inc=(s_pe, 1))
                        P.emit("vector", lambda e: e.tensor_copy(
                            kdb[0:64, 0:CH], U.pb_bf[0:CH, 0:CH]),
                            waits=[(s_pe, s_pe.n)], inc=(s_ev, 1))
                        P.wait("tensor", s_ev)
                        P.emit("tensor", lambda e: e.matmul(
                            U.pb[4][0:64, 0:CH], qtb[0:64, CH:2 * CH],
                            qtb[0:64, 0:CH], start=True, stop=True), inc=(s_pe, 1))
                        P.emit("vector", lambda e: e.tensor_mul(
                            atb[0:64, 0:CH], U.pb[4][0:64, 0:CH],
                            trik[0:64, 0:CH]),
                            waits=[(s_pe, s_pe.n), (s_ev, s_ev.n)], inc=(s_ev, 1))
                        P.wait("tensor", s_ev)
                        for c2 in range(0, 16, 8):
                            for c3 in range(8):
                                cc = c2 + c3
                                P.emit("tensor", lambda e, cc=cc, c3=c3, ch0=ch0: e.transpose(
                                    U.pb_bf[0:64, c3 * 128:(c3 + 1) * 128],
                                    vb[0:128, cc * SW + ch0:cc * SW + ch0 + CH],
                                    C["ident128b"][0:128, 0:128]),
                                    inc=(s_pe, 1) if c3 == 7 else None)
                            P.emit("vector", lambda e, c2=c2: e.tensor_copy(
                                vtokb[0:64, c2 * 128:(c2 + 8) * 128],
                                U.pb_bf[0:64, 0:1024]),
                                waits=[(s_pe, s_pe.n)], inc=(s_ev, 1))
                            P.wait("tensor", s_ev)
                        for vq in range(4):
                            bank = U.pb[vq % 2]
                            P.wait("tensor", s_ev, s_ev.n)
                            P.emit("tensor", lambda e, vq=vq, bank=bank: e.matmul(
                                bank[0:64, 0:512], qtb[0:64, 2 * CH:3 * CH],
                                Sb[0:64, vq * 512:(vq + 1) * 512], start=True, stop=False))
                            P.emit("tensor", lambda e, vq=vq, bank=bank: e.matmul(
                                bank[0:64, 0:512], atb[0:64, 0:CH],
                                vtokb[0:64, vq * 512:(vq + 1) * 512], start=False,
                                stop=True), inc=(s_pe, 1))
                            P.emit("vector", lambda e, vq=vq, sub=sub, bank=bank: e.tensor_copy(
                                otok[sub * 64:sub * 64 + 64, vq * 512:(vq + 1) * 512],
                                bank[0:64, 0:512]),
                                waits=[(s_pe, s_pe.n)], inc=(s_ev, 1))
                        for vq in range(4):
                            bank = U.pb[2 + vq % 2]
                            P.wait("tensor", s_ev, s_ev.n)
                            P.emit("tensor", lambda e, vq=vq, bank=bank: e.matmul(
                                bank[0:64, 0:512], kdb[0:64, 0:CH],
                                vtokb[0:64, vq * 512:(vq + 1) * 512],
                                start=True, stop=True), inc=(s_pe, 1))
                            VS(lambda e, vq=vq, bank=bank: e.scalar_tensor_tensor(
                                Sf[0:64, vq * 512:(vq + 1) * 512],
                                Sf[0:64, vq * 512:(vq + 1) * 512], crow[0:64, 2:3],
                                bank[0:64, 0:512], ALU.mult, ALU.add),
                                w=[(s_pe, s_pe.n)])
                        VS(lambda e: e.tensor_copy(Sb[0:64, 0:2048], Sf[0:64, 0:2048]))
                    P.wait("tensor", s_ev)
                    for cc in range(16):
                        P.emit("tensor", lambda e, cc=cc: e.transpose(
                            U.pb_bf[0:128, 0:128], otok[0:128, cc * 128:(cc + 1) * 128],
                            C["ident128b"][0:128, 0:128]), inc=(s_pe, 1))
                        P.emit("vector", lambda e, cc=cc, cp=cp: e.tensor_copy(
                            ofb[0:128, cc * SW + cp * 2 * CH:cc * SW + (cp + 1) * 2 * CH],
                            U.pb_bf[0:128, 0:128]),
                            waits=[(s_pe, s_pe.n)], inc=(s_ev, 1))
                        P.wait("tensor", s_ev)
                if PH < 7:
                    continue
                for cc in range(16):
                    VS(lambda e, cc=cc: e.tensor_scalar(
                        tmp[0:128, 0:SW], shf[0:128, cc * SW:(cc + 1) * SW],
                        bcol[0:128, 32 + cc:33 + cc], None, ALU.mult))
                    VS(lambda e, cc=cc: e.tensor_add(
                        tmp[0:128, 0:SW], tmp[0:128, 0:SW],
                        ofb[0:128, cc * SW:(cc + 1) * SW]))
                    VS(lambda e, cc=cc: e.tensor_mul(
                        ofb[0:128, cc * SW:(cc + 1) * SW], tmp[0:128, 0:SW],
                        sfg[0:128, cc * SW:(cc + 1) * SW]))
                    sqd = hsn if cc < 8 else xsq
                    SC_(lambda e, cc=cc, sqd=sqd: e.activation(
                        sqd[0:128, (cc % 8) * SW:(cc % 8 + 1) * SW],
                        ofb[0:128, cc * SW:(cc + 1) * SW], AF.Square),
                        w=[(s_ev, s_ev.n), (s_pe, s_pe.n)])
                P.wait("tensor", s_sc)
                for cc in range(16):
                    sqd = hsn if cc < 8 else xsq
                    P.emit("tensor", lambda e, cc=cc, sqd=sqd: e.matmul(
                        U.pb[4][0:1, 0:SW], C["ones_col_b"][0:128, 0:1],
                        sqd[0:128, (cc % 8) * SW:(cc % 8 + 1) * SW],
                        start=(cc == 0), stop=(cc == 15)),
                        inc=(s_pe, 1) if cc == 15 else None)
                VS(lambda e: e.tensor_scalar(
                    stat[0:1, 0:SW], U.pb[4][0:1, 0:SW], 1.0 / 2048.0, None, ALU.mult),
                    w=[(s_pe, s_pe.n)])
                SC_(lambda e: e.activation(
                    stat[0:1, 0:SW], stat[0:1, 0:SW], AF.Ln,
                    bias=C["epsb"][0:1, 0:1]), w=[(s_ev, s_ev.n)])
                SC_(lambda e: e.activation(
                    stat[0:1, 0:SW], stat[0:1, 0:SW], AF.Exp, scale=-0.5))
                P.wait("tensor", s_sc)
                P.emit("tensor", lambda e: e.matmul(
                    U.pb[5][0:128, 0:SW], C["ones_row_f"][0:1, 0:128],
                    stat[0:1, 0:SW], start=True, stop=True), inc=(s_pe, 1))
                P.wait("vector", s_pe)
                for cc in range(16):
                    VS(lambda e, cc=cc: e.scalar_tensor_tensor(
                        onb[0:128, cc * SW:(cc + 1) * SW],
                        ofb[0:128, cc * SW:(cc + 1) * SW],
                        bcol[0:128, 48 + cc:49 + cc], U.pb[5][0:128, 0:SW],
                        ALU.mult, ALU.mult))
                if PH < 8:
                    continue
                P.wait("tensor", s_ev)
                for op in range(8):
                    bank = U.pb[op % 2]
                    P.wait("tensor", s_ev, s_ev.n)
                    for cc in range(16):
                        P.emit("tensor", lambda e, op=op, cc=cc, bank=bank: e.matmul(
                            bank[0:128, 0:SW],
                            dnT[0:128, cc * 1024 + op * 128:cc * 1024 + (op + 1) * 128],
                            onb[0:128, cc * SW:(cc + 1) * SW],
                            start=(cc == 0), stop=(cc == 15)),
                            inc=(s_pe, 1) if cc == 15 else None)
                    P.emit("vector", lambda e, op=op, bank=bank: e.tensor_add(
                        tmp[0:128, op * SW:(op + 1) * SW], bank[0:128, 0:SW],
                        xtile[0:128, op * SW:(op + 1) * SW]),
                        waits=[(s_pe, s_pe.n)], inc=(s_ev, 1))
                P.emit("sync", lambda e, t0c=t0c: e.dma_start(
                    io["x_out"][0:1024, t0c:t0c + SW].rearrange(
                        "(c p) w -> p c w", p=128),
                    tmp[0:128, 0:8 * SW]),
                    waits=[(s_ev, s_ev.n)], inc=(s_st, 16))
            P.wait("sync", s_st)
    P.barrier()



def _sub_run(build_fns, specs, weights, xs):
    nc = bass.Bass("TRN2", target_bir_lowering=False)
    P, U = build_base(nc)
    io = {}
    for name, shape, dt, kind in specs:
        io[name] = nc.dram_tensor(name, shape, dt, kind=kind).ap()
    for fn, tag in build_fns:
        fn(P, U, io, tag)
    P.finalize()
    in_maps = []
    for xb in xs:
        m = dict(U.cin)
        m.update(weights)
        m["x_in"] = xb
        in_maps.append(m)
    res = run_bass_kernel_spmd(nc, in_maps, core_ids=list(range(len(xs))))
    return [np.asarray(res.results[b]["x_out"]) for b in range(len(xs))]


def kernel(**inp):
    B = 8
    u = np.asarray(inp["L0_attn_a_u"], np.float32)
    xs = [np.ascontiguousarray(u[b].T) for b in range(B)]
    H = host_constants()
    XIO = [("x_in", [1024, T], F32, "ExternalInput"),
           ("x_out", [1024, T], F32, "ExternalOutput")]

    xs = _sub_run([(build_L0_attn_a, "ssm")], XIO + [
        ("a_params", [512], F32, "ExternalInput"),
        ("B_w", [1024, 1024], F32, "ExternalInput"),
        ("C_w", [1024, 1024], F32, "ExternalInput"),
        ("Dd", [1024], F32, "ExternalInput"),
        ("C_wbf", [1024, 1024], BF16, "Internal")],
        {"a_params": inp["L0_attn_a_a_params"], "B_w": inp["L0_attn_a_B_w"],
         "C_w": inp["L0_attn_a_C_w"], "Dd": inp["L0_attn_a_D"]}, xs)

    ffn_specs = XIO + [
        ("W1", [4096, 1024], F32, "ExternalInput"),
        ("b1", [4096], F32, "ExternalInput"),
        ("W2", [1024, 4096], F32, "ExternalInput"),
        ("b2", [1024], F32, "ExternalInput"),
        ("scale", [1, 1], F32, "ExternalInput"),
        ("W1bf", [4096, 1024], BF16, "Internal"),
        ("W2bf", [1024, 4096], BF16, "Internal"),
        ("h_scr", [4096, 4096], BF16, "Internal")]
    ffn_w = {"W1": inp["L0_ffn_W1"], "b1": inp["L0_ffn_b1"],
             "W2": inp["L0_ffn_W2"], "b2": inp["L0_ffn_b2"],
             "scale": np.asarray(inp["L0_ffn_scale"], np.float32).reshape(1, 1)}
    xs = _sub_run([(build_L0_ffn, "ffn")], ffn_specs, ffn_w, xs)

    Lx = 4
    enc_shapes = dict(
        Wq=[Lx * 1024, 1024], Wk=[Lx * 1024, 1024], Wv=[Lx * 1024, 1024],
        Wo=[Lx * 1024, 1024], W1=[Lx * 1024, 4096], W2=[Lx * 4096, 1024],
        bq=[Lx * 1024], bk=[Lx * 1024], bv=[Lx * 1024], bo=[Lx * 1024],
        b1=[Lx * 4096], b2=[Lx * 1024], ln1_s=[Lx * 1024], ln1_b=[Lx * 1024],
        ln2_s=[Lx * 1024], ln2_b=[Lx * 1024], lnf_s=[1024], lnf_b=[1024])
    enc_specs = XIO + [(n, s, F32, "ExternalInput") for n, s in enc_shapes.items()]
    enc_specs += [("posemb", [1024, T], F32, "ExternalInput"),
                  ("enc_scr", [1024, T], F32, "Internal"),
                  ("enc2_scr", [1024, T], F32, "Internal"),
                  ("h_scr", [4096, 4096], BF16, "Internal")]
    enc_w = {n: np.asarray(inp[f"L0_attn_b_{n}"], np.float32).reshape(s)
             for n, s in enc_shapes.items()}
    enc_w["posemb"] = H["posemb_f"]
    xs = _sub_run([(build_L0_attn_b, "enc")], enc_specs, enc_w, xs)

    xs = _sub_run([(build_L0_ffn, "ffn")], ffn_specs, ffn_w, xs)

    attn_shapes = dict(ln1_g=[1024], ln1_b=[1024], Wqkv=[1024, 3072],
                       bqkv=[3072], Wo=[1024, 1024], bo=[1024], ln2_g=[1024],
                       ln2_b=[1024], W1=[1024, 4096], b1=[4096],
                       W2=[4096, 1024], b2=[1024])
    attn_specs = XIO + [(n, s, F32, "ExternalInput") for n, s in attn_shapes.items()]
    attn_specs += [("xin_scr", [1024, T], F32, "Internal"),
                   ("xa_scr", [1024, T], F32, "Internal"),
                   ("qkv_scr", [3072, T], BF16, "Internal"),
                   ("attn_scr", [1024, T], BF16, "Internal"),
                   ("h_scr", [4096, 4096], BF16, "Internal")]
    attn_w = {n: inp[f"L1_attn_a_{n}"] for n in attn_shapes}
    xs = _sub_run([(build_L1_attn_a, "ata")], attn_specs, attn_w, xs)

    moe_specs = XIO + [
        ("norm_scale", [1024], F32, "ExternalInput"),
        ("w_router", [8, 1024], F32, "ExternalInput"),
        ("w_up", [8 * 5632, 1024], F32, "ExternalInput"),
        ("w_down", [8 * 1024, 2816], F32, "ExternalInput"),
        ("up_bf", [8 * 5632, 1024], BF16, "Internal"),
        ("down_bf", [8 * 1024, 2816], BF16, "Internal"),
        ("w_upT_bf", [8 * 1024, 5632], BF16, "Internal"),
        ("w_downT_bf", [8 * 2816, 1024], BF16, "Internal"),
        ("xn_scr", [1024, T], BF16, "Internal"),
        ("xin_scr", [1024, T], F32, "Internal")]
    moe_w = {"norm_scale": inp["L1_ffn_norm_scale"],
             "w_router": inp["L1_ffn_w_router"],
             "w_up": np.asarray(inp["L1_ffn_w_up"], np.float32).reshape(8 * 5632, 1024),
             "w_down": np.asarray(inp["L1_ffn_w_down"], np.float32).reshape(8 * 1024, 2816)}
    moe_fns = [(build_moe_prologue, "moe"), (build_L1_ffn, "moe")]
    xs = _sub_run(moe_fns, moe_specs, moe_w, xs)

    gla_shapes = dict(up_w=[2048, 1024], gate_w=[2048, 1024], conv_w=[2048, 4],
                      conv_b=[2048], q_w=[64, 2048], k_w=[64, 2048],
                      g_w=[64, 2048], g_b=[64], tau_w=[64, 2048], tau_b=[64],
                      ig_w1=[16, 2048], ig_w2=[2048, 16], ig_b2=[2048],
                      residual_weight=[2048], norm_weight=[2048],
                      down_w=[1024, 2048])
    gla_specs = XIO + [(n, s, F32, "ExternalInput") for n, s in gla_shapes.items()]
    gla_specs += [("qkgt_w", [256, 2048], F32, "ExternalInput"),
                  ("upw_bf", [2048, 1024], BF16, "Internal"),
                  ("gatew_bf", [2048, 1024], BF16, "Internal"),
                  ("downw_bf", [1024, 2048], BF16, "Internal"),
                  ("qkgt_bf", [256, 2048], BF16, "Internal"),
                  ("ig1_bf", [16, 2048], BF16, "Internal")]
    gla_w = {n: inp[f"L1_attn_b_{n}"] for n in gla_shapes}
    gla_w["qkgt_w"] = np.ascontiguousarray(np.concatenate(
        [np.asarray(inp[f"L1_attn_b_{n}"], np.float32)
         for n in ("q_w", "k_w", "g_w", "tau_w")], 0))
    xs = _sub_run([(build_L1_attn_b, "gla")], gla_specs, gla_w, xs)

    xs = _sub_run(moe_fns, moe_specs, moe_w, xs)

    return np.stack([xb.T for xb in xs]).astype(np.float32)


# version 02 (final)
# speedup vs baseline: 1.0386x; 1.0386x over previous
import math
from contextlib import ExitStack
import numpy as np
import ml_dtypes
import concourse.bass as bass
import concourse.mybir as mybir
from concourse.bass_utils import run_bass_kernel_spmd

F32 = mybir.dt.float32
BF16 = mybir.dt.bfloat16
AF = mybir.ActivationFunctionType
ALU = mybir.AluOpType
AX = mybir.AxisListType

D = 1024
T = 4096
NCORE = 8
ENG = ("sync", "tensor", "vector", "scalar", "gpsimd")

BF = ml_dtypes.bfloat16

_CONST_CACHE = {}


def host_constants():
    if _CONST_CACHE:
        return _CONST_CACHE
    N = D
    j = np.arange(N)
    M_om = np.zeros((N, 512), dtype=np.float64)
    for jj in range(1, 512):
        M_om[:, jj - 1] += -np.sin(2 * np.pi * jj * j / N)
    for jj in range(513, 1024):
        M_om[:, 1023 - jj] -= -np.sin(2 * np.pi * jj * j / N)
    k5 = np.arange(512)
    ang = 2 * np.pi * np.outer(k5, j) / N
    P_pack = np.concatenate([np.cos(ang), -np.sin(ang)], axis=0)
    P_pack[512] = (-1.0) ** j
    G = np.zeros((N, N))
    G[:, 0] = 1.0 / N
    G[:, 512] = ((-1.0) ** j) / N
    for k in range(1, 512):
        G[:, k] = 2.0 / N * np.cos(2 * np.pi * j * k / N)
        G[:, 512 + k] = -2.0 / N * np.sin(2 * np.pi * j * k / N)
    half = D // 2
    freq = np.exp(np.arange(half) * (-math.log(10000.0) / (half - 1)))
    pos = (np.arange(T) + 1.0)[:, None] * freq[None, :]
    pe = np.concatenate([np.sin(pos), np.cos(pos)], -1).astype(np.float32)
    hd = 64
    inv_freq = 1.0 / 10000.0 ** (np.arange(0, hd, 2) / hd)
    freqs = np.arange(T)[:, None] * inv_freq[None, :]
    emb = np.concatenate([freqs, freqs], -1)
    _CONST_CACHE.update(dict(
        M_om=M_om.astype(np.float32),
        P_packT=np.ascontiguousarray(P_pack.T).astype(BF),
        G=np.ascontiguousarray(G).astype(BF),
        posemb_f=np.ascontiguousarray(pe.T).astype(np.float32),
        rope_cos=np.ascontiguousarray(np.cos(emb).T).astype(np.float32),
        rope_sin=np.ascontiguousarray(np.sin(emb).T).astype(np.float32),
        cmask=(np.triu(np.ones((128, 128), np.float32), k=1) * -1e9),
        tri_keepT=np.ascontiguousarray(
            np.tril(np.ones((64, 64), np.float32)).T),
        ident128=np.eye(128, dtype=np.float32),
        ident128b=np.eye(128, dtype=BF),
        ones_row_f=np.ones((1, 128), np.float32),
        ones_row_b=np.ones((1, 128), dtype=BF),
        ones_col_b=np.ones((128, 1), dtype=BF),
        zeros64=np.zeros((64, 64), np.float32),
    ))
    return _CONST_CACHE


class Sem:
    __slots__ = ("h", "n", "name", "maxwait")

    def __init__(self, h, name):
        self.h = h
        self.n = 0
        self.name = name
        self.maxwait = 0


class Prog:
    def __init__(self, nc):
        self.nc = nc
        self.ops = {e: [] for e in ENG}
        self.sems = []
        self.bsem = None

    def sem(self, name):
        s = Sem(self.nc.alloc_semaphore(name), name)
        self.sems.append(s)
        return s

    def emit(self, eng, fn, waits=(), inc=None, inc_k=None):
        wl = []
        for s, v in waits:
            s.maxwait = max(s.maxwait, v)
            wl.append((s.h, v))
        incv = None
        if inc is not None:
            if isinstance(inc, tuple):
                sem, k = inc
            else:
                sem, k = inc, 1
            if inc_k is not None:
                k = inc_k
            sem.n += k
            incv = (sem.h, k)

        def g(engine, fn=fn, wl=wl, incv=incv):
            for h, v in wl:
                engine.wait_ge(h, v)
            ins = fn(engine)
            if incv is not None:
                ins.then_inc(*incv)
        self.ops[eng].append(g)

    def wait(self, eng, sem, val=None):
        v = sem.n if val is None else val
        sem.maxwait = max(sem.maxwait, v)
        self.ops[eng].append(lambda engine, h=sem.h, v=v: engine.wait_ge(h, v))

    def barrier(self, engines=ENG):
        if self.bsem is None:
            self.bsem = self.sem("bsem")
        s = self.bsem
        d = self.dummy
        target = s.n + len(engines)
        for e in engines:
            if e == "vector":
                self.ops[e].append(lambda engine, h=s.h: engine.tensor_copy(
                    d[0:1, 0:1], d[0:1, 1:2]).then_inc(h, 1))
            elif e == "scalar":
                self.ops[e].append(lambda engine, h=s.h: engine.copy(
                    d[0:1, 2:3], d[0:1, 3:4]).then_inc(h, 1))
            elif e == "gpsimd":
                self.ops[e].append(lambda engine, h=s.h: engine.tensor_copy(
                    d[0:1, 4:5], d[0:1, 5:6]).then_inc(h, 1))
            else:
                self.ops[e].append(lambda engine, h=s.h: engine.sem_inc(h, 1))
        s.n = target
        for e in engines:
            self.wait(e, s, target)

    def finalize(self):
        for s in self.sems:
            assert s.maxwait <= s.n, (
                f"sem {s.name}: waits up to {s.maxwait} but only {s.n} incs")
        nc = self.nc
        with nc.Block() as block:
            for name in ENG:
                fns = self.ops[name]
                getattr(block, name)(lambda eng, fns=fns: [f(eng) for f in fns])


def ceil_div(a, b):
    return (a + b - 1) // b


class Util:
    def __init__(self, P, consts_sb, pb, pb_bf):
        self.P = P
        self.C = consts_sb
        self.pb = pb
        self.pb_bf = pb_bf

    def scoped_sb(self, stack, name, shape, dtype):
        return stack.enter_context(self.P.nc.sbuf_tensor(name, shape, dtype))


def build_base(nc):
    H = host_constants()
    P = Prog(nc)
    pb = [nc.alloc_psum_tensor(f"pb{i}", [128, 512], F32) for i in range(7)]
    pb_bf = nc.alloc_psum_tensor("pb_bf", [128, 1024], BF16)
    consts = {}
    cin = {}
    s_c = P.sem("const_dma")

    def add_const(name, arr, dtype, stack=None):
        d = nc.dram_tensor("c_" + name, list(arr.shape), dtype,
                           kind="ExternalInput").ap()
        if stack is None:
            s = nc.alloc_sbuf_tensor("sb_" + name, list(arr.shape), dtype)
        else:
            s = stack.enter_context(nc.sbuf_tensor("sb_" + name, list(arr.shape), dtype))
        consts[name] = s
        cin["c_" + name] = arr
        P.emit("sync", lambda e, s=s, d=d: e.dma_start(s[:], d),
               inc=(s_c, 16))
        return s

    add_const("ident128", H["ident128"], F32)
    add_const("ident128b", H["ident128b"], BF16)
    add_const("ones_row_f", H["ones_row_f"], F32)
    add_const("ones_row_b", H["ones_row_b"], BF16)
    add_const("ones_col_b", H["ones_col_b"], BF16)
    add_const("ones_col2_b", np.ones((128, 2), dtype=BF), BF16)
    add_const("ones_col_f", np.ones((128, 1), np.float32), F32)
    epsvals = np.array([1e-5, 1e-6, -6.0, 0.0, 1.0, 1e-12], np.float32)
    add_const("epsb", np.broadcast_to(epsvals, (128, 6)).copy(), F32)
    add_const("ones_row_b512", np.ones((1, 512), dtype=BF), BF16)
    P.dummy = nc.alloc_sbuf_tensor("sb_dummy", [1, 16], F32)
    P.emit("vector", lambda e: e.memset(P.dummy[0:1, 0:16], 0.0))
    U = Util(P, consts, pb, pb_bf)
    U.s_const = s_c
    U.cin = cin
    U.add_const = add_const
    for e in ENG:
        P.wait(e, s_c)
    return P, U


def emit_cast_bf16(P, d_src, d_dst, s_gp):
    P.emit("gpsimd", lambda e: e.dma_start(d_dst, d_src), inc=(s_gp, 16))


def emit_transpose_load(P, d_bf, rows, cols, out_sb, out_col_stride, s_tr,
                        waits=()):
    nchunk = cols // 128
    for c in range(nchunk):
        P.emit("sync", lambda e, c=c: e.dma_start(
            out_sb[0:128, c * out_col_stride:c * out_col_stride + rows],
            d_bf[0:rows, c * 128:(c + 1) * 128], transpose=True),
            waits=waits if c == 0 else (), inc=(s_tr, 16))


def emit_load_bias_cols(P, d_vec, n, out_sb, col0, s_gp, queue="gpsimd"):
    for c in range(n // 128):
        P.emit(queue, lambda e, c=c: e.dma_start(
            out_sb[0:128, col0 + c:col0 + c + 1], d_vec[c * 128:(c + 1) * 128]),
            inc=(s_gp, 16))


def emit_ln_tile(P, U, xtile_f32, xn_bf, tw, sems, eps_col=0, affine=None,
                 extra_f32=None, rms_only=False, out_fn=None):
    if out_fn is None:
        out_fn = lambda c: xn_bf[0:128, c * tw:(c + 1) * tw]
    """xn = (x - mean)/sqrt(var+eps) [* scale + bias].
    Stats psum: pb[5] (S1 row0, S2 row32); replicate: m->pb[6], rstd->pb[5].
    scratch tensors live in U.ln_scr (xb, xsq, stat, tmp)."""
    C = U.C
    pbs = U.pb[5]
    pbr = U.pb[6]
    s_pe, s_ev, s_sc = sems["pe"], sems["ev"], sems["sc"]
    scr = U.ln_scr
    xsq, stat, tmp = scr["xsq"], scr["stat"], scr["tmp"]
    P.wait("tensor", s_ev)
    P.wait("scalar", s_pe)
    P.wait("scalar", s_ev)
    for c in range(8):
        sl = slice(c * tw, (c + 1) * tw)
        P.emit("scalar", lambda e, sl=sl: e.activation(
            xsq[0:128, sl], xtile_f32[0:128, sl], AF.Square),
            inc=(s_sc, 1) if c == 7 else None)
    P.wait("tensor", s_sc)
    if not rms_only:
        for c in range(8):
            sl = slice(c * tw, (c + 1) * tw)
            P.emit("tensor", lambda e, sl=sl, c=c: e.matmul(
                pbs[0:1, 0:tw], C["ones_col_f"][0:128, 0:1], xtile_f32[0:128, sl],
                start=(c == 0), stop=(c == 7)))
    for c in range(8):
        sl = slice(c * tw, (c + 1) * tw)
        P.emit("tensor", lambda e, sl=sl, c=c: e.matmul(
            pbs[32:33, 0:tw], C["ones_col_b"][0:128, 0:1], xsq[0:128, sl],
            start=(c == 0), stop=(c == 7)),
            inc=(s_pe, 1) if c == 7 else None)
    P.wait("vector", s_pe)
    if rms_only:
        P.emit("vector", lambda e: e.tensor_scalar(
            stat[0:1, tw:2 * tw], pbs[32:33, 0:tw], 1.0 / D, None, ALU.mult),
            inc=(s_ev, 1))
    else:
        P.emit("vector", lambda e: e.tensor_scalar(
            stat[0:1, 0:tw], pbs[0:1, 0:tw], 1.0 / D, None, ALU.mult),
            inc=(s_ev, 1))
        P.emit("vector", lambda e: e.tensor_mul(
            stat[0:1, tw:2 * tw], stat[0:1, 0:tw], stat[0:1, 0:tw]),
            waits=[(s_ev, s_ev.n)], inc=(s_ev, 1))
        P.emit("vector", lambda e: e.scalar_tensor_tensor(
            stat[0:1, tw:2 * tw], pbs[32:33, 0:tw], 1.0 / D,
            stat[0:1, tw:2 * tw], ALU.mult, ALU.subtract),
            waits=[(s_ev, s_ev.n)], inc=(s_ev, 1))
    P.wait("scalar", s_ev)
    P.emit("scalar", lambda e: e.activation(
        stat[0:1, tw:2 * tw], stat[0:1, tw:2 * tw], AF.Ln,
        bias=C["epsb"][0:1, eps_col:eps_col + 1]), inc=(s_sc, 1))
    P.emit("scalar", lambda e: e.activation(
        stat[0:1, tw:2 * tw], stat[0:1, tw:2 * tw], AF.Exp, scale=-0.5),
        waits=[(s_sc, s_sc.n)], inc=(s_sc, 1))
    P.wait("tensor", s_sc)
    P.wait("tensor", s_ev)
    if not rms_only:
        P.emit("tensor", lambda e: e.matmul(
            pbr[0:128, 0:tw], C["ones_row_f"][0:1, 0:128], stat[0:1, 0:tw],
            start=True, stop=True))
    P.emit("tensor", lambda e: e.matmul(
        pbs[0:128, 0:tw], C["ones_row_f"][0:1, 0:128], stat[0:1, tw:2 * tw],
        start=True, stop=True), inc=(s_pe, 1))
    P.wait("vector", s_pe)
    for c in range(8):
        sl = slice(c * tw, (c + 1) * tw)
        if rms_only:
            src = xtile_f32
        else:
            P.emit("vector", lambda e, sl=sl: e.tensor_sub(
                tmp[0:128, sl], xtile_f32[0:128, sl], pbr[0:128, 0:tw]))
            src = tmp
        last = (c == 7)
        if affine is None:
            P.emit("vector", lambda e, sl=sl, src=src, c=c: e.tensor_mul(
                out_fn(c), src[0:128, sl], pbs[0:128, 0:tw]),
                inc=(s_ev, 1) if last else None)
        else:
            sc_cols, bi_cols = affine
            P.emit("vector", lambda e, sl=sl, src=src: e.tensor_mul(
                tmp[0:128, sl], src[0:128, sl], pbs[0:128, 0:tw]))
            if bi_cols is None:
                P.emit("vector", lambda e, sl=sl, c=c: e.tensor_scalar(
                    out_fn(c), tmp[0:128, sl],
                    sc_cols[0:128, c:c + 1], None, ALU.mult),
                    inc=(s_ev, 1) if last else None)
            else:
                P.emit("vector", lambda e, sl=sl, c=c: e.tensor_scalar(
                    out_fn(c), tmp[0:128, sl],
                    sc_cols[0:128, c:c + 1], bi_cols[0:128, c:c + 1],
                    ALU.mult, ALU.add),
                    inc=(s_ev, 1) if last else None)
        if extra_f32 is not None:
            P.emit("vector", lambda e, sl=sl, src=src: e.tensor_mul(
                extra_f32[0:128, sl], src[0:128, sl], pbs[0:128, 0:tw]),
                inc=(s_ev, 1) if last else None)


def build_L0_ffn(P, U, io, tag):
    with ExitStack() as _ST:
        _build_L0_ffn(P, U, io, tag, _ST)
    P.barrier()


def _build_L0_ffn(P, U, io, tag, _ST):
    nc, C = P.nc, U.C
    TW = 512
    s_gp = P.sem(f"{tag}_gp")
    s_tr = P.sem(f"{tag}_tr")
    s_x = P.sem(f"{tag}_x")
    s_pe = P.sem(f"{tag}_pe")
    s_ev = P.sem(f"{tag}_ev")
    s_sc = P.sem(f"{tag}_sc")
    s_st = P.sem(f"{tag}_st")
    sems = {"pe": s_pe, "ev": s_ev, "sc": s_sc}

    sb = lambda name, shape, dtype: U.scoped_sb(_ST, name, shape, dtype)
    xtile = sb(f"{tag}_xtile", [128, 8 * TW], F32)
    xn = sb(f"{tag}_xn", [128, 8 * TW], BF16)
    xsq = sb(f"{tag}_xsq", [128, 8 * TW], BF16)
    tmp = sb(f"{tag}_tmp", [128, 8 * TW], F32)
    stat = sb(f"{tag}_stat", [1, 2 * TW], F32)
    WT = sb(f"{tag}_WT", [128, 8 * 4096], BF16)
    htile = sb(f"{tag}_htile", [128, 32 * TW], BF16)
    xtok = sb(f"{tag}_xtok", [1, 8 * 1024], BF16)
    erow = sb(f"{tag}_erow", [1, 8 * 1024], BF16)
    ocol = sb(f"{tag}_ocol", [128, 64], BF16)
    dots = [sb(f"{tag}_dot{i}", [1, TW], BF16) for i in range(4)]
    d2sb = [sb(f"{tag}_d2_{k}", [1, TW], BF16) for k in range(16)]
    bcol = sb(f"{tag}_bcol", [128, 40], F32)
    scl = sb(f"{tag}_scl", [1, 1], F32)
    U.ln_scr = {"xsq": xsq, "stat": stat, "tmp": tmp}

    emit_cast_bf16(P, io["W1"], io["W1bf"], s_gp)
    emit_cast_bf16(P, io["W2"], io["W2bf"], s_gp)
    emit_load_bias_cols(P, io["b1"], 4096, bcol, 0, s_gp)
    emit_load_bias_cols(P, io["b2"], 1024, bcol, 32, s_gp)
    P.emit("gpsimd", lambda e: e.dma_start(scl[0:1, 0:1], io["scale"]),
           inc=(s_gp, 16))
    GP_ALL = s_gp.n
    emit_transpose_load(P, io["W1bf"], 4096, 1024, WT, 4096, s_tr,
                        waits=[(s_gp, GP_ALL)])
    TR1 = s_tr.n

    def ln_tile(t):
        P.emit("sync", lambda e, t=t: e.dma_start(
            xtile[0:128, 0:8 * TW],
            io["x_in"][0:1024, t * TW:(t + 1) * TW].rearrange(
                "(c p) w -> p c w", p=128)),
            waits=[(s_ev, s_ev.n)], inc=(s_x, 16))
        P.wait("scalar", s_x)
        P.wait("vector", s_x)
        emit_ln_tile(P, U, xtile, xn, TW, sems)

    ln_tile(0)
    P.wait("tensor", s_ev)
    for j in range(8):
        col = 2 * j + 1
        for c in range(8):
            P.emit("tensor", lambda e, c=c, col=col: e.transpose(
                U.pb_bf[0:1, c * 128:(c + 1) * 128],
                xn[0:128, c * TW + col:c * TW + col + 1],
                C["ident128b"][0:128, 0:128]),
                inc=(s_pe, 1) if c == 7 else None)
        P.emit("vector", lambda e, j=j: e.tensor_copy(
            xtok[0:1, j * 1024:(j + 1) * 1024], U.pb_bf[0:1, 0:1024]),
            waits=[(s_pe, s_pe.n)], inc=(s_ev, 1))
        P.wait("tensor", s_ev)
    P.wait("vector", s_gp, GP_ALL)
    for c in range(8):
        P.emit("vector", lambda e, c=c: e.tensor_copy(
            ocol[0:128, c * 8:c * 8 + 4],
            xn[0:128, c * TW + 2:c * TW + 10:2]))
        P.emit("vector", lambda e, c=c: e.tensor_copy(
            ocol[0:128, c * 8 + 4:c * 8 + 8],
            xn[0:128, c * TW + 10:c * TW + 18:2]),
            inc=(s_ev, 1) if c == 7 else None)
    for i in range(4):
        P.emit("vector", lambda e, i=i: e.tensor_scalar(
            erow[0:1, i * 1024:(i + 1) * 1024],
            xtok[0:1, i * 1024:(i + 1) * 1024],
            scl[0:1, 0:1], None, ALU.mult), inc=(s_ev, 1))
    for i in range(4):
        for pm in range(4):
            P.emit("vector", lambda e, i=i, pm=pm: e.tensor_scalar(
                erow[0:1, 4096 + (i * 4 + pm) * 256:4096 + (i * 4 + pm) * 256 + 256],
                xtok[0:1, (4 + i) * 1024 + pm:(5 + i) * 1024:4],
                scl[0:1, 0:1], None, ALU.mult),
                inc=(s_ev, 1))
    EXTRACT_EV = s_ev.n

    P.wait("tensor", s_tr, TR1)
    for t in range(8):
        if t > 0:
            ln_tile(t)
        P.wait("tensor", s_ev)
        P.wait("scalar", s_st)
        for i in range(4):
            for c in range(8):
                P.emit("tensor", lambda e, i=i, c=c: e.matmul(
                    U.pb[4][0:1, 0:TW], ocol[0:128, c * 8 + i:c * 8 + i + 1],
                    xn[0:128, c * TW:(c + 1) * TW],
                    start=(c == 0), stop=(c == 7)),
                    inc=(s_pe, 1) if c == 7 else None)
            P.emit("vector", lambda e, i=i: e.tensor_copy(
                dots[i][0:1, 0:TW], U.pb[4][0:1, 0:TW]),
                waits=[(s_pe, s_pe.n)], inc=(s_ev, 1))
            P.wait("tensor", s_ev)
        for hp in range(32):
            bank = U.pb[hp % 2]
            i = hp // 8
            cch = hp % 8
            P.wait("tensor", s_sc, s_sc.n)
            for c in range(8):
                P.emit("tensor", lambda e, hp=hp, c=c, bank=bank: e.matmul(
                    bank[0:128, 0:512],
                    WT[0:128, c * 4096 + hp * 128:c * 4096 + (hp + 1) * 128],
                    xn[0:128, c * TW:(c + 1) * TW], start=(c == 0), stop=False))
            P.emit("tensor", lambda e, i=i, cch=cch, bank=bank: e.matmul(
                bank[0:128, 0:512],
                erow[0:1, i * 1024 + cch * 128:i * 1024 + (cch + 1) * 128],
                dots[i][0:1, 0:TW], start=False, stop=True),
                inc=(s_pe, 1))
            P.emit("scalar", lambda e, hp=hp, bank=bank: e.activation(
                htile[0:128, hp * TW:(hp + 1) * TW], bank[0:128, 0:512],
                AF.Gelu, bias=bcol[0:128, hp:hp + 1]),
                waits=[(s_pe, s_pe.n)], inc=(s_sc, 1))
        P.emit("sync", lambda e, t=t: e.dma_start(
            io["h_scr"][0:4096, t * TW:(t + 1) * TW].rearrange(
                "(c p) w -> p c w", p=128),
            htile[0:128, 0:32 * TW]),
            waits=[(s_sc, s_sc.n)], inc=(s_st, 16))
    ST1 = s_st.n

    emit_transpose_load(P, io["W2bf"], 1024, 4096, WT, 1024, s_tr,
                        waits=[(s_st, ST1)])
    TR2 = s_tr.n

    P.wait("tensor", s_tr, TR2)
    P.wait("sync", s_st, ST1)
    for t in range(8):
        P.emit("sync", lambda e, t=t: e.dma_start(
            htile[0:128, 0:32 * TW],
            io["h_scr"][0:4096, t * TW:(t + 1) * TW].rearrange(
                "(c p) w -> p c w", p=128)),
            waits=[(s_ev, s_ev.n)], inc=(s_x, 16))
        P.emit("sync", lambda e, t=t: e.dma_start(
            xtile[0:128, 0:8 * TW],
            io["x_in"][0:1024, t * TW:(t + 1) * TW].rearrange(
                "(c p) w -> p c w", p=128)),
            inc=(s_x, 16))
        P.wait("tensor", s_x)
        P.wait("vector", s_x)
        P.wait("vector", s_st)
        for jj in range(4):
            for i in range(4):
                for c in range(8):
                    cc = jj * 8 + c
                    P.emit("tensor", lambda e, i=i, c=c, cc=cc: e.matmul(
                        U.pb[4][0:1, 0:TW], ocol[0:128, c * 8 + 4 + i:c * 8 + 5 + i],
                        htile[0:128, cc * TW:(cc + 1) * TW],
                        start=(c == 0), stop=(c == 7)),
                        inc=(s_pe, 1) if c == 7 else None)
                P.emit("vector", lambda e, jj=jj, i=i: e.tensor_copy(
                    d2sb[jj * 4 + i][0:1, 0:TW], U.pb[4][0:1, 0:TW]),
                    waits=[(s_pe, s_pe.n)], inc=(s_ev, 1))
                P.wait("tensor", s_ev)
        for op in range(8):
            bank = U.pb[2 + op % 2]
            i = op // 2
            P.wait("tensor", s_ev, s_ev.n)
            for cc in range(32):
                P.emit("tensor", lambda e, op=op, cc=cc, bank=bank: e.matmul(
                    bank[0:128, 0:512],
                    WT[0:128, cc * 1024 + op * 128:cc * 1024 + (op + 1) * 128],
                    htile[0:128, cc * TW:(cc + 1) * TW], start=(cc == 0), stop=False))
            for jj in range(4):
                P.emit("tensor", lambda e, op=op, jj=jj, i=i, bank=bank: e.matmul(
                    bank[0:128, 0:512],
                    erow[0:1, 4096 + (i * 4 + jj) * 256 + (op % 2) * 128:
                         4096 + (i * 4 + jj) * 256 + (op % 2) * 128 + 128],
                    d2sb[jj * 4 + i][0:1, 0:TW], start=False, stop=(jj == 3)),
                    inc=(s_pe, 1) if jj == 3 else None)
            P.emit("vector", lambda e, op=op, bank=bank: e.scalar_tensor_tensor(
                tmp[0:128, op * TW:(op + 1) * TW], bank[0:128, 0:512],
                bcol[0:128, 32 + op:33 + op], xtile[0:128, op * TW:(op + 1) * TW],
                ALU.add, ALU.add),
                waits=[(s_pe, s_pe.n)], inc=(s_ev, 1))
        P.emit("sync", lambda e, t=t: e.dma_start(
            io["x_out"][0:1024, t * TW:(t + 1) * TW].rearrange(
                "(c p) w -> p c w", p=128),
            tmp[0:128, 0:8 * TW]),
            waits=[(s_ev, s_ev.n)], inc=(s_st, 16))
    P.wait("sync", s_st)


def build_L1_attn_a(P, U, io, tag):
    nc, C = P.nc, U.C
    H = host_constants()
    TW = 512
    s_gp = P.sem(f"{tag}_gp")
    s_x = P.sem(f"{tag}_x")
    s_pe = P.sem(f"{tag}_pe")
    s_ev = P.sem(f"{tag}_ev")
    s_sc = P.sem(f"{tag}_sc")
    s_st = P.sem(f"{tag}_st")
    s_c = U.s_const
    sems = {"pe": s_pe, "ev": s_ev, "sc": s_sc}

    with ExitStack() as STC:
        sb = lambda name, shape, dtype: U.scoped_sb(STC, name, shape, dtype)
        xtile = sb(f"{tag}_xtile", [128, 8 * TW], F32)
        xin = sb(f"{tag}_xin", [128, 8 * TW], F32)
        xn = sb(f"{tag}_xn", [128, 8 * TW], BF16)
        normed = sb(f"{tag}_nrm", [128, 8 * TW], BF16)
        xsq = sb(f"{tag}_xsq", [128, 8 * TW], BF16)
        tmp = sb(f"{tag}_tmp", [128, 8 * TW], F32)
        stat = sb(f"{tag}_stat", [1, 2 * TW], F32)
        gcol = sb(f"{tag}_gcol", [128, 32], F32)
        brow = sb(f"{tag}_brow", [1, 9216], BF16)
        U.ln_scr = {"xsq": xsq, "stat": stat, "tmp": tmp}

        emit_load_bias_cols(P, io["ln1_g"], 1024, gcol, 0, s_gp)
        emit_load_bias_cols(P, io["ln1_b"], 1024, gcol, 8, s_gp)
        emit_load_bias_cols(P, io["ln2_g"], 1024, gcol, 16, s_gp)
        emit_load_bias_cols(P, io["ln2_b"], 1024, gcol, 24, s_gp)
        P.emit("gpsimd", lambda e: e.dma_start(brow[0:1, 0:3072], io["bqkv"]), inc=(s_gp, 16))
        P.emit("gpsimd", lambda e: e.dma_start(brow[0:1, 3072:4096], io["bo"]), inc=(s_gp, 16))
        P.emit("gpsimd", lambda e: e.dma_start(brow[0:1, 4096:8192], io["b1"]), inc=(s_gp, 16))
        P.emit("gpsimd", lambda e: e.dma_start(brow[0:1, 8192:9216], io["b2"]), inc=(s_gp, 16))

        with ExitStack() as STB:
            sbB = lambda name, shape, dtype: U.scoped_sb(STB, name, shape, dtype)
            WQKV = sbB(f"{tag}_wqkv", [128, 8 * 3072], BF16)
            acc = sbB(f"{tag}_acc", [128, 24 * TW], BF16)
            emit_cast_bf16(P, io["Wqkv"].rearrange("(c p) n -> p c n", p=128),
                           WQKV[0:128, 0:8 * 3072], s_gp)
            GP_ALL = s_gp.n
            for e_ in ("tensor", "vector", "scalar"):
                P.wait(e_, s_gp, GP_ALL)
            for t in range(8):
                P.emit("sync", lambda e, t=t: e.dma_start(
                    xtile[0:128, 0:8 * TW],
                    io["x_in"][0:1024, t * TW:(t + 1) * TW].rearrange(
                        "(c p) w -> p c w", p=128)),
                    waits=[(s_ev, s_ev.n)], inc=(s_x, 16))
                P.wait("scalar", s_x)
                P.wait("vector", s_x)
                P.wait("vector", s_st)
                emit_ln_tile(P, U, xtile, xn, TW, sems, extra_f32=xin)
                P.wait("sync", s_ev)
                P.emit("sync", lambda e, t=t: e.dma_start(
                    io["xin_scr"][0:1024, t * TW:(t + 1) * TW].rearrange(
                        "(c p) w -> p c w", p=128),
                    xin[0:128, 0:8 * TW]), inc=(s_st, 16))
                emit_ln_tile(P, U, xin, normed, TW, sems,
                             affine=(gcol[0:128, 0:8], gcol[0:128, 8:16]))
                P.wait("tensor", s_ev)
                for op in range(24):
                    bank = U.pb[op % 2]
                    P.wait("tensor", s_ev, s_ev.n)
                    for c in range(8):
                        P.emit("tensor", lambda e, op=op, c=c, bank=bank: e.matmul(
                            bank[0:128, 0:512],
                            WQKV[0:128, c * 3072 + op * 128:c * 3072 + (op + 1) * 128],
                            normed[0:128, c * TW:(c + 1) * TW], start=(c == 0), stop=False))
                    P.emit("tensor", lambda e, op=op, bank=bank: e.matmul(
                        bank[0:128, 0:512], brow[0:1, op * 128:(op + 1) * 128],
                        C["ones_row_b512"][0:1, 0:TW], start=False, stop=True),
                        inc=(s_pe, 1))
                    P.emit("vector", lambda e, op=op, bank=bank: e.tensor_copy(
                        acc[0:128, op * TW:(op + 1) * TW], bank[0:128, 0:512]),
                        waits=[(s_pe, s_pe.n)], inc=(s_ev, 1))
                P.emit("sync", lambda e, t=t: e.dma_start(
                    io["qkv_scr"][0:3072, t * TW:(t + 1) * TW].rearrange(
                        "(c p) w -> p c w", p=128),
                    acc[0:128, 0:24 * TW]),
                    waits=[(s_ev, s_ev.n)], inc=(s_st, 16))
        ST_B = s_st.n
        P.barrier()

        with ExitStack() as STCC:
            sbC = lambda name, shape, dtype: U.scoped_sb(STCC, name, shape, dtype)
            rc = U.add_const(tag + "ropec", H["rope_cos"].astype(BF), BF16, stack=STCC)
            rs2_np = H["rope_sin"].copy()
            rs2_np[0:32] = -rs2_np[0:32]
            rs = U.add_const(tag + "ropes2", rs2_np.astype(BF), BF16, stack=STCC)
            swap_np = np.zeros((64, 64), np.float32)
            for m in range(64):
                swap_np[(m + 32) % 64, m] = 1.0
            swp = U.add_const(tag + "swap64", swap_np.astype(BF), BF16, stack=STCC)
            qs = U.scoped_sb(STCC, f"{tag}_qs", [64, T], BF16)
            cm4 = np.zeros((128, 2048), np.float32)
            for a in range(4):
                i_ = np.arange(128)[:, None]
                j_ = np.arange(512)[None, :]
                cm4[:, a * 512:(a + 1) * 512] = np.where(j_ > a * 128 + i_, -1e9, 0.0)
            cmask = U.add_const(tag + "cmask", cm4, F32, stack=STCC)
            qh = sbC(f"{tag}_qh", [64, T], BF16)
            kh = sbC(f"{tag}_kh", [64, T], BF16)
            vh = sbC(f"{tag}_vh", [64, T], BF16)
            qr = sbC(f"{tag}_qr", [64, T], BF16)
            kr = sbC(f"{tag}_kr", [64, T], BF16)
            vtok = sbC(f"{tag}_vtok", [128, 32 * 64], BF16)
            Pbuf = sbC(f"{tag}_P", [128, T], BF16)
            PT = sbC(f"{tag}_PT", [128, T], BF16)
            oh = sbC(f"{tag}_oh", [64, T], BF16)
            lrow = sbC(f"{tag}_l", [128, 16], F32)
            P.wait("sync", s_st, ST_B)
            P.wait("vector", s_c)
            P.wait("tensor", s_c)
            for h in range(16):
                for (dst, off) in ((qh, 0), (kh, 1024), (vh, 2048)):
                    P.emit("sync", lambda e, dst=dst, off=off, h=h: e.dma_start(
                        dst[0:64, 0:T],
                        io["qkv_scr"][off + h * 64:off + (h + 1) * 64, 0:T]),
                        waits=[(s_ev, s_ev.n), (s_pe, s_pe.n), (s_st, s_st.n)],
                        inc=(s_x, 16))
                P.wait("vector", s_x)
                P.wait("vector", s_st)
                P.wait("tensor", s_x)
                for (src_, dst) in ((qh, qr), (kh, kr)):
                    for blk in range(8):
                        P.emit("tensor", lambda e, src=src_, blk=blk: e.matmul(
                            U.pb[3][0:64, 0:512], swp[0:64, 0:64],
                            src[0:64, blk * 512:(blk + 1) * 512],
                            start=True, stop=True),
                            waits=[(s_ev, s_ev.n)], inc=(s_pe, 1))
                        P.emit("vector", lambda e, blk=blk: e.tensor_copy(
                            qs[0:64, blk * 512:(blk + 1) * 512], U.pb[3][0:64, 0:512]),
                            waits=[(s_pe, s_pe.n)], inc=(s_ev, 1))
                    P.wait("vector", s_ev)
                    P.emit("vector", lambda e: e.tensor_mul(
                        qs[0:64, 0:T], qs[0:64, 0:T], rs[0:64, 0:T]))
                    P.emit("vector", lambda e, src=src_, dst=dst: e.tensor_mul(
                        dst[0:64, 0:T], src[0:64, 0:T], rc[0:64, 0:T]))
                    P.emit("vector", lambda e, dst=dst: e.tensor_add(
                        dst[0:64, 0:T], dst[0:64, 0:T], qs[0:64, 0:T]),
                        inc=(s_ev, 1))
                for kc in range(0, 32, 4):
                    for k2 in range(4):
                        P.emit("tensor", lambda e, kc=kc, k2=k2: e.transpose(
                            U.pb_bf[0:128, k2 * 64:(k2 + 1) * 64],
                            vh[0:64, (kc + k2) * 128:(kc + k2 + 1) * 128],
                            C["ident128b"][0:64, 0:64]),
                            inc=(s_pe, 1) if k2 == 3 else None)
                    P.emit("vector", lambda e, kc=kc: e.tensor_copy(
                        vtok[0:128, kc * 64:(kc + 4) * 64], U.pb_bf[0:128, 0:256]),
                        waits=[(s_pe, s_pe.n)], inc=(s_ev, 1))
                    P.wait("tensor", s_ev)
                P.wait("tensor", s_ev)
                if h == 0 and "dbg_qr" in io:
                    P.emit("sync", lambda e: e.dma_start(io["dbg_qr"], qr[0:64, 0:T]),
                           waits=[(s_ev, s_ev.n)], inc=(s_st, 16))
                    P.emit("sync", lambda e: e.dma_start(io["dbg_kr"], kr[0:64, 0:T]),
                           inc=(s_st, 16))
                    P.emit("sync", lambda e: e.dma_start(io["dbg_vt"], vtok[0:128, 0:2048]),
                           inc=(s_st, 16))
                for qt in range(32):
                    nb = (qt * 128 + 128 + 511) // 512
                    a = qt % 4
                    for b in range(nb):
                        bank = U.pb[b % 2]
                        P.wait("tensor", s_sc, s_sc.n)
                        P.emit("tensor", lambda e, qt=qt, b=b, bank=bank: e.matmul(
                            bank[0:128, 0:512], qr[0:64, qt * 128:(qt + 1) * 128],
                            kr[0:64, b * 512:(b + 1) * 512], start=True, stop=True),
                            inc=(s_pe, 1))
                        if b == nb - 1:
                            P.emit("vector", lambda e, a=a, bank=bank: e.tensor_add(
                                bank[0:128, 0:512], bank[0:128, 0:512],
                                cmask[0:128, a * 512:(a + 1) * 512]),
                                waits=[(s_pe, s_pe.n)], inc=(s_ev, 1))
                            P.emit("scalar", lambda e, b=b, bank=bank: e.activation(
                                Pbuf[0:128, b * 512:(b + 1) * 512], bank[0:128, 0:512],
                                AF.Exp, bias=C["epsb"][0:128, 2:3], scale=0.125),
                                waits=[(s_ev, s_ev.n)], inc=(s_sc, 1))
                            P.emit("vector", lambda e, b=b: e.reduce_sum(
                                lrow[0:128, b:b + 1], Pbuf[0:128, b * 512:(b + 1) * 512],
                                axis=AX.X),
                                waits=[(s_sc, s_sc.n)], inc=(s_ev, 1))
                        else:
                            P.emit("scalar", lambda e, b=b, bank=bank: e.activation(
                                Pbuf[0:128, b * 512:(b + 1) * 512], bank[0:128, 0:512],
                                AF.Exp, bias=C["epsb"][0:128, 2:3], scale=0.125,
                                accum_out=lrow[0:128, b:b + 1]),
                                waits=[(s_pe, s_pe.n), (s_ev, s_ev.n)], inc=(s_sc, 1))
                    P.wait("vector", s_sc)
                    P.emit("vector", lambda e, nb=nb: e.reduce_sum(
                        lrow[0:128, 8:9], lrow[0:128, 0:nb], axis=AX.X),
                        waits=[(s_ev, s_ev.n)], inc=(s_ev, 1))
                    P.emit("vector", lambda e: e.reciprocal(
                        lrow[0:128, 9:10], lrow[0:128, 8:9]),
                        waits=[(s_ev, s_ev.n)], inc=(s_ev, 1))
                    P.emit("vector", lambda e, nb=nb: e.tensor_scalar(
                        Pbuf[0:128, 0:nb * 512], Pbuf[0:128, 0:nb * 512],
                        lrow[0:128, 9:10], None, ALU.mult),
                        waits=[(s_ev, s_ev.n)], inc=(s_ev, 1))
                    P.wait("tensor", s_ev)
                    nkc = nb * 4
                    for kc0 in range(0, nkc, 4):
                        kn = min(4, nkc - kc0)
                        for k2 in range(kn):
                            P.emit("tensor", lambda e, kc0=kc0, k2=k2: e.transpose(
                                U.pb_bf[0:128, k2 * 128:(k2 + 1) * 128],
                                Pbuf[0:128, (kc0 + k2) * 128:(kc0 + k2 + 1) * 128],
                                C["ident128b"][0:128, 0:128]),
                                inc=(s_pe, 1) if k2 == kn - 1 else None)
                        P.emit("vector", lambda e, kc0=kc0, kn=kn: e.tensor_copy(
                            PT[0:128, kc0 * 128:(kc0 + kn) * 128],
                            U.pb_bf[0:128, 0:kn * 128]),
                            waits=[(s_pe, s_pe.n)], inc=(s_ev, 1))
                        P.wait("tensor", s_ev)
                    for kc in range(nkc):
                        P.emit("tensor", lambda e, kc=kc, nkc=nkc: e.matmul(
                            U.pb[2][0:64, 0:128], vtok[0:128, kc * 64:(kc + 1) * 64],
                            PT[0:128, kc * 128:(kc + 1) * 128],
                            start=(kc == 0), stop=(kc == nkc - 1)),
                            inc=(s_pe, 1) if kc == nkc - 1 else None)
                    P.emit("vector", lambda e, qt=qt: e.tensor_copy(
                        oh[0:64, qt * 128:(qt + 1) * 128], U.pb[2][0:64, 0:128]),
                        waits=[(s_pe, s_pe.n)], inc=(s_ev, 1))
                    P.wait("tensor", s_ev)
                    if h == 0 and qt == 4 and "dbg_P" in io:
                        P.emit("sync", lambda e: e.dma_start(io["dbg_P"], Pbuf[0:128, 0:1024]),
                               waits=[(s_ev, s_ev.n)], inc=(s_st, 16))
                        P.emit("sync", lambda e: e.dma_start(io["dbg_l"], lrow[0:128, 0:16]),
                               inc=(s_st, 16))
                P.emit("sync", lambda e, h=h: e.dma_start(
                    io["attn_scr"][h * 64:(h + 1) * 64, 0:T], oh[0:64, 0:T]),
                    waits=[(s_ev, s_ev.n)], inc=(s_st, 16))
        ST_C = s_st.n
        P.barrier()

        with ExitStack() as STD:
            sbD = lambda name, shape, dtype: U.scoped_sb(STD, name, shape, dtype)
            WO = sbD(f"{tag}_wo", [128, 8 * 1024], BF16)
            accD = sbD(f"{tag}_accD", [128, 8 * TW], BF16)
            P.wait("gpsimd", s_st, ST_C)
            emit_cast_bf16(P, io["Wo"].rearrange("(c p) n -> p c n", p=128),
                           WO[0:128, 0:8 * 1024], s_gp)
            P.wait("tensor", s_gp)
            P.wait("sync", s_st, ST_C)
            for t in range(8):
                P.emit("sync", lambda e, t=t: e.dma_start(
                    accD[0:128, 0:8 * TW],
                    io["attn_scr"][0:1024, t * TW:(t + 1) * TW].rearrange(
                        "(c p) w -> p c w", p=128)),
                    waits=[(s_ev, s_ev.n)], inc=(s_x, 16))
                P.emit("sync", lambda e, t=t: e.dma_start(
                    xin[0:128, 0:8 * TW],
                    io["xin_scr"][0:1024, t * TW:(t + 1) * TW].rearrange(
                        "(c p) w -> p c w", p=128)), inc=(s_x, 16))
                P.wait("tensor", s_x)
                P.wait("vector", s_x)
                P.wait("vector", s_st)
                for op in range(8):
                    bank = U.pb[op % 2]
                    P.wait("tensor", s_ev, s_ev.n)
                    for c in range(8):
                        P.emit("tensor", lambda e, op=op, c=c, bank=bank: e.matmul(
                            bank[0:128, 0:512],
                            WO[0:128, c * 1024 + op * 128:c * 1024 + (op + 1) * 128],
                            accD[0:128, c * TW:(c + 1) * TW], start=(c == 0), stop=False))
                    P.emit("tensor", lambda e, op=op, bank=bank: e.matmul(
                        bank[0:128, 0:512], brow[0:1, 3072 + op * 128:3072 + (op + 1) * 128],
                        C["ones_row_b512"][0:1, 0:TW], start=False, stop=True),
                        inc=(s_pe, 1))
                    P.emit("vector", lambda e, op=op, bank=bank: e.tensor_add(
                        tmp[0:128, op * TW:(op + 1) * TW], bank[0:128, 0:512],
                        xin[0:128, op * TW:(op + 1) * TW]),
                        waits=[(s_pe, s_pe.n)], inc=(s_ev, 1))
                P.emit("sync", lambda e, t=t: e.dma_start(
                    io["xa_scr"][0:1024, t * TW:(t + 1) * TW].rearrange(
                        "(c p) w -> p c w", p=128),
                    tmp[0:128, 0:8 * TW]),
                    waits=[(s_ev, s_ev.n)], inc=(s_st, 16))
        ST_D = s_st.n
        P.barrier()

        with ExitStack() as STE:
            sbE = lambda name, shape, dtype: U.scoped_sb(STE, name, shape, dtype)
            W1 = sbE(f"{tag}_W1", [128, 8 * 4096], BF16)
            ffn1 = sbE(f"{tag}_ffn1", [128, 32 * TW], BF16)
            P.wait("gpsimd", s_st, ST_D)
            emit_cast_bf16(P, io["W1"].rearrange("(c p) n -> p c n", p=128),
                           W1[0:128, 0:8 * 4096], s_gp)
            P.wait("tensor", s_gp)
            P.wait("sync", s_st, ST_D)
            for t in range(8):
                P.emit("sync", lambda e, t=t: e.dma_start(
                    xin[0:128, 0:8 * TW],
                    io["xa_scr"][0:1024, t * TW:(t + 1) * TW].rearrange(
                        "(c p) w -> p c w", p=128)),
                    waits=[(s_ev, s_ev.n)], inc=(s_x, 16))
                P.wait("scalar", s_x)
                P.wait("vector", s_x)
                emit_ln_tile(P, U, xin, normed, TW, sems,
                             affine=(gcol[0:128, 16:24], gcol[0:128, 24:32]))
                P.wait("tensor", s_ev)
                P.wait("scalar", s_st)
                for hp in range(32):
                    bank = U.pb[hp % 2]
                    P.wait("tensor", s_sc, s_sc.n)
                    for c in range(8):
                        P.emit("tensor", lambda e, hp=hp, c=c, bank=bank: e.matmul(
                            bank[0:128, 0:512],
                            W1[0:128, c * 4096 + hp * 128:c * 4096 + (hp + 1) * 128],
                            normed[0:128, c * TW:(c + 1) * TW], start=(c == 0), stop=False))
                    P.emit("tensor", lambda e, hp=hp, bank=bank: e.matmul(
                        bank[0:128, 0:512], brow[0:1, 4096 + hp * 128:4096 + (hp + 1) * 128],
                        C["ones_row_b512"][0:1, 0:TW], start=False, stop=True),
                        inc=(s_pe, 1))
                    P.emit("scalar", lambda e, hp=hp, bank=bank: e.activation(
                        ffn1[0:128, hp * TW:(hp + 1) * TW], bank[0:128, 0:512], AF.Gelu),
                        waits=[(s_pe, s_pe.n)], inc=(s_sc, 1))
                P.emit("sync", lambda e, t=t: e.dma_start(
                    io["h_scr"][0:4096, t * TW:(t + 1) * TW].rearrange(
                        "(c p) w -> p c w", p=128),
                    ffn1[0:128, 0:32 * TW]),
                    waits=[(s_sc, s_sc.n)], inc=(s_st, 16))
        ST_E1 = s_st.n
        P.barrier()

        with ExitStack() as STE2:
            sbE2 = lambda name, shape, dtype: U.scoped_sb(STE2, name, shape, dtype)
            W2 = sbE2(f"{tag}_W2", [128, 32 * 1024], BF16)
            ffn1b = sbE2(f"{tag}_ffn1b", [128, 32 * TW], BF16)
            P.wait("gpsimd", s_st, ST_E1)
            emit_cast_bf16(P, io["W2"].rearrange("(c p) n -> p c n", p=128),
                           W2[0:128, 0:32 * 1024], s_gp)
            P.wait("tensor", s_gp)
            P.wait("sync", s_st, ST_E1)
            for t in range(8):
                P.emit("sync", lambda e, t=t: e.dma_start(
                    ffn1b[0:128, 0:32 * TW],
                    io["h_scr"][0:4096, t * TW:(t + 1) * TW].rearrange(
                        "(c p) w -> p c w", p=128)),
                    waits=[(s_ev, s_ev.n)], inc=(s_x, 16))
                P.emit("sync", lambda e, t=t: e.dma_start(
                    xin[0:128, 0:8 * TW],
                    io["xa_scr"][0:1024, t * TW:(t + 1) * TW].rearrange(
                        "(c p) w -> p c w", p=128)), inc=(s_x, 16))
                P.emit("sync", lambda e, t=t: e.dma_start(
                    xtile[0:128, 0:8 * TW],
                    io["x_in"][0:1024, t * TW:(t + 1) * TW].rearrange(
                        "(c p) w -> p c w", p=128)), inc=(s_x, 16))
                P.wait("tensor", s_x)
                P.wait("vector", s_x)
                P.wait("vector", s_st)
                for op in range(8):
                    bank = U.pb[op % 2]
                    P.wait("tensor", s_ev, s_ev.n)
                    for cc in range(32):
                        P.emit("tensor", lambda e, op=op, cc=cc, bank=bank: e.matmul(
                            bank[0:128, 0:512],
                            W2[0:128, cc * 1024 + op * 128:cc * 1024 + (op + 1) * 128],
                            ffn1b[0:128, cc * TW:(cc + 1) * TW], start=(cc == 0), stop=False))
                    P.emit("tensor", lambda e, op=op, bank=bank: e.matmul(
                        bank[0:128, 0:512], brow[0:1, 8192 + op * 128:8192 + (op + 1) * 128],
                        C["ones_row_b512"][0:1, 0:TW], start=False, stop=True),
                        inc=(s_pe, 1))
                    P.emit("vector", lambda e, op=op, bank=bank: e.tensor_add(
                        tmp[0:128, op * TW:(op + 1) * TW], bank[0:128, 0:512],
                        xin[0:128, op * TW:(op + 1) * TW]),
                        waits=[(s_pe, s_pe.n)])
                    P.emit("vector", lambda e, op=op: e.tensor_add(
                        tmp[0:128, op * TW:(op + 1) * TW],
                        tmp[0:128, op * TW:(op + 1) * TW],
                        xtile[0:128, op * TW:(op + 1) * TW]), inc=(s_ev, 1))
                P.emit("sync", lambda e, t=t: e.dma_start(
                    io["x_out"][0:1024, t * TW:(t + 1) * TW].rearrange(
                        "(c p) w -> p c w", p=128),
                    tmp[0:128, 0:8 * TW]),
                    waits=[(s_ev, s_ev.n)], inc=(s_st, 16))
            P.wait("sync", s_st)
    P.barrier()


def build_moe_prologue(P, U, io, tag):
    s_gp = P.sem(f"{tag}_pgp")
    s_tr = P.sem(f"{tag}_ptr")
    s_st = P.sem(f"{tag}_pst")
    with ExitStack() as ST:
        wt = U.scoped_sb(ST, f"{tag}_wtbuf", [128, 5632], BF16)
        wt2 = U.scoped_sb(ST, f"{tag}_wtbuf2", [128, 5632], BF16)
        for e in range(8):
            P.emit("gpsimd", lambda en, e=e: en.dma_start(
                io["up_bf"][e * 5632:(e + 1) * 5632, 0:1024],
                io["w_up"][e * 5632:(e + 1) * 5632, 0:1024]), inc=(s_gp, 16))
            P.emit("gpsimd", lambda en, e=e: en.dma_start(
                io["down_bf"][e * 1024:(e + 1) * 1024, 0:2816],
                io["w_down"][e * 1024:(e + 1) * 1024, 0:2816]), inc=(s_gp, 16))
        GP = s_gp.n
        for e in range(8):
            for j in range(8):
                buf = wt if j % 2 == 0 else wt2
                P.emit("sync", lambda en, e=e, j=j, buf=buf: en.dma_start(
                    buf[0:128, 0:5632],
                    io["up_bf"][e * 5632:(e + 1) * 5632, j * 128:(j + 1) * 128],
                    transpose=True),
                    waits=[(s_gp, GP), (s_st, max(0, s_st.n - 16))], inc=(s_tr, 16))
                P.emit("sync", lambda en, e=e, j=j, buf=buf: en.dma_start(
                    io["w_upT_bf"][e * 1024 + j * 128:e * 1024 + (j + 1) * 128, 0:5632],
                    buf[0:128, 0:5632]),
                    waits=[(s_tr, s_tr.n)], inc=(s_st, 16))
        for e in range(8):
            for j in range(22):
                buf = wt if j % 2 == 0 else wt2
                P.emit("sync", lambda en, e=e, j=j, buf=buf: en.dma_start(
                    buf[0:128, 0:1024],
                    io["down_bf"][e * 1024:(e + 1) * 1024, j * 128:(j + 1) * 128],
                    transpose=True),
                    waits=[(s_st, max(0, s_st.n - 16))], inc=(s_tr, 16))
                P.emit("sync", lambda en, e=e, j=j, buf=buf: en.dma_start(
                    io["w_downT_bf"][e * 2816 + j * 128:e * 2816 + (j + 1) * 128, 0:1024],
                    buf[0:128, 0:1024]),
                    waits=[(s_tr, s_tr.n)], inc=(s_st, 16))
        P.wait("sync", s_st)
    P.barrier()


def build_L1_ffn(P, U, io, tag):
    nc, C = P.nc, U.C
    TW = 512
    s_gp = P.sem(f"{tag}_gp")
    s_x = P.sem(f"{tag}_x")
    s_pe = P.sem(f"{tag}_pe")
    s_ev = P.sem(f"{tag}_ev")
    s_sc = P.sem(f"{tag}_sc")
    s_st = P.sem(f"{tag}_st")
    sems = {"pe": s_pe, "ev": s_ev, "sc": s_sc}
    with ExitStack() as ST0:
        c_tok = U.scoped_sb(ST0, f"{tag}_ctok", [128, 256], F32)
        with ExitStack() as STN:
            sbN = lambda n, s, d: U.scoped_sb(STN, n, s, d)
            xtile = sbN(f"{tag}_xt", [128, 8 * TW], F32)
            xin_t = sbN(f"{tag}_xint", [128, 8 * TW], F32)
            xn = sbN(f"{tag}_xn", [128, 8 * TW], BF16)
            xsq = sbN(f"{tag}_xsq", [128, 8 * TW], BF16)
            tmp = sbN(f"{tag}_tmp", [128, 8 * TW], F32)
            stat = sbN(f"{tag}_stat", [1, 2 * TW], F32)
            scol = sbN(f"{tag}_scol", [128, 8], F32)
            rrow = sbN(f"{tag}_rr", [8, 1024], F32)
            rTs = sbN(f"{tag}_rTs", [128, 64], F32)
            rt = sbN(f"{tag}_rt", [128, 48], F32)
            rs = sbN(f"{tag}_rs", [128, 8], F32)
            U.ln_scr = {"xsq": xsq, "stat": stat, "tmp": tmp}
            emit_load_bias_cols(P, io["norm_scale"], 1024, scol, 0, s_gp)
            P.emit("gpsimd", lambda e: e.dma_start(rrow[0:8, 0:1024], io["w_router"]),
                   inc=(s_gp, 16))
            GP_N = s_gp.n
            P.wait("tensor", s_gp, GP_N)
            P.wait("vector", s_gp, GP_N)
            for c in range(8):
                P.emit("tensor", lambda e, c=c: e.transpose(
                    U.pb[4][0:128, 0:8], rrow[0:8, c * 128:(c + 1) * 128],
                    C["ident128"][0:8, 0:8]), inc=(s_pe, 1))
                P.emit("vector", lambda e, c=c: e.tensor_scalar(
                    rTs[0:128, c * 8:(c + 1) * 8], U.pb[4][0:128, 0:8],
                    scol[0:128, c:c + 1], None, ALU.mult),
                    waits=[(s_pe, s_pe.n)], inc=(s_ev, 1))
                P.wait("tensor", s_ev)
            for t in range(8):
                P.emit("sync", lambda e, t=t: e.dma_start(
                    xtile[0:128, 0:8 * TW],
                    io["x_in"][0:1024, t * TW:(t + 1) * TW].rearrange(
                        "(c p) w -> p c w", p=128)),
                    waits=[(s_ev, s_ev.n)], inc=(s_x, 16))
                P.wait("scalar", s_x)
                P.wait("vector", s_x)
                P.wait("vector", s_st)
                emit_ln_tile(P, U, xtile, xn, TW, sems, extra_f32=xin_t)
                P.wait("sync", s_ev)
                P.emit("sync", lambda e, t=t: e.dma_start(
                    io["xin_scr"][0:1024, t * TW:(t + 1) * TW].rearrange(
                        "(c p) w -> p c w", p=128),
                    xin_t[0:128, 0:8 * TW]), inc=(s_st, 16))
                emit_ln_tile(P, U, xin_t, xn, TW, sems, eps_col=1,
                             affine=(scol, None), rms_only=True)
                P.wait("sync", s_ev)
                P.emit("sync", lambda e, t=t: e.dma_start(
                    io["xn_scr"][0:1024, t * TW:(t + 1) * TW].rearrange(
                        "(c p) w -> p c w", p=128),
                    xn[0:128, 0:8 * TW]), inc=(s_st, 16))
                P.wait("tensor", s_ev)
                for i in range(4):
                    tt = t * 4 + i
                    for c in range(8):
                        P.emit("tensor", lambda e, c=c, i=i: e.matmul(
                            U.pb[4][0:128, 0:8],
                            xin_t[0:128, c * TW + i * 128:c * TW + (i + 1) * 128],
                            rTs[0:128, c * 8:(c + 1) * 8],
                            start=(c == 0), stop=(c == 7)),
                            waits=[(s_ev, s_ev.n)] if c == 0 else ())
                    P.emit("tensor", lambda e, i=i: e.transpose(
                        U.pb[4][0:128, 8:9], stat[0:1, TW + i * 128:TW + (i + 1) * 128],
                        C["ident128"][0:1, 0:1]), inc=(s_pe, 1))
                    def ve(fn, w=None):
                        P.emit("vector", fn,
                               waits=[(s_ev, s_ev.n)] if w is None else w,
                               inc=(s_ev, 1))
                    ve(lambda e: e.tensor_copy(rt[0:128, 0:9], U.pb[4][0:128, 0:9]),
                       w=[(s_pe, s_pe.n)])
                    ve(lambda e: e.tensor_scalar(rt[0:128, 0:8], rt[0:128, 0:8],
                                                 rt[0:128, 8:9], None, ALU.mult))
                    ve(lambda e: e.reduce_max(rs[0:128, 0:1], rt[0:128, 0:8], axis=AX.X))
                    ve(lambda e: e.tensor_scalar(rt[0:128, 8:16], rt[0:128, 0:8],
                                                 rs[0:128, 0:1], None, ALU.is_ge))
                    ve(lambda e: e.scalar_tensor_tensor(rt[0:128, 16:24], rt[0:128, 8:16],
                                                        -1e9, rt[0:128, 0:8],
                                                        ALU.mult, ALU.add))
                    ve(lambda e: e.reduce_max(rs[0:128, 1:2], rt[0:128, 16:24], axis=AX.X))
                    ve(lambda e: e.tensor_scalar(rt[0:128, 24:32], rt[0:128, 16:24],
                                                 rs[0:128, 1:2], None, ALU.is_ge))
                    ve(lambda e: e.tensor_add(rt[0:128, 32:40], rt[0:128, 8:16],
                                              rt[0:128, 24:32]))
                    ve(lambda e: e.tensor_scalar(rs[0:128, 2:3], rs[0:128, 0:1],
                                                 -1.0, None, ALU.mult))
                    P.emit("scalar", lambda e: e.activation(
                        rt[0:128, 40:48], rt[0:128, 0:8], AF.Exp,
                        bias=rs[0:128, 2:3], accum_out=rs[0:128, 3:4]),
                        waits=[(s_ev, s_ev.n)], inc=(s_sc, 1))
                    ve(lambda e: e.reciprocal(rs[0:128, 4:5], rs[0:128, 3:4]),
                       w=[(s_sc, s_sc.n)])
                    ve(lambda e, tt=tt: e.scalar_tensor_tensor(
                        c_tok[0:128, tt * 8:(tt + 1) * 8], rt[0:128, 40:48],
                        rs[0:128, 4:5], rt[0:128, 32:40], ALU.mult, ALU.mult))
        ST_N = s_st.n
        P.barrier()

        with ExitStack() as STE:
            sbE = lambda n, s, d: U.scoped_sb(STE, n, s, d)
            xn_q = sbE(f"{tag}_xnq", [128, 8 * 1024], BF16)
            wbuf = sbE(f"{tag}_wbuf", [128, 8 * 5632], BF16)
            hm = sbE(f"{tag}_hm", [128, 22 * 1024], BF16)
            y_acc = sbE(f"{tag}_yacc", [128, 8 * 1024], F32)
            sg = sbE(f"{tag}_sg", [128, TW], BF16)
            x_q = wbuf[0:128, 0:16384].bitcast(F32)
            xin_q = wbuf[0:128, 16384:32768].bitcast(F32)
            for q in range(4):
                P.emit("sync", lambda e, q=q: e.dma_start(
                    xn_q[0:128, 0:8 * 1024],
                    io["xn_scr"][0:1024, q * 1024:(q + 1) * 1024].rearrange(
                        "(c p) w -> p c w", p=128)),
                    waits=[(s_st, ST_N), (s_ev, s_ev.n)], inc=(s_x, 16))
                P.wait("vector", s_x)
                P.wait("vector", s_st)
                P.wait("vector", s_pe)
                P.emit("vector", lambda e: e.memset(y_acc[0:128, 0:8 * 1024], 0.0),
                       inc=(s_ev, 1))
                P.wait("tensor", s_x)
                for ex in range(8):
                    P.emit("sync", lambda e, ex=ex: e.dma_start(
                        wbuf[0:128, 0:8 * 5632],
                        io["w_upT_bf"][ex * 1024:(ex + 1) * 1024, 0:5632].rearrange(
                            "(c p) n -> p c n", p=128)),
                        waits=[(s_pe, s_pe.n), (s_ev, s_ev.n), (s_st, s_st.n)],
                        inc=(s_x, 16))
                    P.wait("tensor", s_x)
                    for th in range(2):
                        for hp in range(22):
                            for c in range(8):
                                P.emit("tensor", lambda e, c=c, hp=hp, th=th: e.matmul(
                                    U.pb[0][0:128, 0:512],
                                    wbuf[0:128, c * 5632 + hp * 128:c * 5632 + (hp + 1) * 128],
                                    xn_q[0:128, c * 1024 + th * TW:c * 1024 + (th + 1) * TW],
                                    start=(c == 0), stop=(c == 7)),
                                    waits=[(s_ev, s_ev.n)] if c == 0 else ())
                            for c in range(8):
                                P.emit("tensor", lambda e, c=c, hp=hp, th=th: e.matmul(
                                    U.pb[1][0:128, 0:512],
                                    wbuf[0:128, c * 5632 + 2816 + hp * 128:
                                         c * 5632 + 2816 + (hp + 1) * 128],
                                    xn_q[0:128, c * 1024 + th * TW:c * 1024 + (th + 1) * TW],
                                    start=(c == 0), stop=(c == 7)),
                                    inc=(s_pe, 1) if c == 7 else None)
                            P.emit("scalar", lambda e: e.activation(
                                sg[0:128, 0:TW], U.pb[1][0:128, 0:512], AF.Silu),
                                waits=[(s_sc, s_sc.n), (s_pe, s_pe.n)], inc=(s_sc, 1))
                            P.emit("vector", lambda e, hp=hp, th=th: e.tensor_mul(
                                hm[0:128, hp * 1024 + th * TW:hp * 1024 + (th + 1) * TW],
                                U.pb[0][0:128, 0:512], sg[0:128, 0:TW]),
                                waits=[(s_sc, s_sc.n)], inc=(s_ev, 1))
                    P.emit("sync", lambda e, ex=ex: e.dma_start(
                        wbuf[0:128, 0:22 * 1024],
                        io["w_downT_bf"][ex * 2816:(ex + 1) * 2816, 0:1024].rearrange(
                            "(c p) n -> p c n", p=128)),
                        waits=[(s_pe, s_pe.n), (s_ev, s_ev.n)], inc=(s_x, 16))
                    P.wait("tensor", s_x)
                    for t1 in range(8):
                        tt = q * 8 + t1
                        for half in range(2):
                            bank = U.pb[2 + half]
                            for hc in range(22):
                                P.emit("tensor", lambda e, hc=hc, t1=t1, half=half, bank=bank: e.matmul(
                                    bank[0:128, 0:512],
                                    hm[0:128, hc * 1024 + t1 * 128:hc * 1024 + (t1 + 1) * 128],
                                    wbuf[0:128, hc * 1024 + half * 512:hc * 1024 + (half + 1) * 512],
                                    start=(hc == 0), stop=(hc == 21)),
                                    waits=[(s_ev, s_ev.n)] if (hc == 0) else (),
                                    inc=(s_pe, 1) if hc == 21 else None)
                            P.emit("vector", lambda e, t1=t1, half=half, tt=tt, ex=ex, bank=bank:
                                   e.scalar_tensor_tensor(
                                y_acc[0:128, t1 * 1024 + half * 512:t1 * 1024 + (half + 1) * 512],
                                bank[0:128, 0:512],
                                c_tok[0:128, tt * 8 + ex:tt * 8 + ex + 1],
                                y_acc[0:128, t1 * 1024 + half * 512:t1 * 1024 + (half + 1) * 512],
                                ALU.mult, ALU.add),
                                waits=[(s_pe, s_pe.n)], inc=(s_ev, 1))
                P.emit("sync", lambda e, q=q: e.dma_start(
                    x_q[0:128, 0:8 * 1024],
                    io["x_in"][0:1024, q * 1024:(q + 1) * 1024].rearrange(
                        "(c p) w -> p c w", p=128)),
                    waits=[(s_st, s_st.n), (s_pe, s_pe.n), (s_sc, s_sc.n),
                           (s_ev, s_ev.n)], inc=(s_x, 16))
                P.emit("sync", lambda e, q=q: e.dma_start(
                    xin_q[0:128, 0:8 * 1024],
                    io["xin_scr"][0:1024, q * 1024:(q + 1) * 1024].rearrange(
                        "(c p) w -> p c w", p=128)), inc=(s_x, 16))
                P.wait("vector", s_x)
                P.wait("tensor", s_x)
                for t1 in range(8):
                    for dc in range(8):
                        P.emit("tensor", lambda e, t1=t1, dc=dc: e.transpose(
                            U.pb[4][0:128, 0:128],
                            y_acc[0:128, t1 * 1024 + dc * 128:t1 * 1024 + (dc + 1) * 128],
                            C["ident128"][0:128, 0:128]),
                            waits=[(s_ev, s_ev.n)], inc=(s_pe, 1))
                        P.emit("vector", lambda e, t1=t1, dc=dc: e.tensor_add(
                            x_q[0:128, dc * 1024 + t1 * 128:dc * 1024 + (t1 + 1) * 128],
                            U.pb[4][0:128, 0:128],
                            x_q[0:128, dc * 1024 + t1 * 128:dc * 1024 + (t1 + 1) * 128]),
                            waits=[(s_pe, s_pe.n)], inc=(s_ev, 1))
                        P.emit("vector", lambda e, t1=t1, dc=dc: e.tensor_add(
                            x_q[0:128, dc * 1024 + t1 * 128:dc * 1024 + (t1 + 1) * 128],
                            x_q[0:128, dc * 1024 + t1 * 128:dc * 1024 + (t1 + 1) * 128],
                            xin_q[0:128, dc * 1024 + t1 * 128:dc * 1024 + (t1 + 1) * 128]),
                            waits=[(s_ev, s_ev.n)], inc=(s_ev, 1))
                        P.wait("tensor", s_ev)
                P.emit("sync", lambda e, q=q: e.dma_start(
                    io["x_out"][0:1024, q * 1024:(q + 1) * 1024].rearrange(
                        "(c p) w -> p c w", p=128),
                    x_q[0:128, 0:8 * 1024]),
                    waits=[(s_ev, s_ev.n)], inc=(s_st, 16))
            P.wait("sync", s_st)
    P.barrier()


def build_L0_attn_a(P, U, io, tag):
    nc, C = P.nc, U.C
    H = host_constants()
    TW = 512
    s_gp = P.sem(f"{tag}_gp")
    s_tr = P.sem(f"{tag}_tr")
    s_x = P.sem(f"{tag}_x")
    s_pe = P.sem(f"{tag}_pe")
    s_ev = P.sem(f"{tag}_ev")
    s_sc = P.sem(f"{tag}_sc")
    s_st = P.sem(f"{tag}_st")
    s_c = U.s_const
    sems = {"pe": s_pe, "ev": s_ev, "sc": s_sc}

    def vsync(fn, extra=()):
        P.emit("vector", fn, waits=[(s_ev, s_ev.n)] + list(extra), inc=(s_ev, 1))

    def pack128(arr, nch):
        n = arr.shape[1]
        return np.ascontiguousarray(
            arr.reshape(nch, 128, n).transpose(1, 0, 2).reshape(128, nch * n))

    with ExitStack() as ST0:
        sb0 = lambda n, s, d: U.scoped_sb(ST0, n, s, d)
        xn = sb0(f"{tag}_xn", [128, 8 * T], BF16)
        Hsb = sb0(f"{tag}_H", [128, 8 * T], BF16)
        BrT = sb0(f"{tag}_BrT", [128, 8 * 1024], BF16)
        lam = sb0(f"{tag}_lam", [128, 8], F32)
        pw = sb0(f"{tag}_pw", [128, 32], F32)
        Dcol = sb0(f"{tag}_D", [128, 8], F32)
        omdbg = sb0(f"{tag}_omdbg", [128, 4], F32)
        acol = sb0(f"{tag}_acol", [128, 8], F32)
        P.emit("vector", lambda e: e.memset(acol[0:128, 0:8], 0.0), inc=(s_ev, 1))
        for j in range(4):
            P.emit("gpsimd", lambda e, j=j: e.dma_start(
                acol[0:128, 2 * j:2 * j + 1], io["a_params"][j * 128:(j + 1) * 128]),
                waits=[(s_ev, s_ev.n)], inc=(s_gp, 16))
        emit_load_bias_cols(P, io["Dd"], 1024, Dcol, 0, s_gp)
        GP0 = s_gp.n

        with ExitStack() as STL:
            sbL = lambda n, s, d: U.scoped_sb(STL, n, s, d)
            xtile = sbL(f"{tag}_xt", [128, 8 * TW], F32)
            xsq = sbL(f"{tag}_xsq", [128, 8 * TW], BF16)
            tmp = sbL(f"{tag}_tmp", [128, 8 * TW], F32)
            stat = sbL(f"{tag}_stat", [1, 2 * TW], F32)
            U.ln_scr = {"xsq": xsq, "stat": stat, "tmp": tmp}
            for t in range(8):
                P.emit("sync", lambda e, t=t: e.dma_start(
                    xtile[0:128, 0:8 * TW],
                    io["x_in"][0:1024, t * TW:(t + 1) * TW].rearrange(
                        "(c p) w -> p c w", p=128)),
                    waits=[(s_ev, s_ev.n)], inc=(s_x, 16))
                P.wait("scalar", s_x)
                P.wait("vector", s_x)
                emit_ln_tile(P, U, xtile, None, TW, sems,
                             out_fn=lambda c, t=t: xn[0:128, c * T + t * TW:
                                                      c * T + t * TW + TW])
        P.barrier()

        with ExitStack() as STO:
            Momk = U.add_const(
                tag + "Momk",
                pack128(np.ascontiguousarray(H["M_om"][:512, :]), 4), F32, stack=STO)
            arow = U.scoped_sb(STO, f"{tag}_arow", [1, 512], F32)
            tmpo = U.scoped_sb(STO, f"{tag}_tmpo", [128, 512], F32)
            P.emit("gpsimd", lambda e: e.dma_start(arow[0:1, 0:512], io["a_params"]),
                   inc=(s_gp, 16))
            GPA = s_gp.n
            P.wait("tensor", s_gp, GPA)
            P.wait("tensor", s_c)
            P.wait("vector", s_c)
            P.emit("tensor", lambda e: e.matmul(
                U.pb[4][0:128, 0:512], C["ones_row_f"][0:1, 0:128],
                arow[0:1, 0:512], start=True, stop=True), inc=(s_pe, 1))
            P.wait("vector", s_pe)
            for kp in range(4):
                vsync(lambda e, kp=kp: e.tensor_mul(
                    tmpo[0:128, 0:512], Momk[0:128, kp * 512:(kp + 1) * 512],
                    U.pb[4][0:128, 0:512]))
                vsync(lambda e, kp=kp: e.reduce_sum(
                    pw[0:128, 20 + kp:21 + kp], tmpo[0:128, 0:512], axis=AX.X))
            P.wait("tensor", s_ev)
            vsync(lambda e: e.tensor_copy(omdbg[0:128, 0:4], pw[0:128, 20:24]))
            vsync(lambda e: e.tensor_mul(pw[0:128, 0:4], pw[0:128, 20:24], pw[0:128, 20:24]))
            vsync(lambda e: e.tensor_scalar(pw[0:128, 4:8], pw[0:128, 0:4], 1.0, None, ALU.add))
            vsync(lambda e: e.reciprocal(pw[0:128, 8:12], pw[0:128, 4:8]))
            vsync(lambda e: e.tensor_scalar(pw[0:128, 0:4], pw[0:128, 0:4], -1.0, 1.0,
                                            ALU.mult, ALU.add))
            vsync(lambda e: e.tensor_mul(lam[0:128, 0:4], pw[0:128, 0:4], pw[0:128, 8:12]))
            vsync(lambda e: e.tensor_scalar(pw[0:128, 12:16], pw[0:128, 20:24], -2.0, None,
                                            ALU.mult))
            vsync(lambda e: e.tensor_mul(lam[0:128, 4:8], pw[0:128, 12:16], pw[0:128, 8:12]))

        if "dbg_xn2" in io:
            P.emit("sync", lambda e: e.dma_start(io["dbg_xn2"], xn[0:128, 0:8 * T]),
                   waits=[(s_ev, s_ev.n)], inc=(s_st, 16))
        if "dbg_lam" in io:
            P.emit("sync", lambda e: e.dma_start(io["dbg_lam"], lam[0:128, 0:8]),
                   waits=[(s_ev, s_ev.n)], inc=(s_st, 16))
        P.barrier()
        with ExitStack() as STB:
            sbB = lambda n, s, d: U.scoped_sb(STB, n, s, d)
            Bw = sbB(f"{tag}_Bw", [128, 8 * 1024], BF16)
            Ppt = U.add_const(tag + "Ppt", pack128(np.asarray(H["P_packT"], np.float32), 8).astype(BF),
                              BF16, stack=STB)
            emit_cast_bf16(P, io["B_w"].rearrange("(c p) n -> p c n", p=128),
                           Bw[0:128, 0:8 * 1024], s_gp)
            P.wait("tensor", s_gp)
            P.wait("tensor", s_c)
            for dp in range(8):
                for half in range(2):
                    bank = U.pb[half]
                    for j in range(8):
                        P.emit("tensor", lambda e, dp=dp, half=half, j=j, bank=bank: e.matmul(
                            bank[0:128, 0:512],
                            Bw[0:128, j * 1024 + dp * 128:j * 1024 + (dp + 1) * 128],
                            Ppt[0:128, j * 1024 + half * 512:j * 1024 + (half + 1) * 512],
                            start=(j == 0), stop=(j == 7)),
                            waits=[(s_ev, s_ev.n)] if j == 0 else (),
                            inc=(s_pe, 1) if j == 7 else None)
                    P.emit("vector", lambda e, dp=dp, half=half, bank=bank: e.tensor_copy(
                        BrT[0:128, dp * 1024 + half * 512:dp * 1024 + (half + 1) * 512],
                        bank[0:128, 0:512]),
                        waits=[(s_pe, s_pe.n)], inc=(s_ev, 1))
        if "dbg_brt" in io:
            P.emit("sync", lambda e: e.dma_start(io["dbg_brt"], BrT[0:128, 0:8192]),
                   waits=[(s_ev, s_ev.n)], inc=(s_st, 16))
        P.barrier()

        with ExitStack() as STS:
            sbS = lambda n, s, d: U.scoped_sb(STS, n, s, d)
            vre = sbS(f"{tag}_vre", [128, T], BF16)
            vim = sbS(f"{tag}_vim", [128, T], BF16)
            Mre = sbS(f"{tag}_Mre", [128, 1024], F32)
            Mim = sbS(f"{tag}_Mim", [128, 1024], F32)
            Tre = sbS(f"{tag}_Tre", [128, 1024], F32)
            Tim = sbS(f"{tag}_Tim", [128, 1024], F32)
            sre = sbS(f"{tag}_sre", [128, 1024], F32)
            sim = sbS(f"{tag}_sim", [128, 1024], F32)
            tt1 = sbS(f"{tag}_tt1", [128, 1024], F32)
            tt2 = sbS(f"{tag}_tt2", [128, 1024], F32)
            tt3 = sbS(f"{tag}_tt3", [128, 1024], F32)
            for p in range(4):
                for (dst, pkoff, bank) in ((vre, p * 128, U.pb[0]),
                                           (vim, 512 + p * 128, U.pb[1])):
                    for t in range(8):
                        for cc in range(8):
                            P.emit("tensor", lambda e, pkoff=pkoff, t=t, cc=cc, bank=bank: e.matmul(
                                bank[0:128, 0:512],
                                BrT[0:128, cc * 1024 + pkoff:cc * 1024 + pkoff + 128],
                                xn[0:128, cc * T + t * TW:cc * T + (t + 1) * TW],
                                start=(cc == 0), stop=(cc == 7)),
                                waits=[(s_ev, s_ev.n)] if cc == 0 else (),
                                inc=(s_pe, 1) if cc == 7 else None)
                        P.emit("vector", lambda e, dst=dst, t=t, bank=bank: e.tensor_copy(
                            dst[0:128, t * TW:(t + 1) * TW], bank[0:128, 0:512]),
                            waits=[(s_pe, s_pe.n)], inc=(s_ev, 1))
                vsync(lambda e, p=p: e.tensor_copy(pw[0:128, 0:1], lam[0:128, p:p + 1]))
                vsync(lambda e, p=p: e.tensor_scalar(pw[0:128, 16:17], lam[0:128, 4 + p:5 + p],
                                                     -1.0, None, ALU.mult))
                for j in range(10):
                    vsync(lambda e, j=j: e.tensor_mul(pw[0:128, 12:13], pw[0:128, j:j + 1],
                                                      pw[0:128, j:j + 1]))
                    vsync(lambda e, j=j: e.tensor_mul(pw[0:128, 13:14], pw[0:128, 16 + j:17 + j],
                                                      pw[0:128, 16 + j:17 + j]))
                    vsync(lambda e, j=j: e.tensor_sub(pw[0:128, j + 1:j + 2], pw[0:128, 12:13],
                                                      pw[0:128, 13:14]))
                    vsync(lambda e, j=j: e.tensor_mul(pw[0:128, 14:15], pw[0:128, j:j + 1],
                                                      pw[0:128, 16 + j:17 + j]))
                    vsync(lambda e, j=j: e.tensor_scalar(pw[0:128, 17 + j:18 + j],
                                                         pw[0:128, 14:15], 2.0, None, ALU.mult))
                vsync(lambda e: e.memset(Mre[0:128, 0:1], 1.0))
                vsync(lambda e: e.memset(Mim[0:128, 0:1], 0.0))
                for j in range(10):
                    w = 1 << j
                    vsync(lambda e, j=j, w=w: e.tensor_scalar(
                        tt1[0:128, 0:w], Mre[0:128, 0:w], pw[0:128, j:j + 1], None, ALU.mult))
                    vsync(lambda e, j=j, w=w: e.tensor_scalar(
                        tt2[0:128, 0:w], Mim[0:128, 0:w], pw[0:128, 16 + j:17 + j], None, ALU.mult))
                    vsync(lambda e, j=j, w=w: e.tensor_scalar(
                        tt3[0:128, 0:w], Mre[0:128, 0:w], pw[0:128, 16 + j:17 + j], None, ALU.mult))
                    vsync(lambda e, j=j, w=w: e.scalar_tensor_tensor(
                        Mim[0:128, w:2 * w], Mim[0:128, 0:w], pw[0:128, j:j + 1],
                        tt3[0:128, 0:w], ALU.mult, ALU.add))
                    vsync(lambda e, j=j, w=w: e.tensor_sub(
                        Mre[0:128, w:2 * w], tt1[0:128, 0:w], tt2[0:128, 0:w]))
                vsync(lambda e: e.memset(pw[0:128, 30:31], 1.0))
                vsync(lambda e: e.memset(pw[0:128, 31:32], 0.0))
                for cch in range(4):
                    sl = slice(cch * 1024, (cch + 1) * 1024)
                    vsync(lambda e: e.tensor_scalar(tt1[0:128, 0:1024], Mre[0:128, 0:1024],
                                                    pw[0:128, 30:31], None, ALU.mult))
                    vsync(lambda e: e.tensor_scalar(tt2[0:128, 0:1024], Mim[0:128, 0:1024],
                                                    pw[0:128, 31:32], None, ALU.mult))
                    vsync(lambda e: e.tensor_sub(Tre[0:128, 0:1024], tt1[0:128, 0:1024],
                                                 tt2[0:128, 0:1024]))
                    vsync(lambda e: e.tensor_scalar(tt1[0:128, 0:1024], Mre[0:128, 0:1024],
                                                    pw[0:128, 31:32], None, ALU.mult))
                    vsync(lambda e: e.tensor_scalar(tt2[0:128, 0:1024], Mim[0:128, 0:1024],
                                                    pw[0:128, 30:31], None, ALU.mult))
                    vsync(lambda e: e.tensor_add(Tim[0:128, 0:1024], tt1[0:128, 0:1024],
                                                 tt2[0:128, 0:1024]))
                    vsync(lambda e, sl=sl: e.tensor_mul(tt1[0:128, 0:1024], Tre[0:128, 0:1024],
                                                        vre[0:128, sl]))
                    vsync(lambda e, sl=sl: e.tensor_mul(tt3[0:128, 0:1024], Tim[0:128, 0:1024],
                                                        vim[0:128, sl]))
                    vsync(lambda e: e.tensor_sub(tt1[0:128, 0:1024], tt1[0:128, 0:1024],
                                                 tt3[0:128, 0:1024]))
                    vsync(lambda e, sl=sl: e.tensor_mul(tt2[0:128, 0:1024], Tre[0:128, 0:1024],
                                                        vim[0:128, sl]))
                    vsync(lambda e, sl=sl: e.tensor_mul(tt3[0:128, 0:1024], Tim[0:128, 0:1024],
                                                        vre[0:128, sl]))
                    vsync(lambda e: e.tensor_add(tt2[0:128, 0:1024], tt2[0:128, 0:1024],
                                                 tt3[0:128, 0:1024]))
                    ini_re = 0.0 if cch == 0 else pw[0:128, 28:29]
                    ini_im = 0.0 if cch == 0 else pw[0:128, 29:30]
                    vsync(lambda e, ini=ini_re: e.tensor_tensor_scan(
                        sre[0:128, 0:1024], tt1[0:128, 0:1024], tt1[0:128, 0:1024],
                        ini, ALU.add, ALU.bypass))
                    vsync(lambda e, ini=ini_im: e.tensor_tensor_scan(
                        sim[0:128, 0:1024], tt2[0:128, 0:1024], tt2[0:128, 0:1024],
                        ini, ALU.add, ALU.bypass))
                    vsync(lambda e: e.tensor_copy(pw[0:128, 28:29], sre[0:128, 1023:1024]))
                    vsync(lambda e: e.tensor_copy(pw[0:128, 29:30], sim[0:128, 1023:1024]))
                    vsync(lambda e: e.tensor_mul(tt1[0:128, 0:1024], Tre[0:128, 0:1024],
                                                 sre[0:128, 0:1024]))
                    vsync(lambda e: e.tensor_mul(tt2[0:128, 0:1024], Tim[0:128, 0:1024],
                                                 sim[0:128, 0:1024]))
                    vsync(lambda e, p=p, cch=cch: e.tensor_add(
                        Hsb[0:128, p * T + cch * 1024:p * T + (cch + 1) * 1024],
                        tt1[0:128, 0:1024], tt2[0:128, 0:1024]))
                    vsync(lambda e: e.tensor_mul(tt1[0:128, 0:1024], Tre[0:128, 0:1024],
                                                 sim[0:128, 0:1024]))
                    vsync(lambda e: e.tensor_mul(tt2[0:128, 0:1024], Tim[0:128, 0:1024],
                                                 sre[0:128, 0:1024]))
                    vsync(lambda e, p=p, cch=cch: e.tensor_sub(
                        Hsb[0:128, (4 + p) * T + cch * 1024:(4 + p) * T + (cch + 1) * 1024],
                        tt1[0:128, 0:1024], tt2[0:128, 0:1024]))
                    vsync(lambda e: e.tensor_mul(pw[0:128, 24:25], pw[0:128, 30:31],
                                                 pw[0:128, 10:11]))
                    vsync(lambda e: e.tensor_mul(pw[0:128, 25:26], pw[0:128, 31:32],
                                                 pw[0:128, 26:27]))
                    vsync(lambda e: e.tensor_mul(pw[0:128, 27:28], pw[0:128, 30:31],
                                                 pw[0:128, 26:27]))
                    vsync(lambda e: e.tensor_mul(pw[0:128, 22:23], pw[0:128, 31:32],
                                                 pw[0:128, 10:11]))
                    vsync(lambda e: e.tensor_sub(pw[0:128, 30:31], pw[0:128, 24:25],
                                                 pw[0:128, 25:26]))
                    vsync(lambda e: e.tensor_add(pw[0:128, 31:32], pw[0:128, 27:28],
                                                 pw[0:128, 22:23]))
        if "dbg_h" in io:
            P.emit("sync", lambda e: e.dma_start(io["dbg_h"], Hsb[0:128, 0:8 * T]),
                   waits=[(s_ev, s_ev.n)], inc=(s_st, 16))
            P.emit("sync", lambda e: e.dma_start(io["dbg_xn"], xn[0:128, 0:8 * T]),
                   inc=(s_st, 16))
        P.barrier()

        with ExitStack() as STC:
            sbC = lambda n, s, d: U.scoped_sb(STC, n, s, d)
            CrT = sbC(f"{tag}_CrT", [128, 8 * 1024], BF16)
            STC1 = STC.enter_context(ExitStack())
            CwT = U.scoped_sb(STC1, f"{tag}_CwT", [128, 8 * 1024], BF16)
            Gc = U.add_const(tag + "G", pack128(np.asarray(H["G"], np.float32), 8).astype(BF),
                             BF16, stack=STC1)
            emit_cast_bf16(P, io["C_w"], io["C_wbf"], s_gp)
            GPC = s_gp.n
            emit_transpose_load(P, io["C_wbf"], 1024, 1024, CwT, 1024, s_tr,
                                waits=[(s_gp, GPC)])
            P.wait("tensor", s_tr)
            P.wait("tensor", s_c)
            for pkp in range(8):
                for half in range(2):
                    bank = U.pb[half]
                    for j in range(8):
                        P.emit("tensor", lambda e, pkp=pkp, half=half, j=j, bank=bank: e.matmul(
                            bank[0:128, 0:512],
                            Gc[0:128, j * 1024 + pkp * 128:j * 1024 + (pkp + 1) * 128],
                            CwT[0:128, j * 1024 + half * 512:j * 1024 + (half + 1) * 512],
                            start=(j == 0), stop=(j == 7)),
                            waits=[(s_ev, s_ev.n)] if j == 0 else (),
                            inc=(s_pe, 1) if j == 7 else None)
                    P.emit("vector", lambda e, pkp=pkp, half=half, bank=bank: e.tensor_copy(
                        CrT[0:128, pkp * 1024 + half * 512:pkp * 1024 + (half + 1) * 512],
                        bank[0:128, 0:512]),
                        waits=[(s_pe, s_pe.n)], inc=(s_ev, 1))
            if "dbg_crt" in io:
                P.emit("sync", lambda e: e.dma_start(io["dbg_crt"], CrT[0:128, 0:8192]),
                       waits=[(s_ev, s_ev.n)], inc=(s_st, 16))
            STC1.close()
            P.barrier()
            xtile2 = sbC(f"{tag}_xt2", [128, 8 * TW], F32)
            out2 = sbC(f"{tag}_out2", [128, 8 * TW], F32)
            for t in range(8):
                P.emit("sync", lambda e, t=t: e.dma_start(
                    xtile2[0:128, 0:8 * TW],
                    io["x_in"][0:1024, t * TW:(t + 1) * TW].rearrange(
                        "(c p) w -> p c w", p=128)),
                    waits=[(s_st, s_st.n), (s_ev, s_ev.n)], inc=(s_x, 16))
                P.wait("tensor", s_x)
                P.wait("vector", s_x)
                P.wait("vector", s_st)
                for dp in range(8):
                    bank = U.pb[2 + dp % 2]
                    P.wait("tensor", s_ev, s_ev.n)
                    for pk in range(8):
                        P.emit("tensor", lambda e, dp=dp, pk=pk, t=t, bank=bank: e.matmul(
                            bank[0:128, 0:512],
                            CrT[0:128, pk * 1024 + dp * 128:pk * 1024 + (dp + 1) * 128],
                            Hsb[0:128, pk * T + t * TW:pk * T + (t + 1) * TW],
                            start=(pk == 0), stop=(pk == 7)),
                            inc=(s_pe, 1) if pk == 7 else None)
                    P.emit("vector", lambda e, dp=dp, bank=bank: e.tensor_add(
                        out2[0:128, dp * TW:(dp + 1) * TW], bank[0:128, 0:512],
                        xtile2[0:128, dp * TW:(dp + 1) * TW]),
                        waits=[(s_pe, s_pe.n)], inc=(s_ev, 1))
                    P.emit("vector", lambda e, dp=dp, t=t: e.scalar_tensor_tensor(
                        out2[0:128, dp * TW:(dp + 1) * TW],
                        xn[0:128, dp * T + t * TW:dp * T + (t + 1) * TW],
                        Dcol[0:128, dp:dp + 1],
                        out2[0:128, dp * TW:(dp + 1) * TW], ALU.mult, ALU.add),
                        waits=[(s_ev, s_ev.n)], inc=(s_ev, 1))
                P.emit("sync", lambda e, t=t: e.dma_start(
                    io["x_out"][0:1024, t * TW:(t + 1) * TW].rearrange(
                        "(c p) w -> p c w", p=128),
                    out2[0:128, 0:8 * TW]),
                    waits=[(s_ev, s_ev.n)], inc=(s_st, 16))
            P.wait("sync", s_st)
    P.barrier()


def _enc_layer(P, U, io, tag, li, kvs, gcol, brow, ln_sweep, elu1_evac,
               sems, s_gp, s_x, s_pe, s_ev, s_sc, s_st):
    nc, C = P.nc, U.C
    TW = 512
    P.emit("gpsimd", lambda e, li=li: e.dma_start(
        brow[0:1, 0:1024], io["bq"][li * 1024:(li + 1) * 1024]), inc=(s_gp, 16))
    P.emit("gpsimd", lambda e, li=li: e.dma_start(
        brow[0:1, 1024:2048], io["bk"][li * 1024:(li + 1) * 1024]), inc=(s_gp, 16))
    P.emit("gpsimd", lambda e, li=li: e.dma_start(
        brow[0:1, 2048:3072], io["bv"][li * 1024:(li + 1) * 1024]), inc=(s_gp, 16))
    P.emit("gpsimd", lambda e, li=li: e.dma_start(
        brow[0:1, 3072:4096], io["bo"][li * 1024:(li + 1) * 1024]), inc=(s_gp, 16))
    P.emit("gpsimd", lambda e, li=li: e.dma_start(
        brow[0:1, 4096:8192], io["b1"][li * 4096:(li + 1) * 4096]), inc=(s_gp, 16))
    P.emit("gpsimd", lambda e, li=li: e.dma_start(
        brow[0:1, 8192:9216], io["b2"][li * 1024:(li + 1) * 1024]), inc=(s_gp, 16))
    emit_load_bias_cols(P, io["ln1_s"][li * 1024:(li + 1) * 1024], 1024, gcol, 0, s_gp)
    emit_load_bias_cols(P, io["ln1_b"][li * 1024:(li + 1) * 1024], 1024, gcol, 8, s_gp)
    emit_load_bias_cols(P, io["ln2_s"][li * 1024:(li + 1) * 1024], 1024, gcol, 16, s_gp)
    emit_load_bias_cols(P, io["ln2_b"][li * 1024:(li + 1) * 1024], 1024, gcol, 24, s_gp)

    with ExitStack() as SA:
        sbA = lambda n, s, d: U.scoped_sb(SA, n, s, d)
        Wk = sbA(f"{tag}_wk{li}", [128, 8 * 1024], BF16)
        Wv = sbA(f"{tag}_wv{li}", [128, 8 * 1024], BF16)
        xtile_pa = sbA(f"{tag}_xtA{li}", [128, 8 * TW], F32)
        xn_pa = sbA(f"{tag}_xnA{li}", [128, 8 * TW], BF16)
        xsq_pa = sbA(f"{tag}_xsqA{li}", [128, 8 * TW], BF16)
        tmp_pa = sbA(f"{tag}_tmpA{li}", [128, 8 * TW], F32)
        stat_pa = sbA(f"{tag}_statA{li}", [1, 2 * TW], F32)
        kp_sb_pa = sbA(f"{tag}_kp{li}", [128, 1024], BF16)
        v_sb_pa = sbA(f"{tag}_vv{li}", [128, 1024], BF16)
        t1_pa = sbA(f"{tag}_t1{li}", [128, TW], F32)
        t2_pa = sbA(f"{tag}_t2{li}", [128, TW], F32)
        U.ln_scr = {"xsq": xsq_pa, "stat": stat_pa, "tmp": tmp_pa}
        emit_cast_bf16(P, io["Wk"][li * 1024:(li + 1) * 1024, 0:1024]
                       .rearrange("(c p) n -> p c n", p=128), Wk[0:128, 0:8192], s_gp)
        emit_cast_bf16(P, io["Wv"][li * 1024:(li + 1) * 1024, 0:1024]
                       .rearrange("(c p) n -> p c n", p=128), Wv[0:128, 0:8192], s_gp)
        GPA = s_gp.n
        P.wait("tensor", s_gp, GPA)
        P.wait("scalar", s_gp, GPA)
        P.wait("vector", s_gp, GPA)
        first_bank_touch = [True, True, True]
        for t in range(8):
            ln_sweep(io["enc_scr"][0:1024, t * TW:(t + 1) * TW].rearrange(
                "(c p) w -> p c w", p=128), None, xtile_pa, xn_pa,
                (gcol[0:128, 0:8], gcol[0:128, 8:16]))
            P.wait("tensor", s_ev)
            for i4 in range(4):
                for (Wm, boff, dst) in ((Wk, 1024, kp_sb_pa), (Wv, 2048, v_sb_pa)):
                    for half in range(2):
                        bank = U.pb[0]
                        for c in range(8):
                            P.emit("tensor", lambda e, Wm=Wm, c=c, half=half, i4=i4, bank=bank: e.matmul(
                                bank[0:128, 0:512],
                                xn_pa[0:128, c * TW + i4 * 128:c * TW + (i4 + 1) * 128],
                                Wm[0:128, c * 1024 + half * 512:c * 1024 + (half + 1) * 512],
                                start=(c == 0), stop=False),
                                waits=[(s_ev, s_ev.n)] if c == 0 else ())
                        P.emit("tensor", lambda e, boff=boff, half=half, bank=bank: e.matmul(
                            bank[0:128, 0:512], C["ones_row_b"][0:1, 0:128],
                            brow[0:1, boff + half * 512:boff + (half + 1) * 512],
                            start=False, stop=True), inc=(s_pe, 1))
                        if dst is kp_sb_pa:
                            elu1_evac(bank, kp_sb_pa[0:128, half * 512:(half + 1) * 512], t1_pa, t2_pa)
                        else:
                            P.emit("vector", lambda e, half=half, bank=bank: e.tensor_copy(
                                v_sb_pa[0:128, half * 512:(half + 1) * 512],
                                bank[0:128, 0:512]),
                                waits=[(s_pe, s_pe.n)], inc=(s_ev, 1))
                P.wait("tensor", s_ev)
                for h in range(16):
                    kvbank = U.pb[2] if h < 8 else U.pb[3]
                    col = (h % 8) * 64
                    stt = first_bank_touch[0 if h < 8 else 1]
                    P.emit("tensor", lambda e, h=h, col=col, kvbank=kvbank, stt=stt: e.matmul(
                        kvbank[0:64, col:col + 64],
                        kp_sb_pa[0:128, h * 64:(h + 1) * 64],
                        v_sb_pa[0:128, h * 64:(h + 1) * 64],
                        start=stt, stop=False, skip_group_check=True))
                    if h < 8:
                        first_bank_touch[0] = False
                    else:
                        first_bank_touch[1] = False
                    P.emit("tensor", lambda e, h=h, stt=first_bank_touch[2]: e.matmul(
                        U.pb[4][0:64, 2 * h:2 * h + 2],
                        kp_sb_pa[0:128, h * 64:(h + 1) * 64],
                        C["ones_col2_b"][0:128, 0:2],
                        start=stt, stop=False, skip_group_check=True),
                        inc=(s_pe, 1) if (h == 15) else None)
                    first_bank_touch[2] = False
        P.wait("vector", s_pe)
        for h in range(16):
            kvbank = U.pb[2] if h < 8 else U.pb[3]
            col = (h % 8) * 64
            po = (h % 2) * 64
            P.emit("vector", lambda e, h=h, col=col, po=po, kvbank=kvbank: e.tensor_copy(
                kvs[po:po + 64, (h // 2) * 65:(h // 2) * 65 + 64],
                kvbank[0:64, col:col + 64]))
            P.emit("vector", lambda e, h=h, po=po: e.tensor_copy(
                kvs[po:po + 64, (h // 2) * 65 + 64:(h // 2) * 65 + 65],
                U.pb[4][0:64, 2 * h:2 * h + 1]),
                inc=(s_ev, 1) if h == 15 else None)
    P.barrier()

    with ExitStack() as SB:
        sbB = lambda n, s, d: U.scoped_sb(SB, n, s, d)
        Wq = sbB(f"{tag}_wq{li}", [128, 8 * 1024], BF16)
        Wo = sbB(f"{tag}_wo{li}", [128, 8 * 1024], BF16)
        xtile_pb = sbB(f"{tag}_xtB{li}", [128, 8 * TW], F32)
        xn_pb = sbB(f"{tag}_xnB{li}", [128, 8 * TW], BF16)
        xsq_pb = sbB(f"{tag}_xsqB{li}", [128, 8 * TW], BF16)
        tmp_pb = sbB(f"{tag}_tmpB{li}", [128, 8 * TW], F32)
        stat_pb = sbB(f"{tag}_statB{li}", [1, 2 * TW], F32)
        qp_sb_pb = sbB(f"{tag}_qp{li}", [128, 8 * TW], BF16)
        at_tok_pb = sbB(f"{tag}_atok{li}", [128, 1024], BF16)
        atT_pb = sbB(f"{tag}_atT{li}", [128, 8 * TW], BF16)
        t1_pb = sbB(f"{tag}_t1B{li}", [128, TW], F32)
        t2_pb = sbB(f"{tag}_t2B{li}", [128, TW], F32)
        zrow_pb = sbB(f"{tag}_z{li}", [128, 4], F32)
        U.ln_scr = {"xsq": xsq_pb, "stat": stat_pb, "tmp": tmp_pb}
        emit_cast_bf16(P, io["Wq"][li * 1024:(li + 1) * 1024, 0:1024]
                       .rearrange("(c p) n -> p c n", p=128), Wq[0:128, 0:8192], s_gp)
        emit_cast_bf16(P, io["Wo"][li * 1024:(li + 1) * 1024, 0:1024]
                       .rearrange("(c p) n -> p c n", p=128), Wo[0:128, 0:8192], s_gp)
        GPB = s_gp.n
        P.wait("tensor", s_gp, GPB)
        for t in range(8):
            ln_sweep(io["enc_scr"][0:1024, t * TW:(t + 1) * TW].rearrange(
                "(c p) w -> p c w", p=128), None, xtile_pb, xn_pb,
                (gcol[0:128, 0:8], gcol[0:128, 8:16]))
            P.wait("tensor", s_ev)
            for op in range(8):
                bank = U.pb[0]
                for c in range(8):
                    P.emit("tensor", lambda e, op=op, c=c, bank=bank: e.matmul(
                        bank[0:128, 0:512],
                        Wq[0:128, c * 1024 + op * 128:c * 1024 + (op + 1) * 128],
                        xn_pb[0:128, c * TW:(c + 1) * TW], start=(c == 0), stop=False),
                        waits=[(s_ev, s_ev.n)] if c == 0 else ())
                P.emit("tensor", lambda e, op=op, bank=bank: e.matmul(
                    bank[0:128, 0:512], brow[0:1, op * 128:(op + 1) * 128],
                    C["ones_row_b512"][0:1, 0:TW], start=False, stop=True),
                    inc=(s_pe, 1))
                elu1_evac(bank, qp_sb_pb[0:128, op * TW:(op + 1) * TW], t1_pb, t2_pb)
            P.wait("tensor", s_ev)
            for i4 in range(4):
                for h in range(16):
                    po = (h % 2) * 64
                    P.emit("tensor", lambda e, h=h, i4=i4, po=po: e.matmul(
                        U.pb[2][0:128, 0:65],
                        qp_sb_pb[po:po + 64, (h // 2) * TW + i4 * 128:(h // 2) * TW + (i4 + 1) * 128],
                        kvs[po:po + 64, (h // 2) * 65:(h // 2 + 1) * 65],
                        start=True, stop=True),
                        waits=[(s_ev, s_ev.n)], inc=(s_pe, 1))
                    P.emit("vector", lambda e: e.tensor_scalar(
                        zrow_pb[0:128, 0:1], U.pb[2][0:128, 64:65], 1e-6, None, ALU.add),
                        waits=[(s_pe, s_pe.n)], inc=(s_ev, 1))
                    P.emit("vector", lambda e: e.reciprocal(
                        zrow_pb[0:128, 1:2], zrow_pb[0:128, 0:1]),
                        waits=[(s_ev, s_ev.n)], inc=(s_ev, 1))
                    P.emit("vector", lambda e, h=h: e.tensor_scalar(
                        at_tok_pb[0:128, h * 64:(h + 1) * 64], U.pb[2][0:128, 0:64],
                        zrow_pb[0:128, 1:2], None, ALU.mult),
                        waits=[(s_ev, s_ev.n)], inc=(s_ev, 1))
                P.wait("tensor", s_ev)
                for dc in range(8):
                    P.emit("tensor", lambda e, dc=dc: e.transpose(
                        U.pb_bf[0:128, 0:128],
                        at_tok_pb[0:128, dc * 128:(dc + 1) * 128],
                        C["ident128b"][0:128, 0:128]), inc=(s_pe, 1))
                    P.emit("vector", lambda e, dc=dc, i4=i4: e.tensor_copy(
                        atT_pb[0:128, dc * TW + i4 * 128:dc * TW + (i4 + 1) * 128],
                        U.pb_bf[0:128, 0:128]),
                        waits=[(s_pe, s_pe.n)], inc=(s_ev, 1))
                    P.wait("tensor", s_ev)
            for op in range(8):
                bank = U.pb[0 if op % 2 == 0 else 1]
                P.wait("tensor", s_ev, s_ev.n)
                for c in range(8):
                    P.emit("tensor", lambda e, op=op, c=c, bank=bank: e.matmul(
                        bank[0:128, 0:512],
                        Wo[0:128, c * 1024 + op * 128:c * 1024 + (op + 1) * 128],
                        atT_pb[0:128, c * TW:(c + 1) * TW], start=(c == 0), stop=False))
                P.emit("tensor", lambda e, op=op, bank=bank: e.matmul(
                    bank[0:128, 0:512], brow[0:1, 3072 + op * 128:3072 + (op + 1) * 128],
                    C["ones_row_b512"][0:1, 0:TW], start=False, stop=True),
                    inc=(s_pe, 1))
                P.emit("vector", lambda e, op=op, bank=bank: e.tensor_add(
                    tmp_pb[0:128, op * TW:(op + 1) * TW], bank[0:128, 0:512],
                    xtile_pb[0:128, op * TW:(op + 1) * TW]),
                    waits=[(s_pe, s_pe.n)], inc=(s_ev, 1))
            P.emit("sync", lambda e, t=t: e.dma_start(
                io["enc2_scr"][0:1024, t * TW:(t + 1) * TW].rearrange(
                    "(c p) w -> p c w", p=128),
                tmp_pb[0:128, 0:8 * TW]),
                waits=[(s_ev, s_ev.n)], inc=(s_st, 16))
    P.barrier()

    with ExitStack() as SC:
        sbC = lambda n, s, d: U.scoped_sb(SC, n, s, d)
        W1 = sbC(f"{tag}_w1{li}", [128, 8 * 4096], BF16)
        xtile_pc = sbC(f"{tag}_xtC{li}", [128, 8 * TW], F32)
        xn_pc = sbC(f"{tag}_xnC{li}", [128, 8 * TW], BF16)
        xsq_pc = sbC(f"{tag}_xsqC{li}", [128, 8 * TW], BF16)
        tmp_pc = sbC(f"{tag}_tmpC{li}", [128, 8 * TW], F32)
        stat_pc = sbC(f"{tag}_statC{li}", [1, 2 * TW], F32)
        f1_pc = sbC(f"{tag}_f1{li}", [128, 32 * TW], BF16)
        U.ln_scr = {"xsq": xsq_pc, "stat": stat_pc, "tmp": tmp_pc}
        emit_cast_bf16(P, io["W1"][li * 1024:(li + 1) * 1024, 0:4096]
                       .rearrange("(c p) n -> p c n", p=128), W1[0:128, 0:32768], s_gp)
        GPC = s_gp.n
        P.wait("tensor", s_gp, GPC)
        for t in range(8):
            ln_sweep(io["enc2_scr"][0:1024, t * TW:(t + 1) * TW].rearrange(
                "(c p) w -> p c w", p=128), None, xtile_pc, xn_pc,
                (gcol[0:128, 16:24], gcol[0:128, 24:32]))
            P.wait("tensor", s_ev)
            P.wait("scalar", s_st)
            for hp in range(32):
                bank = U.pb[hp % 2]
                P.wait("tensor", s_sc, s_sc.n)
                for c in range(8):
                    P.emit("tensor", lambda e, hp=hp, c=c, bank=bank: e.matmul(
                        bank[0:128, 0:512],
                        W1[0:128, c * 4096 + hp * 128:c * 4096 + (hp + 1) * 128],
                        xn_pc[0:128, c * TW:(c + 1) * TW], start=(c == 0), stop=False))
                P.emit("tensor", lambda e, hp=hp, bank=bank: e.matmul(
                    bank[0:128, 0:512], brow[0:1, 4096 + hp * 128:4096 + (hp + 1) * 128],
                    C["ones_row_b512"][0:1, 0:TW], start=False, stop=True),
                    inc=(s_pe, 1))
                P.emit("scalar", lambda e, hp=hp, bank=bank: e.activation(
                    f1_pc[0:128, hp * TW:(hp + 1) * TW], bank[0:128, 0:512], AF.Relu),
                    waits=[(s_pe, s_pe.n)], inc=(s_sc, 1))
            P.emit("sync", lambda e, t=t: e.dma_start(
                io["h_scr"][0:4096, t * TW:(t + 1) * TW].rearrange(
                    "(c p) w -> p c w", p=128),
                f1_pc[0:128, 0:32 * TW]),
                waits=[(s_sc, s_sc.n)], inc=(s_st, 16))
    P.barrier()

    with ExitStack() as SD:
        sbD = lambda n, s, d: U.scoped_sb(SD, n, s, d)
        W2 = sbD(f"{tag}_w2{li}", [128, 32 * 1024], BF16)
        xtile_pd = sbD(f"{tag}_xtD{li}", [128, 8 * TW], F32)
        tmp_pd = sbD(f"{tag}_tmpD{li}", [128, 8 * TW], F32)
        f1_pd = sbD(f"{tag}_f1D{li}", [128, 32 * TW], BF16)
        emit_cast_bf16(P, io["W2"][li * 4096:(li + 1) * 4096, 0:1024]
                       .rearrange("(c p) n -> p c n", p=128), W2[0:128, 0:32768], s_gp)
        GPD = s_gp.n
        P.wait("tensor", s_gp, GPD)
        for t in range(8):
            P.emit("sync", lambda e, t=t: e.dma_start(
                f1_pd[0:128, 0:32 * TW],
                io["h_scr"][0:4096, t * TW:(t + 1) * TW].rearrange(
                    "(c p) w -> p c w", p=128)),
                waits=[(s_ev, s_ev.n)], inc=(s_x, 16))
            P.emit("sync", lambda e, t=t: e.dma_start(
                xtile_pd[0:128, 0:8 * TW],
                io["enc2_scr"][0:1024, t * TW:(t + 1) * TW].rearrange(
                    "(c p) w -> p c w", p=128)), inc=(s_x, 16))
            P.wait("tensor", s_x)
            P.wait("vector", s_x)
            P.wait("vector", s_st)
            for op in range(8):
                bank = U.pb[2 + op % 2]
                P.wait("tensor", s_ev, s_ev.n)
                for cc in range(32):
                    P.emit("tensor", lambda e, op=op, cc=cc, bank=bank: e.matmul(
                        bank[0:128, 0:512],
                        W2[0:128, cc * 1024 + op * 128:cc * 1024 + (op + 1) * 128],
                        f1_pd[0:128, cc * TW:(cc + 1) * TW], start=(cc == 0), stop=False))
                P.emit("tensor", lambda e, op=op, bank=bank: e.matmul(
                    bank[0:128, 0:512], brow[0:1, 8192 + op * 128:8192 + (op + 1) * 128],
                    C["ones_row_b512"][0:1, 0:TW], start=False, stop=True),
                    inc=(s_pe, 1))
                P.emit("vector", lambda e, op=op, bank=bank: e.tensor_add(
                    tmp_pd[0:128, op * TW:(op + 1) * TW], bank[0:128, 0:512],
                    xtile_pd[0:128, op * TW:(op + 1) * TW]),
                    waits=[(s_pe, s_pe.n)], inc=(s_ev, 1))
            P.emit("sync", lambda e, t=t: e.dma_start(
                io["enc_scr"][0:1024, t * TW:(t + 1) * TW].rearrange(
                    "(c p) w -> p c w", p=128),
                tmp_pd[0:128, 0:8 * TW]),
                waits=[(s_ev, s_ev.n)], inc=(s_st, 16))
    P.barrier()



def build_L0_attn_b(P, U, io, tag):
    nc, C = P.nc, U.C
    H = host_constants()
    TW = 512
    s_gp = P.sem(f"{tag}_gp")
    s_x = P.sem(f"{tag}_x")
    s_pe = P.sem(f"{tag}_pe")
    s_ev = P.sem(f"{tag}_ev")
    s_sc = P.sem(f"{tag}_sc")
    s_st = P.sem(f"{tag}_st")
    sems = {"pe": s_pe, "ev": s_ev, "sc": s_sc}

    def ln_sweep(src_hbm, dst_buf, xtile, xn, aff, extra=None, eps_col=0,
                 out_fn=None):
        P.emit("sync", lambda e: e.dma_start(
            xtile[0:128, 0:8 * TW], src_hbm),
            waits=[(s_ev, s_ev.n)], inc=(s_x, 16))
        P.wait("scalar", s_x)
        P.wait("vector", s_x)
        P.wait("vector", s_st)
        emit_ln_tile(P, U, xtile, xn, TW, sems, affine=aff, extra_f32=extra,
                     eps_col=eps_col, out_fn=out_fn)

    def elu1_evac(bank, out_ap, tmp1, tmp2):
        P.emit("vector", lambda e: e.tensor_scalar(
            tmp2[0:128, 0:TW], bank[0:128, 0:512], 0.0, None, ALU.max),
            waits=[(s_pe, s_pe.n), (s_ev, s_ev.n)], inc=(s_ev, 1))
        P.emit("scalar", lambda e: e.activation(
            tmp1[0:128, 0:TW], bank[0:128, 0:512], AF.Relu, scale=-1.0),
            waits=[(s_pe, s_pe.n), (s_sc, s_sc.n), (s_ev, s_ev.n)], inc=(s_sc, 1))
        P.emit("scalar", lambda e: e.activation(
            tmp1[0:128, 0:TW], tmp1[0:128, 0:TW], AF.Exp, scale=-1.0),
            waits=[(s_sc, s_sc.n)], inc=(s_sc, 1))
        P.emit("vector", lambda e: e.tensor_add(
            out_ap, tmp1[0:128, 0:TW], tmp2[0:128, 0:TW]),
            waits=[(s_sc, s_sc.n), (s_ev, s_ev.n)], inc=(s_ev, 1))

    with ExitStack() as ST0:
        sb0 = lambda n, s, d: U.scoped_sb(ST0, n, s, d)
        kvs = sb0(f"{tag}_kvs", [128, 8 * 65], BF16)
        gcol = sb0(f"{tag}_gcol", [128, 6 * 8], F32)
        brow = sb0(f"{tag}_brow", [1, 4 * 1024 + 4096 + 1024], BF16)

        with ExitStack() as S0:
            sbA = lambda n, s, d: U.scoped_sb(S0, n, s, d)
            xtile_p0 = sbA(f"{tag}_xt0", [128, 8 * TW], F32)
            xn_p0 = sbA(f"{tag}_xn0", [128, 8 * TW], BF16)
            xin_p0 = sbA(f"{tag}_xin0", [128, 8 * TW], F32)
            pe_t_p0 = sbA(f"{tag}_pe0", [128, 8 * TW], F32)
            xsq_p0 = sbA(f"{tag}_xsq0", [128, 8 * TW], BF16)
            tmp_p0 = sbA(f"{tag}_tmp0", [128, 8 * TW], F32)
            stat_p0 = sbA(f"{tag}_stat0", [1, 2 * TW], F32)
            U.ln_scr = {"xsq": xsq_p0, "stat": stat_p0, "tmp": tmp_p0}
            for t in range(8):
                P.emit("sync", lambda e, t=t: e.dma_start(
                    pe_t_p0[0:128, 0:8 * TW],
                    io["posemb"][0:1024, t * TW:(t + 1) * TW].rearrange(
                        "(c p) w -> p c w", p=128)),
                    waits=[(s_ev, s_ev.n)], inc=(s_x, 16))
                ln_sweep(io["x_in"][0:1024, t * TW:(t + 1) * TW].rearrange(
                    "(c p) w -> p c w", p=128), None, xtile_p0, xn_p0, None, extra=xin_p0)
                P.wait("vector", s_x)
                for c in range(8):
                    P.emit("vector", lambda e, c=c: e.scalar_tensor_tensor(
                        tmp_p0[0:128, c * TW:(c + 1) * TW],
                        xin_p0[0:128, c * TW:(c + 1) * TW], 32.0,
                        pe_t_p0[0:128, c * TW:(c + 1) * TW], ALU.mult, ALU.add),
                        waits=[(s_ev, s_ev.n)], inc=(s_ev, 1))
                P.emit("sync", lambda e, t=t: e.dma_start(
                    io["enc_scr"][0:1024, t * TW:(t + 1) * TW].rearrange(
                        "(c p) w -> p c w", p=128),
                    tmp_p0[0:128, 0:8 * TW]),
                    waits=[(s_ev, s_ev.n)], inc=(s_st, 16))
        P.barrier()

        for li in range(4):
            _enc_layer(P, U, io, tag, li, kvs, gcol, brow, ln_sweep,
                       elu1_evac, sems, s_gp, s_x, s_pe, s_ev, s_sc, s_st)
        with ExitStack() as SF:
            sbF = lambda n, s, d: U.scoped_sb(SF, n, s, d)
            xtile_pf = sbF(f"{tag}_xtF", [128, 8 * TW], F32)
            x0_pf = sbF(f"{tag}_x0F", [128, 8 * TW], F32)
            xn_pf = sbF(f"{tag}_xnF", [128, 8 * TW], BF16)
            xin_pf = sbF(f"{tag}_xinF", [128, 8 * TW], F32)
            xsq_pf = sbF(f"{tag}_xsqF", [128, 8 * TW], BF16)
            tmp_pf = sbF(f"{tag}_tmpF", [128, 8 * TW], F32)
            stat_pf = sbF(f"{tag}_statF", [1, 2 * TW], F32)
            U.ln_scr = {"xsq": xsq_pf, "stat": stat_pf, "tmp": tmp_pf}
            emit_load_bias_cols(P, io["lnf_s"], 1024, gcol, 32, s_gp)
            emit_load_bias_cols(P, io["lnf_b"], 1024, gcol, 40, s_gp)
            P.wait("vector", s_gp)
            for t in range(8):
                P.emit("sync", lambda e, t=t: e.dma_start(
                    x0_pf[0:128, 0:8 * TW],
                    io["x_in"][0:1024, t * TW:(t + 1) * TW].rearrange(
                        "(c p) w -> p c w", p=128)),
                    waits=[(s_ev, s_ev.n)], inc=(s_x, 16))
                ln_sweep(io["enc_scr"][0:1024, t * TW:(t + 1) * TW].rearrange(
                    "(c p) w -> p c w", p=128), None, xtile_pf, xn_pf,
                    (gcol[0:128, 32:40], gcol[0:128, 40:48]),
                    out_fn=lambda c: xin_pf[0:128, c * TW:(c + 1) * TW])
                P.wait("vector", s_x)
                for c in range(8):
                    P.emit("vector", lambda e, c=c: e.tensor_add(
                        xin_pf[0:128, c * TW:(c + 1) * TW],
                        xin_pf[0:128, c * TW:(c + 1) * TW],
                        x0_pf[0:128, c * TW:(c + 1) * TW]),
                        waits=[(s_ev, s_ev.n)], inc=(s_ev, 1))
                P.emit("sync", lambda e, t=t: e.dma_start(
                    io["x_out"][0:1024, t * TW:(t + 1) * TW].rearrange(
                        "(c p) w -> p c w", p=128),
                    xin_pf[0:128, 0:8 * TW]),
                    waits=[(s_ev, s_ev.n)], inc=(s_st, 16))
            P.wait("sync", s_st)
    P.barrier()

def build_L1_attn_b(P, U, io, tag):
    nc, C = P.nc, U.C
    H = host_constants()
    SW = 256
    NSLAB = T // SW
    NSLAB = min(NSLAB, globals().get("GLA_LIMIT", NSLAB))
    PH = globals().get("GLA_PHASE", 8)
    CH = 64
    s_gp = P.sem(f"{tag}_gp")
    s_tr = P.sem(f"{tag}_tr")
    s_x = P.sem(f"{tag}_x")
    s_pe = P.sem(f"{tag}_pe")
    s_ev = P.sem(f"{tag}_ev")
    s_sc = P.sem(f"{tag}_sc")
    s_st = P.sem(f"{tag}_st")
    sems = {"pe": s_pe, "ev": s_ev, "sc": s_sc}

    def VS(fn, w=()):
        P.emit("vector", fn, waits=[(s_ev, s_ev.n)] + list(w), inc=(s_ev, 1))

    def SC_(fn, w=()):
        P.emit("scalar", fn, waits=[(s_sc, s_sc.n)] + list(w), inc=(s_sc, 1))

    with ExitStack() as ST0:
        sb0 = lambda n, s, d: U.scoped_sb(ST0, n, s, d)
        upT = sb0(f"{tag}_upT", [128, 8 * 2048], BF16)
        gaT = sb0(f"{tag}_gaT", [128, 8 * 2048], BF16)
        dnT = sb0(f"{tag}_dnT", [128, 16 * 1024], BF16)
        qkT = sb0(f"{tag}_qkT", [128, 16 * 256], BF16)
        ig1T = sb0(f"{tag}_ig1T", [128, 16 * 16], BF16)
        ig2T = sb0(f"{tag}_ig2T", [16, 2048], BF16)
        bcol = sb0(f"{tag}_bcol", [128, 16 * 4], F32)
        ccol = sb0(f"{tag}_ccol", [128, 64], F32)
        gbr = sb0(f"{tag}_gbr", [1, 128], BF16)
        Sf = sb0(f"{tag}_Sf", [64, 2048], F32)
        Sb = sb0(f"{tag}_Sb", [64, 2048], BF16)
        emit_cast_bf16(P, io["up_w"], io["upw_bf"], s_gp)
        emit_cast_bf16(P, io["gate_w"], io["gatew_bf"], s_gp)
        emit_cast_bf16(P, io["down_w"], io["downw_bf"], s_gp)
        emit_cast_bf16(P, io["qkgt_w"], io["qkgt_bf"], s_gp)
        emit_cast_bf16(P, io["ig_w1"], io["ig1_bf"], s_gp)
        GP0 = s_gp.n
        emit_transpose_load(P, io["upw_bf"], 2048, 1024, upT, 2048, s_tr,
                            waits=[(s_gp, GP0)])
        emit_transpose_load(P, io["gatew_bf"], 2048, 1024, gaT, 2048, s_tr)
        emit_transpose_load(P, io["downw_bf"], 1024, 2048, dnT, 1024, s_tr)
        for c in range(16):
            P.emit("sync", lambda e, c=c: e.dma_start(
                qkT[0:128, c * 256:(c + 1) * 256],
                io["qkgt_bf"][0:256, c * 128:(c + 1) * 128], transpose=True),
                inc=(s_tr, 16))
        for c in range(16):
            P.emit("sync", lambda e, c=c: e.dma_start(
                ig1T[0:128, c * 16:(c + 1) * 16],
                io["ig1_bf"][0:16, c * 128:(c + 1) * 128], transpose=True),
                inc=(s_tr, 16))
        TR0 = s_tr.n
        ig2tmp = sb0(f"{tag}_ig2tmp", [128, 16 * 16], BF16)
        P.emit("gpsimd", lambda e: e.dma_start(
            ig2tmp[0:128, 0:256],
            io["ig_w2"].rearrange("(c p) n -> p c n", p=128)), inc=(s_gp, 16))
        P.wait("tensor", s_gp)
        for c in range(16):
            P.emit("tensor", lambda e, c=c: e.transpose(
                U.pb_bf[0:16, 0:128], ig2tmp[0:128, c * 16:(c + 1) * 16],
                C["ident128b"][0:128, 0:128]), inc=(s_pe, 1))
            P.emit("vector", lambda e, c=c: e.tensor_copy(
                ig2T[0:16, c * 128:(c + 1) * 128], U.pb_bf[0:16, 0:128]),
                waits=[(s_pe, s_pe.n)], inc=(s_ev, 1))
            P.wait("tensor", s_ev)
        emit_load_bias_cols(P, io["conv_b"], 2048, bcol, 0, s_gp)
        emit_load_bias_cols(P, io["ig_b2"], 2048, bcol, 16, s_gp)
        emit_load_bias_cols(P, io["residual_weight"], 2048, bcol, 32, s_gp)
        emit_load_bias_cols(P, io["norm_weight"], 2048, bcol, 48, s_gp)
        P.emit("gpsimd", lambda e: e.dma_start(gbr[0:1, 0:64], io["g_b"]), inc=(s_gp, 16))
        P.emit("gpsimd", lambda e: e.dma_start(gbr[0:1, 64:128], io["tau_b"]), inc=(s_gp, 16))
        P.emit("gpsimd", lambda e: e.dma_start(
            ccol[0:128, 0:64], io["conv_w"].rearrange("(c p) j -> p c j", p=128),
            allow_slow_non_contiguous=True), inc=(s_gp, 16))
        GP1 = s_gp.n

        with ExitStack() as SL:
            sbL = lambda n, s, d: U.scoped_sb(SL, n, s, d)
            xtile = sbL(f"{tag}_xt", [128, 8 * SW], F32)
            hsn = sbL(f"{tag}_hsn", [128, 8 * SW], BF16)
            xsq = sbL(f"{tag}_xsq", [128, 8 * SW], BF16)
            tmp = sbL(f"{tag}_tmp", [128, 8 * SW], F32)
            stat = sbL(f"{tag}_stat", [1, 2 * SW], F32)
            xup = sbL(f"{tag}_xup", [128, 16 * (SW + 4)], BF16)
            shf = sbL(f"{tag}_shf", [128, 16 * SW], BF16)
            sfg = sbL(f"{tag}_sfg", [128, 16 * SW], BF16)
            vb = sbL(f"{tag}_vb", [128, 16 * SW], BF16)
            ofb = sbL(f"{tag}_ofb", [128, 16 * SW], BF16)
            onb = vb
            qrow = sbL(f"{tag}_qrow", [64, 9 * SW], F32)
            crow = sbL(f"{tag}_crow", [64, 32], F32)
            qtb = sbL(f"{tag}_qtb", [64, 3 * CH], BF16)
            atb = sbL(f"{tag}_atb", [64, CH], BF16)
            kdb = sbL(f"{tag}_kdb", [64, CH], BF16)
            vtokb = sbL(f"{tag}_vtok", [64, 2048], BF16)
            otok = hsn
            t16a = sbL(f"{tag}_t16a", [16, SW], BF16)
            zer = sbL(f"{tag}_zer", [64, CH], F32)
            trik = sbL(f"{tag}_trik", [64, CH], F32)
            U.ln_scr = {"xsq": xsq, "stat": stat, "tmp": tmp}
            P.emit("vector", lambda e: e.memset(Sf[0:64, 0:2048], 0.0), inc=(s_ev, 1))
            P.emit("vector", lambda e: e.memset(Sb[0:64, 0:2048], 0.0), inc=(s_ev, 1))
            P.emit("vector", lambda e: e.memset(zer[0:64, 0:CH], 0.0), inc=(s_ev, 1))
            for c in range(16):
                P.emit("vector", lambda e, c=c: e.memset(
                    xup[0:128, c * (SW + 4):c * (SW + 4) + 4], 0.0),
                    inc=(s_ev, 1) if c == 15 else None)
            trid = nc.inline_tensor(H["tri_keepT"], "c_" + tag + "_trik").ap()
            P.emit("sync", lambda e: e.dma_start(trik[0:64, 0:CH], trid), inc=(s_x, 16))
            P.wait("tensor", s_tr, TR0)
            P.wait("tensor", s_gp, GP1)
            P.wait("vector", s_gp, GP1)
            P.wait("scalar", s_gp, GP1)
            P.wait("vector", s_x)

            for sl_i in range(NSLAB):
                t0c = sl_i * SW
                P.emit("sync", lambda e, t0c=t0c: e.dma_start(
                    xtile[0:128, 0:8 * SW],
                    io["x_in"][0:1024, t0c:t0c + SW].rearrange(
                        "(c p) w -> p c w", p=128)),
                    waits=[(s_ev, s_ev.n), (s_st, s_st.n)], inc=(s_x, 16))
                P.wait("scalar", s_x)
                P.wait("vector", s_x)
                P.wait("vector", s_st)
                emit_ln_tile(P, U, xtile, hsn, SW, sems)
                if PH < 2:
                    continue
                P.wait("tensor", s_ev)
                for op in range(16):
                    bank = U.pb[op % 2]
                    P.wait("tensor", s_sc, s_sc.n)
                    P.wait("tensor", s_ev, s_ev.n)
                    for cc in range(8):
                        P.emit("tensor", lambda e, op=op, cc=cc, bank=bank: e.matmul(
                            bank[0:128, 0:SW],
                            upT[0:128, cc * 2048 + op * 128:cc * 2048 + (op + 1) * 128],
                            hsn[0:128, cc * SW:(cc + 1) * SW], start=(cc == 0), stop=(cc == 7)),
                            inc=(s_pe, 1) if cc == 7 else None)
                    P.emit("scalar", lambda e, op=op, bank=bank: e.copy(
                        xup[0:128, op * (SW + 4) + 4:op * (SW + 4) + 4 + SW],
                        bank[0:128, 0:SW]),
                        waits=[(s_pe, s_pe.n), (s_sc, s_sc.n)], inc=(s_sc, 1))
                    bank2 = U.pb[2 + op % 2]
                    for cc in range(8):
                        P.emit("tensor", lambda e, op=op, cc=cc, bank2=bank2: e.matmul(
                            bank2[0:128, 0:SW],
                            gaT[0:128, cc * 2048 + op * 128:cc * 2048 + (op + 1) * 128],
                            hsn[0:128, cc * SW:(cc + 1) * SW], start=(cc == 0), stop=(cc == 7)),
                            inc=(s_pe, 1) if cc == 7 else None)
                    P.emit("scalar", lambda e, op=op, bank2=bank2: e.activation(
                        sfg[0:128, op * SW:(op + 1) * SW], bank2[0:128, 0:SW], AF.Silu),
                        waits=[(s_pe, s_pe.n), (s_sc, s_sc.n)], inc=(s_sc, 1))
                if PH < 3:
                    continue
                P.wait("vector", s_sc)
                for cc in range(16):
                    base = cc * (SW + 4)
                    VS(lambda e, cc=cc, base=base: e.tensor_scalar(
                        tmp[0:128, 0:SW], xup[0:128, base + 1:base + 1 + SW],
                        ccol[0:128, cc * 4:cc * 4 + 1], None, ALU.mult),
                       w=[(s_sc, s_sc.n)])
                    for j in range(1, 4):
                        VS(lambda e, cc=cc, base=base, j=j: e.scalar_tensor_tensor(
                            tmp[0:128, 0:SW], xup[0:128, base + 1 + j:base + 1 + j + SW],
                            ccol[0:128, cc * 4 + j:cc * 4 + j + 1],
                            tmp[0:128, 0:SW], ALU.mult, ALU.add))
                    VS(lambda e, cc=cc: e.tensor_scalar(
                        tmp[0:128, 0:SW], tmp[0:128, 0:SW],
                        bcol[0:128, cc:cc + 1], None, ALU.add))
                    SC_(lambda e, cc=cc: e.activation(
                        shf[0:128, cc * SW:(cc + 1) * SW], tmp[0:128, 0:SW],
                        AF.Silu), w=[(s_ev, s_ev.n)])
                if PH < 4:
                    continue
                P.wait("tensor", s_sc)
                for cc in range(16):
                    P.emit("tensor", lambda e, cc=cc: e.matmul(
                        U.pb[4][0:16, 0:SW], ig1T[0:128, cc * 16:(cc + 1) * 16],
                        xup[0:128, cc * (SW + 4) + 4:cc * (SW + 4) + 4 + SW],
                        start=(cc == 0), stop=(cc == 15)),
                        inc=(s_pe, 1) if cc == 15 else None)
                P.emit("vector", lambda e: e.tensor_copy(
                    t16a[0:16, 0:SW], U.pb[4][0:16, 0:SW]),
                    waits=[(s_pe, s_pe.n)], inc=(s_ev, 1))
                P.wait("tensor", s_ev)
                for cc in range(16):
                    bank = U.pb[cc % 2]
                    P.wait("tensor", s_ev, s_ev.n)
                    P.emit("tensor", lambda e, cc=cc, bank=bank: e.matmul(
                        bank[0:128, 0:SW], ig2T[0:16, cc * 128:(cc + 1) * 128],
                        t16a[0:16, 0:SW], start=True, stop=True),
                        inc=(s_pe, 1))
                    SC_(lambda e, cc=cc, bank=bank: e.activation(
                        tmp[0:128, 0:SW], bank[0:128, 0:SW], AF.Sigmoid,
                        bias=bcol[0:128, 16 + cc:17 + cc]), w=[(s_pe, s_pe.n), (s_ev, s_ev.n)])
                    VS(lambda e, cc=cc: e.tensor_mul(
                        vb[0:128, cc * SW:(cc + 1) * SW], tmp[0:128, 0:SW],
                        xup[0:128, cc * (SW + 4) + 4:cc * (SW + 4) + 4 + SW]),
                        w=[(s_sc, s_sc.n)])
                for cc in range(16):
                    VS(lambda e, cc=cc: e.tensor_copy(
                        xup[0:128, cc * (SW + 4) + 1:cc * (SW + 4) + 4],
                        xup[0:128, cc * (SW + 4) + 4 + SW - 3:cc * (SW + 4) + 4 + SW]))
                if PH < 5:
                    continue
                P.wait("tensor", s_ev)
                for w_i, bias_mm, evac in (
                    (0, None, "q"), (1, None, "k"), (2, 0, "g"), (3, 64, "tau")
                )[:globals().get("GLA_NGRP", 4)]:
                    if globals().get("GLA_NOBIAS", 0):
                        bias_mm = None
                    bank = U.pb[4]
                    for cc in range(16):
                        last = (cc == 15 and bias_mm is None)
                        P.emit("tensor", lambda e, w_i=w_i, cc=cc, bank=bank, last=last: e.matmul(
                            bank[0:64, 0:SW],
                            qkT[0:128, cc * 256 + w_i * 64:cc * 256 + (w_i + 1) * 64],
                            shf[0:128, cc * SW:(cc + 1) * SW],
                            start=(cc == 0), stop=last),
                            waits=[(s_ev, s_ev.n), (s_sc, s_sc.n)] if cc == 0 else (),
                            inc=(s_pe, 1) if last else None)
                    if bias_mm is not None:
                        P.emit("tensor", lambda e, bias_mm=bias_mm, bank=bank: e.matmul(
                            bank[0:64, 0:SW], gbr[0:1, bias_mm:bias_mm + 64],
                            C["ones_row_b512"][0:1, 0:SW], start=False, stop=True),
                            inc=(s_pe, 1))
                    if evac == "q":
                        VS(lambda e, bank=bank: e.tensor_scalar(
                            qrow[0:64, 0:SW], bank[0:64, 0:SW], 0.125, None, ALU.mult),
                            w=[(s_pe, s_pe.n)])
                    elif evac == "k":
                        VS(lambda e, bank=bank: e.tensor_copy(
                            qrow[0:64, SW:2 * SW], bank[0:64, 0:SW]),
                            w=[(s_pe, s_pe.n)])
                    elif evac == "g":
                        VS(lambda e, bank=bank: e.tensor_scalar(
                            qrow[0:64, 6 * SW:7 * SW], bank[0:64, 0:SW], 0.0, None,
                            ALU.max), w=[(s_pe, s_pe.n)])
                        SC_(lambda e, bank=bank: e.activation(
                            qrow[0:64, 7 * SW:8 * SW], bank[0:64, 0:SW], AF.Abs),
                            w=[(s_pe, s_pe.n), (s_ev, s_ev.n)])
                        SC_(lambda e: e.activation(
                            qrow[0:64, 7 * SW:8 * SW], qrow[0:64, 7 * SW:8 * SW],
                            AF.Exp, scale=-1.0))
                        SC_(lambda e: e.activation(
                            qrow[0:64, 7 * SW:8 * SW], qrow[0:64, 7 * SW:8 * SW],
                            AF.Ln, bias=C["epsb"][0:64, 4:5]))
                        VS(lambda e: e.tensor_add(
                            qrow[0:64, 2 * SW:3 * SW], qrow[0:64, 6 * SW:7 * SW],
                            qrow[0:64, 7 * SW:8 * SW]), w=[(s_sc, s_sc.n)])
                    else:
                        SC_(lambda e, bank=bank: e.activation(
                            qrow[0:64, 3 * SW:4 * SW], bank[0:64, 0:SW], AF.Sigmoid),
                            w=[(s_pe, s_pe.n)])
                if PH < 5.3:
                    continue
                P.wait("scalar", s_ev)
                SC_(lambda e: e.activation(
                    qrow[0:64, 7 * SW:8 * SW], qrow[0:64, 2 * SW:3 * SW], AF.Ln))
                VS(lambda e: e.tensor_mul(
                    qrow[0:64, 7 * SW:8 * SW], qrow[0:64, 7 * SW:8 * SW],
                    qrow[0:64, 3 * SW:4 * SW]), w=[(s_sc, s_sc.n)])
                SC_(lambda e: e.activation(
                    qrow[0:64, 4 * SW:5 * SW], qrow[0:64, 7 * SW:8 * SW], AF.Exp),
                    w=[(s_ev, s_ev.n)])
                VS(lambda e: e.scalar_tensor_tensor(
                    qrow[0:64, 5 * SW:6 * SW], qrow[0:64, 2 * SW:3 * SW], -1.0,
                    qrow[0:64, 3 * SW:4 * SW], ALU.mult, ALU.mult))
                if PH < 5.6:
                    continue
                SC_(lambda e: e.activation(
                    qrow[0:64, 6 * SW:7 * SW], qrow[0:64, SW:2 * SW], AF.Square),
                    w=[(s_ev, s_ev.n)])
                P.wait("tensor", s_sc)
                P.emit("tensor", lambda e: e.matmul(
                    U.pb[4][0:1, 0:SW], C["ones_col_f"][0:64, 0:1],
                    qrow[0:64, 6 * SW:7 * SW], start=True, stop=True), inc=(s_pe, 1))
                SC_(lambda e: e.activation(
                    stat[0:1, 0:SW], U.pb[4][0:1, 0:SW], AF.Sqrt),
                    w=[(s_pe, s_pe.n)])
                VS(lambda e: e.tensor_scalar(
                    stat[0:1, 0:SW], stat[0:1, 0:SW], 1e-12, None, ALU.max),
                    w=[(s_sc, s_sc.n)])
                VS(lambda e: e.reciprocal(stat[0:1, SW:2 * SW], stat[0:1, 0:SW]))
                P.wait("tensor", s_ev)
                P.emit("tensor", lambda e: e.matmul(
                    U.pb[4][0:64, 0:SW], C["ones_row_f"][0:1, 0:64],
                    stat[0:1, SW:2 * SW], start=True, stop=True), inc=(s_pe, 1))
                VS(lambda e: e.tensor_mul(
                    qrow[0:64, SW:2 * SW], qrow[0:64, SW:2 * SW], U.pb[4][0:64, 0:SW]),
                    w=[(s_pe, s_pe.n)])
                VS(lambda e: e.tensor_mul(
                    qrow[0:64, SW:2 * SW], qrow[0:64, SW:2 * SW],
                    qrow[0:64, 4 * SW:5 * SW]))
                if PH < 6:
                    continue
                for cp in range(SW // (2 * CH)):
                    for sub in range(2):
                        ch0 = cp * 2 * CH + sub * CH
                        csl = slice(ch0, ch0 + CH)
                        VS(lambda e, csl=csl: e.tensor_tensor_scan(
                            qrow[0:64, 8 * SW + csl.start:8 * SW + csl.stop],
                            qrow[0:64, 5 * SW + csl.start:5 * SW + csl.stop],
                            qrow[0:64, 5 * SW + csl.start:5 * SW + csl.stop],
                            0.0, ALU.add, ALU.bypass))
                        bsl = slice(8 * SW + csl.start, 8 * SW + csl.stop)
                        VS(lambda e, bsl=bsl: e.tensor_copy(
                            crow[0:64, 0:1], qrow[0:64, bsl.start + 31:bsl.start + 32]))
                        VS(lambda e, bsl=bsl: e.tensor_copy(
                            crow[0:64, 1:2], qrow[0:64, bsl.start + 63:bsl.start + 64]))
                        SC_(lambda e: e.activation(
                            crow[0:64, 2:3], crow[0:64, 1:2], AF.Exp),
                            w=[(s_ev, s_ev.n)])
                        VS(lambda e: e.tensor_sub(
                            crow[0:64, 3:4], crow[0:64, 1:2], crow[0:64, 0:1]))
                        SC_(lambda e: e.activation(
                            crow[0:64, 3:4], crow[0:64, 3:4], AF.Exp),
                            w=[(s_ev, s_ev.n)])
                        VS(lambda e, bsl=bsl: e.tensor_scalar(
                            tmp[0:64, 0:CH], qrow[0:64, bsl], crow[0:64, 0:1],
                            None, ALU.subtract))
                        SC_(lambda e: e.activation(
                            tmp[0:64, CH:2 * CH], tmp[0:64, 0:CH], AF.Exp),
                            w=[(s_ev, s_ev.n)])
                        SC_(lambda e: e.activation(
                            tmp[0:64, 2 * CH:3 * CH], tmp[0:64, 0:CH], AF.Exp,
                            scale=-1.0))
                        SC_(lambda e, bsl=bsl: e.activation(
                            tmp[0:64, 3 * CH:4 * CH], qrow[0:64, bsl], AF.Exp))
                        VS(lambda e, csl=csl: e.tensor_mul(
                            qtb[0:64, 0:CH], qrow[0:64, csl], tmp[0:64, CH:2 * CH]),
                            w=[(s_sc, s_sc.n)])
                        VS(lambda e, csl=csl: e.tensor_mul(
                            qtb[0:64, CH:2 * CH],
                            qrow[0:64, SW + csl.start:SW + csl.stop],
                            tmp[0:64, 2 * CH:3 * CH]))
                        VS(lambda e, csl=csl: e.tensor_mul(
                            qtb[0:64, 2 * CH:3 * CH], qrow[0:64, csl],
                            tmp[0:64, 3 * CH:4 * CH]))
                        VS(lambda e: e.tensor_scalar(
                            tmp[0:64, 0:CH], qtb[0:64, CH:2 * CH], crow[0:64, 3:4],
                            None, ALU.mult))
                        P.wait("tensor", s_ev)
                        P.emit("scalar", lambda e: e.copy(
                            atb[0:64, 0:CH], tmp[0:64, 0:CH]),
                            waits=[(s_sc, s_sc.n), (s_ev, s_ev.n)], inc=(s_sc, 1))
                        P.wait("tensor", s_sc)
                        P.emit("tensor", lambda e: e.transpose(
                            U.pb_bf[0:CH, 0:CH], atb[0:64, 0:CH],
                            C["ident128b"][0:64, 0:64]), inc=(s_pe, 1))
                        P.emit("vector", lambda e: e.tensor_copy(
                            kdb[0:64, 0:CH], U.pb_bf[0:CH, 0:CH]),
                            waits=[(s_pe, s_pe.n)], inc=(s_ev, 1))
                        P.wait("tensor", s_ev)
                        P.emit("tensor", lambda e: e.matmul(
                            U.pb[4][0:64, 0:CH], qtb[0:64, CH:2 * CH],
                            qtb[0:64, 0:CH], start=True, stop=True), inc=(s_pe, 1))
                        P.emit("vector", lambda e: e.tensor_mul(
                            atb[0:64, 0:CH], U.pb[4][0:64, 0:CH],
                            trik[0:64, 0:CH]),
                            waits=[(s_pe, s_pe.n), (s_ev, s_ev.n)], inc=(s_ev, 1))
                        P.wait("tensor", s_ev)
                        for c2 in range(0, 16, 8):
                            for c3 in range(8):
                                cc = c2 + c3
                                P.emit("tensor", lambda e, cc=cc, c3=c3, ch0=ch0: e.transpose(
                                    U.pb_bf[0:64, c3 * 128:(c3 + 1) * 128],
                                    vb[0:128, cc * SW + ch0:cc * SW + ch0 + CH],
                                    C["ident128b"][0:128, 0:128]),
                                    inc=(s_pe, 1) if c3 == 7 else None)
                            P.emit("vector", lambda e, c2=c2: e.tensor_copy(
                                vtokb[0:64, c2 * 128:(c2 + 8) * 128],
                                U.pb_bf[0:64, 0:1024]),
                                waits=[(s_pe, s_pe.n)], inc=(s_ev, 1))
                            P.wait("tensor", s_ev)
                        for vq in range(4):
                            bank = U.pb[vq % 2]
                            P.wait("tensor", s_ev, s_ev.n)
                            P.emit("tensor", lambda e, vq=vq, bank=bank: e.matmul(
                                bank[0:64, 0:512], qtb[0:64, 2 * CH:3 * CH],
                                Sb[0:64, vq * 512:(vq + 1) * 512], start=True, stop=False))
                            P.emit("tensor", lambda e, vq=vq, bank=bank: e.matmul(
                                bank[0:64, 0:512], atb[0:64, 0:CH],
                                vtokb[0:64, vq * 512:(vq + 1) * 512], start=False,
                                stop=True), inc=(s_pe, 1))
                            P.emit("vector", lambda e, vq=vq, sub=sub, bank=bank: e.tensor_copy(
                                otok[sub * 64:sub * 64 + 64, vq * 512:(vq + 1) * 512],
                                bank[0:64, 0:512]),
                                waits=[(s_pe, s_pe.n)], inc=(s_ev, 1))
                        for vq in range(4):
                            bank = U.pb[2 + vq % 2]
                            P.wait("tensor", s_ev, s_ev.n)
                            P.emit("tensor", lambda e, vq=vq, bank=bank: e.matmul(
                                bank[0:64, 0:512], kdb[0:64, 0:CH],
                                vtokb[0:64, vq * 512:(vq + 1) * 512],
                                start=True, stop=True), inc=(s_pe, 1))
                            VS(lambda e, vq=vq, bank=bank: e.scalar_tensor_tensor(
                                Sf[0:64, vq * 512:(vq + 1) * 512],
                                Sf[0:64, vq * 512:(vq + 1) * 512], crow[0:64, 2:3],
                                bank[0:64, 0:512], ALU.mult, ALU.add),
                                w=[(s_pe, s_pe.n)])
                        VS(lambda e: e.tensor_copy(Sb[0:64, 0:2048], Sf[0:64, 0:2048]))
                    P.wait("tensor", s_ev)
                    for cc in range(16):
                        P.emit("tensor", lambda e, cc=cc: e.transpose(
                            U.pb_bf[0:128, 0:128], otok[0:128, cc * 128:(cc + 1) * 128],
                            C["ident128b"][0:128, 0:128]), inc=(s_pe, 1))
                        P.emit("vector", lambda e, cc=cc, cp=cp: e.tensor_copy(
                            ofb[0:128, cc * SW + cp * 2 * CH:cc * SW + (cp + 1) * 2 * CH],
                            U.pb_bf[0:128, 0:128]),
                            waits=[(s_pe, s_pe.n)], inc=(s_ev, 1))
                        P.wait("tensor", s_ev)
                if PH < 7:
                    continue
                for cc in range(16):
                    VS(lambda e, cc=cc: e.tensor_scalar(
                        tmp[0:128, 0:SW], shf[0:128, cc * SW:(cc + 1) * SW],
                        bcol[0:128, 32 + cc:33 + cc], None, ALU.mult))
                    VS(lambda e, cc=cc: e.tensor_add(
                        tmp[0:128, 0:SW], tmp[0:128, 0:SW],
                        ofb[0:128, cc * SW:(cc + 1) * SW]))
                    VS(lambda e, cc=cc: e.tensor_mul(
                        ofb[0:128, cc * SW:(cc + 1) * SW], tmp[0:128, 0:SW],
                        sfg[0:128, cc * SW:(cc + 1) * SW]))
                    sqd = hsn if cc < 8 else xsq
                    SC_(lambda e, cc=cc, sqd=sqd: e.activation(
                        sqd[0:128, (cc % 8) * SW:(cc % 8 + 1) * SW],
                        ofb[0:128, cc * SW:(cc + 1) * SW], AF.Square),
                        w=[(s_ev, s_ev.n), (s_pe, s_pe.n)])
                P.wait("tensor", s_sc)
                for cc in range(16):
                    sqd = hsn if cc < 8 else xsq
                    P.emit("tensor", lambda e, cc=cc, sqd=sqd: e.matmul(
                        U.pb[4][0:1, 0:SW], C["ones_col_b"][0:128, 0:1],
                        sqd[0:128, (cc % 8) * SW:(cc % 8 + 1) * SW],
                        start=(cc == 0), stop=(cc == 15)),
                        inc=(s_pe, 1) if cc == 15 else None)
                VS(lambda e: e.tensor_scalar(
                    stat[0:1, 0:SW], U.pb[4][0:1, 0:SW], 1.0 / 2048.0, None, ALU.mult),
                    w=[(s_pe, s_pe.n)])
                SC_(lambda e: e.activation(
                    stat[0:1, 0:SW], stat[0:1, 0:SW], AF.Ln,
                    bias=C["epsb"][0:1, 0:1]), w=[(s_ev, s_ev.n)])
                SC_(lambda e: e.activation(
                    stat[0:1, 0:SW], stat[0:1, 0:SW], AF.Exp, scale=-0.5))
                P.wait("tensor", s_sc)
                P.emit("tensor", lambda e: e.matmul(
                    U.pb[5][0:128, 0:SW], C["ones_row_f"][0:1, 0:128],
                    stat[0:1, 0:SW], start=True, stop=True), inc=(s_pe, 1))
                P.wait("vector", s_pe)
                for cc in range(16):
                    VS(lambda e, cc=cc: e.scalar_tensor_tensor(
                        onb[0:128, cc * SW:(cc + 1) * SW],
                        ofb[0:128, cc * SW:(cc + 1) * SW],
                        bcol[0:128, 48 + cc:49 + cc], U.pb[5][0:128, 0:SW],
                        ALU.mult, ALU.mult))
                if PH < 8:
                    continue
                P.wait("tensor", s_ev)
                for op in range(8):
                    bank = U.pb[op % 2]
                    P.wait("tensor", s_ev, s_ev.n)
                    for cc in range(16):
                        P.emit("tensor", lambda e, op=op, cc=cc, bank=bank: e.matmul(
                            bank[0:128, 0:SW],
                            dnT[0:128, cc * 1024 + op * 128:cc * 1024 + (op + 1) * 128],
                            onb[0:128, cc * SW:(cc + 1) * SW],
                            start=(cc == 0), stop=(cc == 15)),
                            inc=(s_pe, 1) if cc == 15 else None)
                    P.emit("vector", lambda e, op=op, bank=bank: e.tensor_add(
                        tmp[0:128, op * SW:(op + 1) * SW], bank[0:128, 0:SW],
                        xtile[0:128, op * SW:(op + 1) * SW]),
                        waits=[(s_pe, s_pe.n)], inc=(s_ev, 1))
                P.emit("sync", lambda e, t0c=t0c: e.dma_start(
                    io["x_out"][0:1024, t0c:t0c + SW].rearrange(
                        "(c p) w -> p c w", p=128),
                    tmp[0:128, 0:8 * SW]),
                    waits=[(s_ev, s_ev.n)], inc=(s_st, 16))
            P.wait("sync", s_st)
    P.barrier()



def _sub_run(build_fns, specs, weights, xs):
    nc = bass.Bass("TRN2", target_bir_lowering=False)
    P, U = build_base(nc)
    io = {}
    for name, shape, dt, kind in specs:
        io[name] = nc.dram_tensor(name, shape, dt, kind=kind).ap()
    for fn, tag in build_fns:
        fn(P, U, io, tag)
    P.finalize()
    in_maps = []
    for xb in xs:
        m = dict(U.cin)
        m.update(weights)
        m["x_in"] = xb
        in_maps.append(m)
    res = run_bass_kernel_spmd(nc, in_maps, core_ids=list(range(len(xs))))
    return [np.asarray(res.results[b]["x_out"]) for b in range(len(xs))]


def _kernel_multi(**inp):
    B = 8
    u = np.asarray(inp["L0_attn_a_u"], np.float32)
    xs = [np.ascontiguousarray(u[b].T) for b in range(B)]
    H = host_constants()
    XIO = [("x_in", [1024, T], F32, "ExternalInput"),
           ("x_out", [1024, T], F32, "ExternalOutput")]

    xs = _sub_run([(build_L0_attn_a, "ssm")], XIO + [
        ("a_params", [512], F32, "ExternalInput"),
        ("B_w", [1024, 1024], F32, "ExternalInput"),
        ("C_w", [1024, 1024], F32, "ExternalInput"),
        ("Dd", [1024], F32, "ExternalInput"),
        ("C_wbf", [1024, 1024], BF16, "Internal")],
        {"a_params": inp["L0_attn_a_a_params"], "B_w": inp["L0_attn_a_B_w"],
         "C_w": inp["L0_attn_a_C_w"], "Dd": inp["L0_attn_a_D"]}, xs)

    ffn_specs = XIO + [
        ("W1", [4096, 1024], F32, "ExternalInput"),
        ("b1", [4096], F32, "ExternalInput"),
        ("W2", [1024, 4096], F32, "ExternalInput"),
        ("b2", [1024], F32, "ExternalInput"),
        ("scale", [1, 1], F32, "ExternalInput"),
        ("W1bf", [4096, 1024], BF16, "Internal"),
        ("W2bf", [1024, 4096], BF16, "Internal"),
        ("h_scr", [4096, 4096], BF16, "Internal")]
    ffn_w = {"W1": inp["L0_ffn_W1"], "b1": inp["L0_ffn_b1"],
             "W2": inp["L0_ffn_W2"], "b2": inp["L0_ffn_b2"],
             "scale": np.asarray(inp["L0_ffn_scale"], np.float32).reshape(1, 1)}
    xs = _sub_run([(build_L0_ffn, "ffn")], ffn_specs, ffn_w, xs)

    Lx = 4
    enc_shapes = dict(
        Wq=[Lx * 1024, 1024], Wk=[Lx * 1024, 1024], Wv=[Lx * 1024, 1024],
        Wo=[Lx * 1024, 1024], W1=[Lx * 1024, 4096], W2=[Lx * 4096, 1024],
        bq=[Lx * 1024], bk=[Lx * 1024], bv=[Lx * 1024], bo=[Lx * 1024],
        b1=[Lx * 4096], b2=[Lx * 1024], ln1_s=[Lx * 1024], ln1_b=[Lx * 1024],
        ln2_s=[Lx * 1024], ln2_b=[Lx * 1024], lnf_s=[1024], lnf_b=[1024])
    enc_specs = XIO + [(n, s, F32, "ExternalInput") for n, s in enc_shapes.items()]
    enc_specs += [("posemb", [1024, T], F32, "ExternalInput"),
                  ("enc_scr", [1024, T], F32, "Internal"),
                  ("enc2_scr", [1024, T], F32, "Internal"),
                  ("h_scr", [4096, 4096], BF16, "Internal")]
    enc_w = {n: np.asarray(inp[f"L0_attn_b_{n}"], np.float32).reshape(s)
             for n, s in enc_shapes.items()}
    enc_w["posemb"] = H["posemb_f"]
    xs = _sub_run([(build_L0_attn_b, "enc")], enc_specs, enc_w, xs)

    xs = _sub_run([(build_L0_ffn, "ffn")], ffn_specs, ffn_w, xs)

    attn_shapes = dict(ln1_g=[1024], ln1_b=[1024], Wqkv=[1024, 3072],
                       bqkv=[3072], Wo=[1024, 1024], bo=[1024], ln2_g=[1024],
                       ln2_b=[1024], W1=[1024, 4096], b1=[4096],
                       W2=[4096, 1024], b2=[1024])
    attn_specs = XIO + [(n, s, F32, "ExternalInput") for n, s in attn_shapes.items()]
    attn_specs += [("xin_scr", [1024, T], F32, "Internal"),
                   ("xa_scr", [1024, T], F32, "Internal"),
                   ("qkv_scr", [3072, T], BF16, "Internal"),
                   ("attn_scr", [1024, T], BF16, "Internal"),
                   ("h_scr", [4096, 4096], BF16, "Internal")]
    attn_w = {n: inp[f"L1_attn_a_{n}"] for n in attn_shapes}
    xs = _sub_run([(build_L1_attn_a, "ata")], attn_specs, attn_w, xs)

    moe_specs = XIO + [
        ("norm_scale", [1024], F32, "ExternalInput"),
        ("w_router", [8, 1024], F32, "ExternalInput"),
        ("w_up", [8 * 5632, 1024], F32, "ExternalInput"),
        ("w_down", [8 * 1024, 2816], F32, "ExternalInput"),
        ("up_bf", [8 * 5632, 1024], BF16, "Internal"),
        ("down_bf", [8 * 1024, 2816], BF16, "Internal"),
        ("w_upT_bf", [8 * 1024, 5632], BF16, "Internal"),
        ("w_downT_bf", [8 * 2816, 1024], BF16, "Internal"),
        ("xn_scr", [1024, T], BF16, "Internal"),
        ("xin_scr", [1024, T], F32, "Internal")]
    moe_w = {"norm_scale": inp["L1_ffn_norm_scale"],
             "w_router": inp["L1_ffn_w_router"],
             "w_up": np.asarray(inp["L1_ffn_w_up"], np.float32).reshape(8 * 5632, 1024),
             "w_down": np.asarray(inp["L1_ffn_w_down"], np.float32).reshape(8 * 1024, 2816)}
    moe_fns = [(build_moe_prologue, "moe"), (build_L1_ffn, "moe")]
    xs = _sub_run(moe_fns, moe_specs, moe_w, xs)

    gla_shapes = dict(up_w=[2048, 1024], gate_w=[2048, 1024], conv_w=[2048, 4],
                      conv_b=[2048], q_w=[64, 2048], k_w=[64, 2048],
                      g_w=[64, 2048], g_b=[64], tau_w=[64, 2048], tau_b=[64],
                      ig_w1=[16, 2048], ig_w2=[2048, 16], ig_b2=[2048],
                      residual_weight=[2048], norm_weight=[2048],
                      down_w=[1024, 2048])
    gla_specs = XIO + [(n, s, F32, "ExternalInput") for n, s in gla_shapes.items()]
    gla_specs += [("qkgt_w", [256, 2048], F32, "ExternalInput"),
                  ("upw_bf", [2048, 1024], BF16, "Internal"),
                  ("gatew_bf", [2048, 1024], BF16, "Internal"),
                  ("downw_bf", [1024, 2048], BF16, "Internal"),
                  ("qkgt_bf", [256, 2048], BF16, "Internal"),
                  ("ig1_bf", [16, 2048], BF16, "Internal")]
    gla_w = {n: inp[f"L1_attn_b_{n}"] for n in gla_shapes}
    gla_w["qkgt_w"] = np.ascontiguousarray(np.concatenate(
        [np.asarray(inp[f"L1_attn_b_{n}"], np.float32)
         for n in ("q_w", "k_w", "g_w", "tau_w")], 0))
    xs = _sub_run([(build_L1_attn_b, "gla")], gla_specs, gla_w, xs)

    xs = _sub_run(moe_fns, moe_specs, moe_w, xs)

    return np.stack([xb.T for xb in xs]).astype(np.float32)


def _stage_weights(inp):
    H = host_constants()
    Lx = 4
    enc_shapes = dict(
        Wq=[Lx * 1024, 1024], Wk=[Lx * 1024, 1024], Wv=[Lx * 1024, 1024],
        Wo=[Lx * 1024, 1024], W1=[Lx * 1024, 4096], W2=[Lx * 4096, 1024],
        bq=[Lx * 1024], bk=[Lx * 1024], bv=[Lx * 1024], bo=[Lx * 1024],
        b1=[Lx * 4096], b2=[Lx * 1024], ln1_s=[Lx * 1024], ln1_b=[Lx * 1024],
        ln2_s=[Lx * 1024], ln2_b=[Lx * 1024], lnf_s=[1024], lnf_b=[1024])
    attn_shapes = dict(ln1_g=[1024], ln1_b=[1024], Wqkv=[1024, 3072],
                       bqkv=[3072], Wo=[1024, 1024], bo=[1024], ln2_g=[1024],
                       ln2_b=[1024], W1=[1024, 4096], b1=[4096],
                       W2=[4096, 1024], b2=[1024])
    gla_shapes = dict(up_w=[2048, 1024], gate_w=[2048, 1024], conv_w=[2048, 4],
                      conv_b=[2048], q_w=[64, 2048], k_w=[64, 2048],
                      g_w=[64, 2048], g_b=[64], tau_w=[64, 2048], tau_b=[64],
                      ig_w1=[16, 2048], ig_w2=[2048, 16], ig_b2=[2048],
                      residual_weight=[2048], norm_weight=[2048],
                      down_w=[1024, 2048])
    W = {}
    W["ssm"] = {"a_params": ([512], inp["L0_attn_a_a_params"]),
                "B_w": ([1024, 1024], inp["L0_attn_a_B_w"]),
                "C_w": ([1024, 1024], inp["L0_attn_a_C_w"]),
                "Dd": ([1024], inp["L0_attn_a_D"])}
    W["ffn"] = {"W1": ([4096, 1024], inp["L0_ffn_W1"]),
                "b1": ([4096], inp["L0_ffn_b1"]),
                "W2": ([1024, 4096], inp["L0_ffn_W2"]),
                "b2": ([1024], inp["L0_ffn_b2"]),
                "scale": ([1, 1], np.asarray(inp["L0_ffn_scale"],
                                             np.float32).reshape(1, 1))}
    W["enc"] = {n: (s, np.asarray(inp[f"L0_attn_b_{n}"], np.float32).reshape(s))
                for n, s in enc_shapes.items()}
    W["enc"]["posemb"] = ([1024, T], H["posemb_f"])
    W["ata"] = {n: (s, inp[f"L1_attn_a_{n}"]) for n, s in attn_shapes.items()}
    W["moe"] = {"norm_scale": ([1024], inp["L1_ffn_norm_scale"]),
                "w_router": ([8, 1024], inp["L1_ffn_w_router"]),
                "w_up": ([8 * 5632, 1024],
                         np.asarray(inp["L1_ffn_w_up"], np.float32).reshape(8 * 5632, 1024)),
                "w_down": ([8 * 1024, 2816],
                           np.asarray(inp["L1_ffn_w_down"], np.float32).reshape(8 * 1024, 2816))}
    W["gla"] = {n: (s, inp[f"L1_attn_b_{n}"]) for n, s in gla_shapes.items()}
    W["gla"]["qkgt_w"] = ([256, 2048], np.ascontiguousarray(np.concatenate(
        [np.asarray(inp[f"L1_attn_b_{n}"], np.float32)
         for n in ("q_w", "k_w", "g_w", "tau_w")], 0)))
    return W


def kernel(**inp):
    B = 8
    u = np.asarray(inp["L0_attn_a_u"], np.float32)
    xs = [np.ascontiguousarray(u[b].T) for b in range(B)]
    W = _stage_weights(inp)

    nc = bass.Bass("TRN2", target_bir_lowering=False)
    P, U = build_base(nc)
    in_w = {}

    def wgt(group):
        io = {}
        for key, (shape, arr) in W[group].items():
            name = f"w_{group}_{key}"
            if name not in in_w:
                in_w[name] = (nc.dram_tensor(name, shape, F32,
                                             kind="ExternalInput").ap(), arr)
            io[key] = in_w[name][0]
        return io

    def scr(name, shape, dt):
        return nc.dram_tensor(name, shape, dt, kind="Internal").ap()

    X = [nc.dram_tensor("x_in", [1024, T], F32, kind="ExternalInput").ap()]
    for k in range(1, 8):
        X.append(scr(f"xh{k}", [1024, T], F32))
    X.append(nc.dram_tensor("x_out", [1024, T], F32, kind="ExternalOutput").ap())
    h_scr = scr("h_scr_shared", [4096, 4096], BF16)

    io = wgt("ssm")
    io.update(x_in=X[0], x_out=X[1], C_wbf=scr("ssm_C_wbf", [1024, 1024], BF16))
    build_L0_attn_a(P, U, io, "ssm")
    P.barrier()

    ffn_scr = dict(W1bf=scr("ffn_W1bf", [4096, 1024], BF16),
                   W2bf=scr("ffn_W2bf", [1024, 4096], BF16), h_scr=h_scr)
    io = wgt("ffn")
    io.update(x_in=X[1], x_out=X[2], **ffn_scr)
    build_L0_ffn(P, U, io, "f2")
    P.barrier()

    io = wgt("enc")
    io.update(x_in=X[2], x_out=X[3],
              enc_scr=scr("enc_scr", [1024, T], F32),
              enc2_scr=scr("enc2_scr", [1024, T], F32), h_scr=h_scr)
    build_L0_attn_b(P, U, io, "enc")
    P.barrier()

    io = wgt("ffn")
    io.update(x_in=X[3], x_out=X[4], **ffn_scr)
    build_L0_ffn(P, U, io, "f4")
    P.barrier()

    io = wgt("ata")
    io.update(x_in=X[4], x_out=X[5],
              xin_scr=scr("ata_xin_scr", [1024, T], F32),
              xa_scr=scr("ata_xa_scr", [1024, T], F32),
              qkv_scr=scr("ata_qkv_scr", [3072, T], BF16),
              attn_scr=scr("ata_attn_scr", [1024, T], BF16), h_scr=h_scr)
    build_L1_attn_a(P, U, io, "ata")
    P.barrier()

    moe_scr = dict(up_bf=scr("moe_up_bf", [8 * 5632, 1024], BF16),
                   down_bf=scr("moe_down_bf", [8 * 1024, 2816], BF16),
                   w_upT_bf=scr("moe_w_upT_bf", [8 * 1024, 5632], BF16),
                   w_downT_bf=scr("moe_w_downT_bf", [8 * 2816, 1024], BF16))
    io = wgt("moe")
    io.update(x_in=X[5], x_out=X[6],
              xn_scr=scr("m6_xn_scr", [1024, T], BF16),
              xin_scr=scr("m6_xin_scr", [1024, T], F32), **moe_scr)
    build_moe_prologue(P, U, io, "m6")
    build_L1_ffn(P, U, io, "m6")
    P.barrier()

    io = wgt("gla")
    io.update(x_in=X[6], x_out=X[7],
              upw_bf=scr("gla_upw_bf", [2048, 1024], BF16),
              gatew_bf=scr("gla_gatew_bf", [2048, 1024], BF16),
              downw_bf=scr("gla_downw_bf", [1024, 2048], BF16),
              qkgt_bf=scr("gla_qkgt_bf", [256, 2048], BF16),
              ig1_bf=scr("gla_ig1_bf", [16, 2048], BF16))
    build_L1_attn_b(P, U, io, "gla")
    P.barrier()

    io = wgt("moe")
    io.update(x_in=X[7], x_out=X[8],
              xn_scr=scr("m8_xn_scr", [1024, T], BF16),
              xin_scr=scr("m8_xin_scr", [1024, T], F32), **moe_scr)
    build_L1_ffn(P, U, io, "m8")
    P.finalize()

    in_maps = []
    for b in range(B):
        m = dict(U.cin)
        for name, (_ap, arr) in in_w.items():
            m[name] = arr
        m["x_in"] = xs[b]
        in_maps.append(m)
    res = run_bass_kernel_spmd(nc, in_maps, core_ids=list(range(B)))
    return np.stack([np.asarray(res.results[b]["x_out"]).T
                     for b in range(B)]).astype(np.float32)
